# version 20
# baseline (speedup 1.0000x reference)
import numpy as np
import concourse.bass as bass
import concourse.mybir as mybir

F32 = mybir.dt.float32
BF16 = mybir.dt.bfloat16
AF = mybir.ActivationFunctionType
ALU = mybir.AluOpType
AX = mybir.AxisListType

ENGINES = ("sync", "scalar", "vector", "tensor", "gpsimd")


class Prog:
    def __init__(self, nc):
        self.nc = nc
        self.q = {e: [] for e in ENGINES}
        self.cnt = {}
        self.waited = {e: {} for e in ENGINES}
        self.handles = {}

    def sem(self, name):
        if name not in self.cnt:
            self.cnt[name] = 0
        return name

    def emit(self, eng, fn):
        self.q[eng].append(fn)

    def inc(self, name, by=1):
        self.sem(name)
        self.cnt[name] += by
        return self.cnt[name]

    def wait(self, eng, sem, val):
        if val is None or val <= 0:
            return
        w = self.waited[eng].get(sem, 0)
        if val <= w:
            return
        self.waited[eng][sem] = val
        self.emit(eng, lambda e, s=sem, v=val: e.wait_ge(self.handles[s], v))

    def dma(self, eng, out, in_, sem):
        assert tuple(out.shape) == tuple(in_.shape), (
            f"dma shape mismatch {tuple(out.shape)} vs {tuple(in_.shape)}")
        c = self.inc(sem, 16)
        self.emit(eng, lambda e, o=out, i=in_, s=sem: e.dma_start(out=o, in_=i)
                  .then_inc(self.handles[s], 16))
        return c

    def op(self, eng, fn, sem=None):
        if sem is None:
            self.emit(eng, fn)
            return None
        c = self.inc(sem, 1)
        self.emit(eng, lambda e, f=fn, s=sem: f(e).then_inc(self.handles[s], 1))
        return c

    DEFSEM = {"vector": "vs", "scalar": "ss", "gpsimd": "gs", "tensor": "ts"}

    def chain(self, eng):
        self.wait(eng, self.DEFSEM[eng], self.cnt.get(self.DEFSEM[eng], 0))

    def barrier(self):
        b = self.sem("bar")
        target = self.cnt["bar"] + len(ENGINES)
        self.cnt["bar"] = target
        for e in ENGINES:
            if e == "sync":
                self.emit(e, lambda en: en.nop().then_inc(self.handles["bar"], 1))
            else:
                self.emit(e, lambda en: en.drain().then_inc(self.handles["bar"], 1))
        for e in ENGINES:
            self.waited[e]["bar"] = target
            self.emit(e, lambda en, v=target: en.wait_ge(self.handles["bar"], v))

    def replay(self, block):
        for e in ENGINES:
            fns = self.q[e]
            getattr(block, e)(lambda eng, fns=fns: [f(eng) for f in fns])

    def build_sems(self, ctx):
        for name in self.cnt:
            self.handles[name] = ctx.enter_context(self.nc.semaphore(name))



class WStream:
    def __init__(self, p, ctx, name, shape, nbuf, dma_eng="sync",
                 cast_engines=("scalar", "vector"), rel_sem="ts"):
        self.p, self.name, self.nbuf = p, name, nbuf
        nc = p.nc
        self.f32 = [ctx.enter_context(nc.sbuf_tensor(f"{name}_f{i}", list(shape), F32))
                    for i in range(nbuf)]
        self.bf = [ctx.enter_context(nc.sbuf_tensor(f"{name}_b{i}", list(shape), BF16))
                   for i in range(nbuf)]
        self.ld = [p.sem(f"{name}_ld{i}") for i in range(nbuf)]
        self.cast_engines = cast_engines
        self.cast_sems = {e: p.sem(f"{name}_c_{e}") for e in cast_engines}
        self.dma_eng = dma_eng
        self.rel_sem = rel_sem
        self.i = 0
        self.tok = {}
        self.rel = {}

    def load(self, in_ap):
        p, i = self.p, self.i
        b = i % self.nbuf
        old = i - self.nbuf
        if old >= 0:
            assert old in self.rel, f"{self.name}: rel({old}) not set before load({i})"
            p.wait(self.dma_eng, self.rel_sem, self.rel[old])
        out_ap = self.f32[b][:, :]
        if len(in_ap.shape) == 3:
            out_ap = out_ap.rearrange("p (k c) -> p k c", k=in_ap.shape[1])
        ldv = p.dma(self.dma_eng, out_ap, in_ap, self.ld[b])
        ce = self.cast_engines[i % len(self.cast_engines)]
        p.wait(ce, self.ld[b], ldv)
        cs = self.cast_sems[ce]
        cv = p.inc(cs, 1)
        if ce == "scalar":
            p.emit(ce, lambda e, b=b, s=cs: e.copy(out=self.bf[b][:, :],
                                                   in_=self.f32[b][:, :])
                   .then_inc(p.handles[s], 1))
        else:
            p.emit(ce, lambda e, b=b, s=cs: e.tensor_copy(out=self.bf[b][:, :],
                                                          in_=self.f32[b][:, :])
                   .then_inc(p.handles[s], 1))
        self.tok[i] = (b, ce, cv)
        self.i += 1
        return i

    def wait_bf(self, eng, tok):
        b, ce, cv = self.tok[tok]
        self.p.wait(eng, self.cast_sems[ce], cv)
        return self.bf[b]

    def set_rel(self, tok, val):
        assert val is not None
        self.rel[tok] = val


class BfRing:
    def __init__(self, p, ctx, name, shape, nbuf, dma_eng="sync"):
        self.p, self.name, self.nbuf = p, name, nbuf
        nc = p.nc
        self.bf = [ctx.enter_context(nc.sbuf_tensor(f"{name}_b{i}", list(shape), BF16))
                   for i in range(nbuf)]
        self.ld = [p.sem(f"{name}_ld{i}") for i in range(nbuf)]
        self.dma_eng = dma_eng
        self.i = 0
        self.tok = {}
        self.rel = {}
        self.rel_sem = "ts"

    def load(self, in_ap):
        p, i = self.p, self.i
        b = i % self.nbuf
        old = i - self.nbuf
        if old >= 0:
            p.wait(self.dma_eng, self.rel_sem, self.rel[old])
        out_ap = self.bf[b][:, :]
        if len(in_ap.shape) == 3:
            out_ap = out_ap.rearrange("p (k c) -> p k c", k=in_ap.shape[1])
        ldv = p.dma(self.dma_eng, out_ap, in_ap, self.ld[b])
        self.tok[i] = (b, self.ld[b], ldv)
        self.i += 1
        return i

    def wait_bf(self, eng, tok):
        b, sem, v = self.tok[tok]
        self.p.wait(eng, sem, v)
        return self.bf[b]

    def set_rel(self, tok, val):
        self.rel[tok] = val



def ffn_phase(p, ctx, aps, cfg):
    nc = p.nc
    NTOK, GRP = cfg["ntok"], cfg["grp"]
    ngrp = NTOK // GRP
    TCH = GRP // 128
    NI, KD = 64, 16
    ts, ss, vs = p.sem("ts"), p.sem("ss"), p.sem("vs")
    gst = p.sem("gst")
    xld2 = [p.sem(f"fxld{i}") for i in range(2)]

    xn_g = ctx.enter_context(nc.sbuf_tensor("xn_g", [128, KD * GRP], BF16))
    h = ctx.enter_context(nc.sbuf_tensor("h", [128, NI * GRP], BF16))
    sg = [ctx.enter_context(nc.sbuf_tensor(f"sg{j}", [128, GRP], BF16)) for j in range(2)]
    gfm = [ctx.enter_context(nc.sbuf_tensor(f"gfm{j}", [128, GRP], BF16)) for j in range(2)]
    g_tm = ctx.enter_context(nc.sbuf_tensor("g_tm", [128, TCH * 2048], BF16))

    st_gu = WStream(p, ctx, "wgu", (128, KD * 128), nbuf=4)
    st_dn = WStream(p, ctx, "wdn", (128, 16 * 128), nbuf=4)

    pb, pbbf, ident = aps["pb"], aps["pbbf"], aps["ident_sb"]
    wgu4, wdn4, xnsc, gsc = aps["wgu4"], aps["wdn4"], aps["xnsc"], aps["gsc"]

    LA = 3
    LA_DN = LA
    assert LA < st_gu.nbuf and LA_DN < st_dn.nbuf

    silu_v = {}
    hmul_v = {}
    dsilu_v = {}
    dlast_v = {}
    tp_v = {}
    ev_v = {}
    gu_last = 0
    gu_last2 = {}
    dn_last = 0
    gst_prev = 0

    for g in range(ngrp):
        if "xn_ready" in aps:
            p.wait("scalar", aps["xn_ready"][0], aps["xn_ready"][1])
        p.wait("scalar", ts, gu_last)
        xv = p.dma("scalar", xn_g[:, :].rearrange("p (k t) -> p k t", k=KD),
                   xnsc[:, :, g * GRP:(g + 1) * GRP].rearrange("k p t -> p k t"),
                   xld2[g % 2])
        p.wait("tensor", xld2[g % 2], xv)

        toks = {}
        def issue_gu(j):
            if 0 <= j < 2 * NI:
                i, is_up = j // 2, j % 2
                toks[j] = st_gu.load(wgu4[64 + i] if is_up else wgu4[i])
        for j in range(LA):
            issue_gu(j)
        p.wait("vector", ts, dn_last)
        for i in range(NI):
            for half in (0, 1):
                j = 2 * i + half
                issue_gu(j + LA)
                bank = pb[2 * half + (i % 2)]
                buf = st_gu.wait_bf("tensor", toks[j])
                if i >= 2:
                    p.wait("tensor", ss if half == 0 else vs,
                           silu_v[g, i - 2] if half == 0 else hmul_v[g, i - 2])
                v = None
                for k in range(KD):
                    last = (k == KD - 1)
                    v = p.op("tensor",
                             lambda e, b=buf, k=k, bank=bank, last=last:
                             e.matmul(bank[:, :GRP], b[:, k * 128:(k + 1) * 128],
                                      xn_g[:, k * GRP:(k + 1) * GRP],
                                      start=(k == 0), stop=last),
                             sem=(ts if last else None)) or v
                st_gu.set_rel(toks[j], v)
            gu_last = v
            gu_last2[g] = v
            p.wait("scalar", ts, st_gu.rel[toks[2 * i]])
            if i >= 2:
                p.wait("scalar", vs, hmul_v[g, i - 2])
            silu_v[g, i] = p.op("scalar", lambda e, i=i: e.activation(
                out=sg[i % 2][:, :], in_=pb[0 + (i % 2)][:, :GRP], func=AF.Silu), sem=ss)
            p.wait("vector", ts, st_gu.rel[toks[2 * i + 1]])
            p.wait("vector", ss, silu_v[g, i])
            hmul_v[g, i] = p.op("vector", lambda e, i=i: e.tensor_mul(
                out=h[:, i * GRP:(i + 1) * GRP], in0=sg[i % 2][:, :],
                in1=pb[2 + (i % 2)][:, :GRP]), sem=vs)

        dtoks = {}
        def issue_dn(jj):
            if 0 <= jj < 16 * 4:
                d, qtr = jj // 4, jj % 4
                dtoks[jj] = st_dn.load(wdn4[d, :, qtr * 16:(qtr + 1) * 16, :])
        for jj in range(LA_DN):
            issue_dn(jj)

        def transpose_batch(d):
            p.wait("tensor", ss, dsilu_v[g, d])
            if (g, d - 2) in ev_v:
                p.wait("tensor", vs, ev_v[g, d - 2])
            v = None
            for t in range(TCH):
                v = p.op("tensor", lambda e, d=d, t=t: e.transpose(
                    pbbf[d % 2][:, t * 128:(t + 1) * 128],
                    gfm[d % 2][:, t * 128:(t + 1) * 128], ident[:, :]),
                    sem=(ts if t == TCH - 1 else None)) or v
            tp_v[g, d] = v
            p.wait("vector", ts, v)
            p.wait("vector", gst, gst_prev)
            out_ap = g_tm[:, :].rearrange("p (t x) -> p t x", t=TCH)[:, :, d * 128:(d + 1) * 128]
            in_ap = pbbf[d % 2][:, :TCH * 128].rearrange("p (t c) -> p t c", t=TCH)
            ev_v[g, d] = p.op("vector", lambda e, o=out_ap, i=in_ap:
                              e.tensor_copy(out=o, in_=i), sem=vs)

        p.wait("tensor", vs, hmul_v[g, NI - 1])
        for d in range(16):
            bank = pb[4 + (d % 2)]
            if d >= 2:
                p.wait("tensor", ss, dsilu_v[g, d - 2])
            v = None
            for qtr in range(4):
                jj = 4 * d + qtr
                issue_dn(jj + LA_DN)
                buf = st_dn.wait_bf("tensor", dtoks[jj])
                for kk in range(16):
                    i = qtr * 16 + kk
                    v = p.op("tensor",
                             lambda e, b=buf, kk=kk, i=i, bank=bank:
                             e.matmul(bank[:, :GRP], b[:, kk * 128:(kk + 1) * 128],
                                      h[:, i * GRP:(i + 1) * GRP],
                                      start=(i == 0), stop=(i == NI - 1)),
                             sem=(ts if kk == 15 else None)) or v
                st_dn.set_rel(dtoks[jj], v)
            dlast_v[g, d] = v
            p.wait("scalar", ts, v)
            if (g, d - 2) in tp_v:
                p.wait("scalar", ts, tp_v[g, d - 2])
            dsilu_v[g, d] = p.op("scalar", lambda e, d=d: e.activation(
                out=gfm[d % 2][:, :], in_=pb[4 + (d % 2)][:, :GRP], func=AF.Silu), sem=ss)
            if d >= 1:
                transpose_batch(d - 1)
        transpose_batch(15)
        dn_last = dlast_v[g, 15]

        for t in range(TCH):
            p.wait("gpsimd", vs, ev_v[g, 15])
            row0 = cfg["row0"] + g * GRP + t * 128
            gst_prev = p.dma("gpsimd", gsc[row0:row0 + 128, :],
                             g_tm[:, t * 2048:(t + 1) * 2048], gst)

    return {"gst": p.cnt["gst"], "ts": p.cnt["ts"]}



class PairPsum:
    def __init__(self, p, banks):
        self.p = p
        self.pairs = [(banks[0], banks[1]), (banks[2], banks[3])]
        self.j = 0
        self.ev = {}

    def next(self):
        i = self.j % 2
        self.j += 1
        for sem, val in self.ev.get(i, ()):
            self.p.wait("tensor", sem, val)
        return i

    def set_ev(self, i, evs):
        self.ev[i] = evs


def sweep_phase(p, ctx, aps, cfg):
    nc = p.nc
    NTOK, GRP = cfg["ntok"], cfg["grp"]
    ngrp, TCH, KD, NH = NTOK // GRP, GRP // 128, 16, 16
    ts, ss, vs, gs = p.sem("ts"), p.sem("ss"), p.sem("vs"), p.sem("gs")
    spl = p.sem("spl")
    sphi, solo = p.sem("sphi"), p.sem("solo")

    x_ap, wqT, wkT, wvT = aps["x"], aps["wqT"], aps["wkT"], aps["wvT"]
    xnsc, phisc, olocsc = aps["xnsc"], aps["phisc"], aps["olocsc"]
    pb, pbbf = aps["pb"], aps["pbbf"]
    ident, ident4 = aps["ident_sb"], aps["ident4_sb"]
    triu4, eye16 = aps["triu4_sb"], aps["eye16"]
    maskD, maskOff = aps["maskD_sb"], aps["maskOff_sb"]

    G = ctx.enter_context(nc.sbuf_tensor("G", [128, NH * 128], F32))
    Hs = ctx.enter_context(nc.sbuf_tensor("Hs", [128, NH * 128], F32))
    Gt = ctx.enter_context(nc.sbuf_tensor("Gt", [128, NH * 128], F32))
    G_bf = ctx.enter_context(nc.sbuf_tensor("G_bf", [128, NH * 128], BF16))
    H_bf = ctx.enter_context(nc.sbuf_tensor("H_bf", [128, NH * 128], BF16))
    aps["G"], aps["Hs"], aps["Gt"] = G, Hs, Gt

    xb = [ctx.enter_context(nc.sbuf_tensor(f"xb{i}", [128, 2048], F32)) for i in range(2)]
    xn_tm = [ctx.enter_context(nc.sbuf_tensor(f"xntm{i}", [128, 2048], BF16)) for i in range(2)]
    xn_fm = ctx.enter_context(nc.sbuf_tensor("xn_fm", [128, KD * GRP], BF16))
    q_fm = ctx.enter_context(nc.sbuf_tensor("q_fm", [128, 16 * GRP], BF16))
    k_tm = ctx.enter_context(nc.sbuf_tensor("k_tm", [128, TCH * 2048], BF16))
    v_tm = ctx.enter_context(nc.sbuf_tensor("v_tm", [128, TCH * 2048], BF16))
    ksl = [ctx.enter_context(nc.sbuf_tensor(f"ksl{i}", [128, 512], BF16)) for i in range(4)]
    ssq = ctx.enter_context(nc.sbuf_tensor("ssq", [128, 64], F32))
    junk = ctx.enter_context(nc.sbuf_tensor("junk", [128, 512], BF16))
    lnst = ctx.enter_context(nc.sbuf_tensor("lnst", [128, 8], F32))
    beta_g = ctx.enter_context(nc.sbuf_tensor("beta_g", [128, TCH * 16], F32))
    wb_sb = ctx.enter_context(nc.sbuf_tensor("wb_sb", [128, 256], BF16))

    def wb(name, dt=BF16):
        return ctx.enter_context(nc.sbuf_tensor(name, [128, 1024], dt))
    WB = []
    for wv in range(2):
        B = {}
        for nm in ("Kf", "NT", "Da", "Dat", "No", "C0K", "C0V",
                   "Pb", "Pbt", "X0", "BK", "BV", "Wp", "U0"):
            B[nm] = wb(f"{nm}_{wv}")
        B["R"] = wb(f"R_{wv}", F32)
        B["Zt"] = B["BK"]
        B["Mk"] = B["BV"]
        B["Qe"] = B["Da"]
        B["phi"] = B["Dat"]
        B["ol"] = B["X0"]
        WB.append(B)
    spill_g = {"phi": {}, "oloc": {}}
    wave_last_ts = {}

    st_w = BfRing(p, ctx, "wsl", (128, 512), nbuf=6)

    dbg_slot = [0]
    def dbg(buf, width=1024):
        if "dbgsc" not in aps:
            return
        i = dbg_slot[0]
        dbg_slot[0] += 1
        for e in ("ss", "vs", "gs", "ts"):
            p.wait("gpsimd", e, p.cnt[e])
        dv = p.dma("gpsimd", aps["dbgsc"][i][:, :width].rearrange("p (a n) -> p a n", n=512 if width >= 512 else width),
              buf[:, :width].rearrange("p (a n) -> p a n", n=512 if width >= 512 else width),
              p.sem("dbgs"))
        for e in ENGINES:
            p.wait(e, "dbgs", dv)

    ild = p.sem("ild")
    e1 = p.dma("gpsimd", G[:, :].rearrange("p (a b) -> p a b", b=512), eye16, ild)
    e2 = p.dma("gpsimd", Gt[:, :].rearrange("p (a b) -> p a b", b=512), eye16, ild)
    b1 = p.dma("gpsimd", xb[0][:, :256], aps["wbeta16_dram"], ild)
    p.wait("vector", ild, b1)
    p.op("vector", lambda e: e.memset(Hs[:, :], 0.0))
    p.op("vector", lambda e: e.memset(H_bf[:, :], 0.0))
    iv = p.op("vector", lambda e: e.tensor_copy(out=G_bf[:, :], in_=G[:, :]), sem=vs)
    p.wait("scalar", ild, b1)
    is_ = p.op("scalar", lambda e: e.copy(out=wb_sb[:, :], in_=xb[0][:, :256]), sem=ss)
    p.wait("tensor", vs, iv)
    p.wait("tensor", ss, is_)
    p.wait("gpsimd", vs, iv)

    wpst = [p.sem(f"wpst{i}") for i in range(2)]
    wpld = [p.sem(f"wpld{i}") for i in range(2)]
    prep_tiles = [(wi, r) for wi in range(3) for r in range(16)]
    wTs = [wqT, wkT, wvT]
    p.wait("scalar", ss, p.cnt[ss])
    prep_last_cast = {}
    for i, (wi, r) in enumerate(prep_tiles):
        bb = i % 2
        if i >= 2:
            p.wait("scalar", wpst[bb], 16 * (i // 2))
        lv = p.dma("scalar", xb[bb][:, :].rearrange("p (a n) -> p a n", n=512),
                   wTs[wi][r * 128:(r + 1) * 128, :]
                   .rearrange("p (a n) -> p a n", n=512), wpld[bb])
        ce = "vector" if i % 2 else "scalar"
        p.wait(ce, wpld[bb], lv)
        if i >= 2:
            p.wait(ce, wpst[bb], 16 * (i // 2))
        sem = vs if ce == "vector" else ss
        if ce == "vector":
            cv2 = p.op(ce, lambda e, bb=bb: e.tensor_copy(
                out=xn_tm[bb][:, :], in_=xb[bb][:, :]), sem=sem)
        else:
            cv2 = p.op(ce, lambda e, bb=bb: e.copy(
                out=xn_tm[bb][:, :], in_=xb[bb][:, :]), sem=sem)
        prep_last_cast[bb] = (sem, cv2)
        p.wait("gpsimd", sem, cv2)
        p.dma("gpsimd", aps["wqkvbf"][wi][r * 128:(r + 1) * 128, :]
              .rearrange("p (a n) -> p a n", n=512),
              xn_tm[bb][:, :].rearrange("p (a n) -> p a n", n=512), wpst[bb])
    wpst_fin = [p.cnt["wpst0"], p.cnt["wpst1"]]

    xld = [p.sem(f"xch{i}") for i in range(2)]
    p.wait("sync", ss, p.cnt[ss])
    ln_v = {}
    tpev = {}
    pbbf_state = {"i": 0, "ev": {}, }

    def transpose_batch4(srcs, out_ap, eng):
        i = pbbf_state["i"] % 2
        pbbf_state["i"] += 1
        bank = pbbf[i]
        pe = pbbf_state["ev"].get(i)
        if pe is not None:
            p.wait("tensor", pe[0], pe[1])
        v = None
        for t, s in enumerate(srcs):
            v = p.op("tensor", lambda e, s=s, t=t, bank=bank: e.transpose(
                bank[:, t * 128:(t + 1) * 128], s, ident[:, :]),
                sem=(ts if t == len(srcs) - 1 else None)) or v
        p.wait(eng, ts, v)
        sem = ss if eng == "scalar" else vs
        if eng == "scalar":
            ev = p.op(eng, lambda e, o=out_ap, bank=bank: e.copy(
                out=o, in_=bank[:, :512]), sem=sem)
        else:
            ev = p.op(eng, lambda e, o=out_ap, bank=bank: e.tensor_copy(
                out=o, in_=bank[:, :512]), sem=sem)
        pbbf_state["ev"][i] = (sem, ev)
        return sem, ev

    pp = PairPsum(p, [pb[0][:, :], pb[1][:, :], pb[2][:, :], pb[3][:, :]])

    def slot(pair_i, h):
        bank = pp.pairs[pair_i][h // 4]
        return bank[:, (h % 4) * 128:(h % 4 + 1) * 128]

    def mms8(ops, accs=None):
        i = pp.next()
        v = None
        for h, (l, r) in enumerate(ops):
            if accs is None:
                v = p.op("tensor", lambda e, s=slot(i, h), l=l, r=r:
                         e.matmul(s, l, r, start=True, stop=True),
                         sem=(ts if h == 7 else None)) or v
            else:
                p.op("tensor", lambda e, s=slot(i, h), l=l, r=r:
                     e.matmul(s, l, r, start=True, stop=False))
                v = p.op("tensor", lambda e, s=slot(i, h), l=accs[h][0], r=accs[h][1]:
                         e.matmul(s, l, r, start=False, stop=True),
                         sem=(ts if h == 7 else None)) or v
        return i, v

    def evac8(pair_i, mm_v, outs, engs=("vector", "scalar"), fns=(None, None)):
        res = []
        for b in range(2):
            eng = engs[b]
            sem = vs if eng == "vector" else ss
            p.wait(eng, ts, mm_v)
            bank = pp.pairs[pair_i][b]
            fn = fns[b]
            if fn is None:
                if eng == "scalar":
                    val = p.op(eng, lambda e, o=outs[b], i_=bank: e.copy(
                        out=o, in_=i_[:, :]), sem=sem)
                else:
                    val = p.op(eng, lambda e, o=outs[b], i_=bank: e.tensor_copy(
                        out=o, in_=i_[:, :]), sem=sem)
            else:
                val = p.op(eng, lambda e, o=outs[b], i_=bank, f=fn: f(e, o, i_),
                           sem=sem)
            res.append((sem, val))
        pp.set_ev(pair_i, res)
        return res

    ev_q, ev_k, ev_v = {}, {}, {}
    beta_v = 0
    spill_seq = [0]

    spl_group = 0
    phi_spl = 0
    oloc_spl = 0
    for g in range(ngrp):
        p.wait("tensor", spl, spl_group)
        for c in range(TCH):
            gc = g * TCH + c
            b = gc % 2
            p.wait("sync", vs, ln_v.get(gc - 2, 0))
            if gc < 2:
                sem_c, v_c = prep_last_cast[b]
                p.wait("sync", sem_c, v_c)
                p.wait("scalar", wpst[b], wpst_fin[b])
                p.wait("vector", wpst[b], wpst_fin[b])
            xv = p.dma("sync", xb[b][:, :].rearrange("p (a n) -> p a n", n=512),
                       x_ap[gc * 128:(gc + 1) * 128, :]
                       .rearrange("p (a n) -> p a n", n=512), xld[b])
            p.wait("vector", xld[b], xv)
            p.wait("scalar", xld[b], xv)
            s1v = p.op("vector", lambda e, b=b: e.reduce_sum(
                out=lnst[:, 0:1], in_=xb[b][:, :], axis=AX.X), sem=vs)
            s2v = p.op("scalar", lambda e, b=b: e.activation(
                out=xn_tm[b][:, :], in_=xb[b][:, :], func=AF.Square,
                accum_out=lnst[:, 1:2]), sem=ss)
            p.wait("vector", ss, s2v)
            p.chain("vector")
            p.op("vector", lambda e: e.tensor_scalar_mul(
                out=lnst[:, 2:3], in0=lnst[:, 0:1], scalar1=1.0 / 2048.0), sem=vs)
            p.chain("vector")
            p.op("vector", lambda e: e.tensor_mul(
                out=lnst[:, 3:4], in0=lnst[:, 2:3], in1=lnst[:, 2:3]), sem=vs)
            p.op("vector", lambda e: e.tensor_scalar_mul(
                out=lnst[:, 4:5], in0=lnst[:, 1:2], scalar1=1.0 / 2048.0), sem=vs)
            p.chain("vector")
            vv2 = p.op("vector", lambda e: e.tensor_sub(
                out=lnst[:, 4:5], in0=lnst[:, 4:5], in1=lnst[:, 3:4]), sem=vs)
            p.wait("scalar", vs, vv2)
            sq2 = p.op("scalar", lambda e: e.activation(
                out=lnst[:, 5:6], in_=lnst[:, 4:5], func=AF.Sqrt, bias=1e-5), sem=ss)
            p.wait("vector", ss, sq2)
            rc2 = p.op("vector", lambda e: e.reciprocal(
                out=lnst[:, 6:7], in_=lnst[:, 5:6]), sem=vs)
            p.chain("vector")
            rc2 = p.op("vector", lambda e: e.scalar_tensor_tensor(
                out=lnst[:, 7:8], in0=lnst[:, 2:3], scalar=-1.0,
                in1=lnst[:, 6:7], op0=ALU.mult, op1=ALU.mult), sem=vs)
            tp = tpev.get(gc - 2)
            if tp is not None:
                p.wait("vector", tp[0], tp[1])
            p.chain("vector")
            ln_v[gc] = p.op("vector", lambda e, b=b: e.tensor_scalar(
                out=xn_tm[b][:, :], in0=xb[b][:, :], scalar1=lnst[:, 6:7],
                scalar2=lnst[:, 7:8], op0=ALU.mult, op1=ALU.add), sem=vs)
            p.wait("tensor", vs, ln_v[gc])
            for kb in range(4):
                out_ap = (xn_fm[:, :].rearrange("p (k t) -> p k t", k=KD)
                          [:, kb * 4:(kb + 1) * 4, c * 128:(c + 1) * 128])
                tpev[gc] = transpose_batch4(
                    [xn_tm[b][:, (kb * 4 + j) * 128:(kb * 4 + j + 1) * 128]
                     for j in range(4)],
                    out_ap, "scalar" if kb % 2 else "vector")
        p.wait("gpsimd", ss, p.cnt[ss])
        p.wait("gpsimd", vs, p.cnt[vs])
        nsp = min(512, GRP)
        for k in range(KD):
            p.dma("gpsimd", xnsc[k][:, g * GRP:(g + 1) * GRP]
                  .rearrange("p (a n) -> p a n", n=nsp),
                  xn_fm[:, k * GRP:(k + 1) * GRP]
                  .rearrange("p (a n) -> p a n", n=nsp), spl)
        spl_group = p.cnt[spl]

        p.wait("tensor", ss, p.cnt[ss])
        p.wait("tensor", vs, p.cnt[vs])
        bank_reader = {}
        tiles = [(s, wn, kk) for s in range(4) for wn in ("q", "k", "v")
                 for kk in range(KD)]
        toks = {}
        LA_C = 4

        def issue_c(i):
            if 0 <= i < len(tiles):
                s_, wn, kk = tiles[i]
                wi = {"q": 0, "k": 1, "v": 2}[wn]
                pi = wi * 16 + kk
                p.wait("sync", wpst[pi % 2], 16 * (pi // 2 + 1))
                toks[i] = st_w.load(aps["wqkvbf"][wi][kk * 128:(kk + 1) * 128,
                                                      s_ * 512:(s_ + 1) * 512])

        for i in range(LA_C):
            issue_c(i)
        for i, (s, wn, kk) in enumerate(tiles):
            issue_c(i + LA_C)
            buf = st_w.wait_bf("tensor", toks[i])
            if kk == 0:
                for t in range(TCH):
                    br = bank_reader.get(t)
                    if br:
                        p.wait("tensor", br[0], br[1])
            v = None
            if wn == "q":
                for j in range(4):
                    v = p.op("tensor", lambda e, kk=kk, j=j, buf=buf:
                             e.matmul(pb[j][:, :GRP],
                                      buf[:, j * 128:(j + 1) * 128],
                                      xn_fm[:, kk * GRP:(kk + 1) * GRP],
                                      start=(kk == 0), stop=(kk == KD - 1)),
                             sem=(ts if j == 3 else None)) or v
            else:
                for t in range(TCH):
                    v = p.op("tensor", lambda e, kk=kk, t=t, buf=buf:
                             e.matmul(pb[t][:, :],
                                      xn_fm[:, kk * GRP + t * 128:kk * GRP + (t + 1) * 128],
                                      buf[:, :],
                                      start=(kk == 0), stop=(kk == KD - 1)),
                             sem=(ts if t == TCH - 1 else None)) or v
            st_w.set_rel(toks[i], v)
            if kk != KD - 1:
                continue
            mm_done = v
            for t in range(TCH):
                if wn == "q":
                    p.wait("scalar", ts, mm_done)
                    h_ = s * 4 + t
                    ev_q[(s, t)] = p.op("scalar", lambda e, h_=h_, t=t: e.activation(
                        out=q_fm[:, h_ * GRP:(h_ + 1) * GRP],
                        in_=pb[t][:, :GRP], func=AF.Silu), sem=ss)
                    bank_reader[t] = (ss, ev_q[(s, t)])
                elif wn == "v":
                    p.wait("vector", ts, mm_done)
                    ev_v[(s, t)] = p.op("vector", lambda e, s=s, t=t: e.tensor_copy(
                        out=v_tm[:, t * 2048 + s * 512:t * 2048 + (s + 1) * 512],
                        in_=pb[t][:, :]), sem=vs)
                    bank_reader[t] = (vs, ev_v[(s, t)])
                else:
                    p.wait("scalar", ts, mm_done)
                    pv = ev_k.get((s - 1, t))
                    if pv is not None:
                        p.wait("scalar", vs, pv)
                    sl_v = p.op("scalar", lambda e, t=t: e.activation(
                        out=ksl[t][:, :], in_=pb[t][:, :], func=AF.Silu), sem=ss)
                    bank_reader[t] = (ss, sl_v)
                    p.wait("vector", ss, sl_v)
                    c0_ = t * 16 + s * 4
                    sq_v = None
                    for hh in range(4):
                        sq_v = p.op("vector", lambda e, t=t, hh=hh, c0_=c0_:
                                    e.scalar_tensor_tensor(
                            out=junk[:, hh * 128:(hh + 1) * 128],
                            in0=ksl[t][:, hh * 128:(hh + 1) * 128],
                            scalar=1.0,
                            in1=ksl[t][:, hh * 128:(hh + 1) * 128],
                            op0=ALU.mult, op1=ALU.mult,
                            accum_out=ssq[:, c0_ + hh:c0_ + hh + 1]), sem=vs)
                    p.wait("scalar", vs, sq_v)
                    sq_s = p.op("scalar", lambda e, c0_=c0_: e.activation(
                        out=ssq[:, c0_:c0_ + 4], in_=ssq[:, c0_:c0_ + 4],
                        func=AF.Sqrt, bias=1e-6), sem=ss)
                    p.wait("vector", ss, sq_s)
                    p.op("vector", lambda e, c0_=c0_: e.reciprocal(
                        out=ssq[:, c0_:c0_ + 4], in_=ssq[:, c0_:c0_ + 4]), sem=vs)
                    p.chain("vector")
                    nv = None
                    for hh in range(4):
                        nv = p.op("vector", lambda e, t=t, s=s, hh=hh, c0_=c0_:
                                  e.tensor_scalar(
                            out=k_tm[:, t * 2048 + (s * 4 + hh) * 128:
                                     t * 2048 + (s * 4 + hh + 1) * 128],
                            in0=ksl[t][:, hh * 128:(hh + 1) * 128],
                            scalar1=ssq[:, c0_ + hh:c0_ + hh + 1],
                            scalar2=None, op0=ALU.mult), sem=vs)
                    ev_k[(s, t)] = nv
        bv = None
        for t in range(TCH):
            for kk in range(KD):
                bv = p.op("tensor", lambda e, t=t, kk=kk: e.matmul(
                    pb[4][:, t * 16:(t + 1) * 16],
                    xn_fm[:, kk * GRP + t * 128:kk * GRP + (t + 1) * 128],
                    wb_sb[:, kk * 16:(kk + 1) * 16],
                    start=(kk == 0), stop=(kk == KD - 1)),
                    sem=(ts if (t == TCH - 1 and kk == KD - 1) else None)) or bv
        p.wait("scalar", ts, bv)
        sg_v = p.op("scalar", lambda e: e.activation(
            out=beta_g[:, :], in_=pb[4][:, :TCH * 16], func=AF.Sigmoid), sem=ss)
        p.wait("vector", ss, sg_v)
        beta_v = p.op("vector", lambda e: e.tensor_scalar_mul(
            out=beta_g[:, :], in0=beta_g[:, :], scalar1=2.0), sem=vs)

        p.wait("tensor", ss, p.cnt[ss])
        p.wait("tensor", vs, p.cnt[vs])

        def wave_gen(c, w, B, pair):
            gc_ = g * TCH + c
            H0 = 8 * w
            kc = lambda h: k_tm[:, c * 2048 + (H0 + h) * 128:c * 2048 + (H0 + h + 1) * 128]
            vc = lambda h: v_tm[:, c * 2048 + (H0 + h) * 128:c * 2048 + (H0 + h + 1) * 128]
            qf = lambda h: q_fm[:, (H0 + h) * GRP + c * 128:(H0 + h) * GRP + (c + 1) * 128]
            bcol = lambda h: beta_g[:, c * 16 + H0 + h:c * 16 + H0 + h + 1]
            gG = lambda buf, h: buf[:, (H0 + h) * 128:(H0 + h + 1) * 128]
            hs = lambda buf, h: buf[:, h * 128:(h + 1) * 128]
            half2 = lambda buf: (buf[:, :512], buf[:, 512:])

            pend = {"ev": ()}

            def mms8p(ops, accs=None):
                for sem, val in pend["ev"]:
                    p.wait("tensor", sem, val)
                v = None
                for h, (l, r) in enumerate(ops):
                    s_ = pair[h // 4][:, (h % 4) * 128:(h % 4 + 1) * 128]
                    if accs is None:
                        v = p.op("tensor", lambda e, s=s_, l=l, r=r:
                                 e.matmul(s, l, r, start=True, stop=True),
                                 sem=(ts if h == 7 else None)) or v
                    else:
                        p.op("tensor", lambda e, s=s_, l=l, r=r:
                             e.matmul(s, l, r, start=True, stop=False))
                        v = p.op("tensor", lambda e, s=s_, l=accs[h][0], r=accs[h][1]:
                                 e.matmul(s, l, r, start=False, stop=True),
                                 sem=(ts if h == 7 else None)) or v
                return v

            def evac8p(mm_v, outs, engs=("vector", "scalar"), fns=(None, None)):
                res = []
                for b_ in range(2):
                    eng = engs[b_]
                    sem = vs if eng == "vector" else ss
                    p.wait(eng, ts, mm_v)
                    bank = pair[b_]
                    fn = fns[b_]
                    if fn is None:
                        if eng == "scalar":
                            val = p.op(eng, lambda e, o=outs[b_], i_=bank: e.copy(
                                out=o, in_=i_[:, :]), sem=sem)
                        else:
                            val = p.op(eng, lambda e, o=outs[b_], i_=bank: e.tensor_copy(
                                out=o, in_=i_[:, :]), sem=sem)
                    else:
                        val = p.op(eng, lambda e, o=outs[b_], i_=bank, f=fn: f(e, o, i_),
                                   sem=sem)
                    res.append((sem, val))
                pend["ev"] = tuple(res)
                return res

            p.wait("gpsimd", ts, wave_last_ts.get(w, 0))
            p.wait("gpsimd", vs, max(ev_k[(s2, c)] for s2 in range(4)))
            p.wait("gpsimd", vs, max(ev_v[(s2, c)] for s2 in range(4)))
            p.wait("gpsimd", vs, beta_v)
            p.wait("scalar", f"sphi{w}", spill_g["phi"].get(w, 0))
            p.wait("vector", f"sphi{w}", spill_g["phi"].get(w, 0))
            p.wait("scalar", f"solo{w}", spill_g["oloc"].get(w, 0))
            p.wait("vector", f"solo{w}", spill_g["oloc"].get(w, 0))
            gv_ = None
            for h in range(8):
                p.op("gpsimd", lambda e, o=B["BK"][:, h * 128:(h + 1) * 128],
                     i_=kc(h), sc=bcol(h):
                     e.tensor_scalar(out=o, in0=i_, scalar1=sc,
                                     scalar2=None, op0=ALU.mult), sem=gs)
                gv_ = p.op("gpsimd", lambda e, o=B["BV"][:, h * 128:(h + 1) * 128],
                           i_=vc(h), sc=bcol(h):
                           e.tensor_scalar(out=o, in0=i_, scalar1=sc,
                                           scalar2=None, op0=ALU.mult), sem=gs)
            bkbv = ((gs, gv_),)
            yield

            tev = []
            for half in range(2):
                tev.append(transpose_batch4([kc(half * 4 + j) for j in range(4)],
                                            B["Kf"][:, half * 512:(half + 1) * 512],
                                            "scalar"))
            yield
            for sem, val in tev:
                p.wait("tensor", sem, val)
            v1 = mms8p([(hs(B["Kf"], h), hs(B["Kf"], h)) for h in range(8)])
            p.wait("vector", ts, v1)
            dv = None
            for h in range(8):
                i_ = pair[h // 4][:, (h % 4) * 128:(h % 4 + 1) * 128]
                p.op("vector", lambda e, o=hs(B["Da"], h), i_=i_,
                     sc=bcol(h): e.scalar_tensor_tensor(
                    out=o, in0=i_, scalar=sc,
                    in1=maskD[:, :], op0=ALU.mult, op1=ALU.mult), sem=vs)
                dv = p.op("vector", lambda e, o=hs(B["No"], h), i_=i_,
                          sc=bcol(h): e.scalar_tensor_tensor(
                    out=o, in0=i_, scalar=sc,
                    in1=maskOff[:, :], op0=ALU.mult, op1=ALU.mult), sem=vs)
            pend["ev"] = ((vs, dv),)
            yield
            x0 = None
            p.wait("tensor", vs, dv)
            dat_ev = []
            for half in range(2):
                sem, ev = transpose_batch4([hs(B["Da"], half * 4 + j) for j in range(4)],
                                           B["Dat"][:, half * 512:(half + 1) * 512],
                                           "scalar")
                dat_ev.append((sem, ev))
                p.wait("vector", sem, ev)
                x0 = p.op("vector", lambda e, half=half: e.tensor_add(
                    out=B["X0"][:, half * 512:(half + 1) * 512],
                    in0=B["Dat"][:, half * 512:(half + 1) * 512],
                    in1=ident4[:, :]), sem=vs)
            yield
            X = B["X0"]
            Pc, Pct, Pn, Pnt = B["Da"], B["Dat"], B["Pb"], B["Pbt"]
            xready = ((vs, x0),)
            for sem, val in dat_ev:
                p.wait("tensor", sem, val)
            prev_ept = None
            for lev in range(1, 5):
                if prev_ept:
                    for sem, val in prev_ept:
                        p.wait("tensor", sem, val)
                vp = mms8p([(hs(Pct, h), hs(Pc, h)) for h in range(8)])
                e_p = evac8p(vp, half2(Pn), engs=("vector", "scalar"))
                yield
                if lev < 4:
                    vpt = mms8p([(hs(Pc, h), hs(Pct, h)) for h in range(8)])
                    prev_ept = evac8p(vpt, half2(Pnt), engs=("scalar", "vector"))
                    yield
                else:
                    prev_ept = None
                for sem, val in e_p:
                    p.wait("tensor", sem, val)
                for sem, val in xready:
                    p.wait("tensor", sem, val)
                vx = mms8p([(hs(Pn, h), hs(X, h)) for h in range(8)],
                           accs=[(ident[:, :], hs(X, h)) for h in range(8)])
                exv = evac8p(vx, half2(X), engs=("vector", "scalar"))
                pend["ev"] = tuple(exv)
                xready = exv
                Pc, Pn = Pn, Pc
                Pct, Pnt = Pnt, Pct
                yield
            XD = X
            for sem, val in xready:
                p.wait("tensor", sem, val)
            for sm, vv_ in bkbv:
                p.wait("tensor", sm, vv_)
            vc_ = mms8p([(hs(XD, h), hs(B["BK"], h)) for h in range(8)])
            e_ck = evac8p(vc_, half2(B["C0K"]), engs=("scalar", "vector"))
            yield
            vc2 = mms8p([(hs(XD, h), hs(B["BV"], h)) for h in range(8)])
            e_cv = evac8p(vc2, half2(B["C0V"]), engs=("vector", "scalar"))
            yield
            vn_ = mms8p([(hs(B["No"], h), hs(XD, h)) for h in range(8)])
            e_nt = evac8p(vn_, half2(B["NT"]), engs=("scalar", "vector"))
            yield
            for evs in (e_ck, e_cv, e_nt):
                for sem, val in evs:
                    p.wait("tensor", sem, val)
            SK, SV = B["C0K"], B["C0V"]
            for j in range(3):
                lastj = (j == 2)
                okK = B["Wp"] if lastj else B["X0"]
                okV = B["U0"] if lastj else B["Pb"]
                vhk = mms8p([(hs(B["NT"], h), hs(SK, h)) for h in range(8)],
                            accs=[(ident[:, :], hs(B["C0K"], h)) for h in range(8)])
                e_k = evac8p(vhk, half2(okK), engs=("vector", "scalar"))
                yield
                vhv = mms8p([(hs(B["NT"], h), hs(SV, h)) for h in range(8)],
                            accs=[(ident[:, :], hs(B["C0V"], h)) for h in range(8)])
                e_v = evac8p(vhv, half2(okV), engs=("scalar", "vector"))
                for sm, vv2_ in list(e_k) + list(e_v):
                    p.wait("tensor", sm, vv2_)
                SK, SV = okK, okV
                yield
            vz = mms8p([(hs(B["Wp"], h), kc(h)) for h in range(8)])
            e_z = evac8p(vz, half2(B["Zt"]), engs=("scalar", "vector"))
            yield
            vr = mms8p([(kc(h), hs(B["U0"], h)) for h in range(8)])
            e_r = evac8p(vr, half2(B["R"]), engs=("vector", "scalar"))
            yield
            va = mms8p([(hs(B["Kf"], h), qf(h)) for h in range(8)])
            mkfn = lambda e, o, i_: e.tensor_mul(out=o, in0=i_[:, :], in1=triu4[:, :])
            e_a = evac8p(va, half2(B["Mk"]), engs=("vector", "vector"),
                         fns=(mkfn, mkfn))
            yield
            for sm, vv2_ in list(e_z) + list(e_a):
                p.wait("tensor", sm, vv2_)
            vf = mms8p([(hs(B["Wp"], h), hs(B["Mk"], h)) for h in range(8)])
            qefn = lambda bk: (lambda e, o, i_, bk=bk, c=c, H0=H0: e.tensor_sub(
                out=o.rearrange("p (hh t) -> p hh t", hh=4),
                in0=q_fm[:, :].rearrange("p (hh t) -> p hh t", t=GRP)
                [:, H0 + bk * 4:H0 + bk * 4 + 4, c * 128:(c + 1) * 128],
                in1=i_[:, :].rearrange("p (hh t) -> p hh t", hh=4)))
            e_q = evac8p(vf, half2(B["Qe"]), engs=("vector", "vector"),
                         fns=(qefn(0), qefn(1)))
            yield
            for sm, vv2_ in e_q:
                p.wait("tensor", sm, vv2_)
            vg = mms8p([(gG(G_bf, h), hs(B["Qe"], h)) for h in range(8)])
            e_phi = evac8p(vg, half2(B["phi"]), engs=("scalar", "vector"))
            yield
            vo = mms8p([(hs(B["Mk"], h), hs(B["U0"], h)) for h in range(8)],
                       accs=[(hs(B["Qe"], h), gG(H_bf, h)) for h in range(8)])
            e_ol = evac8p(vo, half2(B["ol"]), engs=("vector", "scalar"))
            yield
            for sm, v_ in e_phi:
                p.wait("gpsimd", sm, v_)
            for sm, v_ in e_ol:
                p.wait("gpsimd", sm, v_)
            spill_g["phi"][w] = p.dma(
                "gpsimd", phisc[gc_, w][:, :].rearrange("p (a n) -> p a n", n=512),
                B["phi"][:, :].rearrange("p (a n) -> p a n", n=512), p.sem(f"sphi{w}"))
            spill_g["oloc"][w] = p.dma(
                "gpsimd", olocsc[gc_, w][:, :].rearrange("p (a n) -> p a n", n=512),
                B["ol"][:, :].rearrange("p (a n) -> p a n", n=512), p.sem(f"solo{w}"))
            gouts = (G[:, (H0 + 0) * 128:(H0 + 4) * 128],
                     G[:, (H0 + 4) * 128:(H0 + 8) * 128])
            gtouts = (Gt[:, (H0 + 0) * 128:(H0 + 4) * 128],
                      Gt[:, (H0 + 4) * 128:(H0 + 8) * 128])
            houts = (Hs[:, (H0 + 0) * 128:(H0 + 4) * 128],
                     Hs[:, (H0 + 4) * 128:(H0 + 8) * 128])
            vgt = mms8p([(gG(G_bf, h), hs(B["Zt"], h)) for h in range(8)])
            p.wait("vector", ts, vgt)
            gt2 = None
            for bk in range(2):
                gt2 = p.op("vector", lambda e, o=gtouts[bk], i_=pair[bk]:
                           e.scalar_tensor_tensor(
                    out=o, in0=i_[:, :], scalar=-1.0,
                    in1=o, op0=ALU.mult, op1=ALU.add), sem=vs)
            pend["ev"] = ((vs, gt2),)
            yield
            vzg = mms8p([(hs(B["Zt"], h), gG(G_bf, h)) for h in range(8)])
            p.wait("vector", ts, vzg)
            gv2 = None
            for bk in range(2):
                gv2 = p.op("vector", lambda e, o=gouts[bk], i_=pair[bk]:
                           e.scalar_tensor_tensor(
                    out=o, in0=i_[:, :], scalar=-1.0,
                    in1=o, op0=ALU.mult, op1=ALU.add), sem=vs)
            pend["ev"] = ((vs, gv2),)
            yield
            for sm, vv2_ in e_r:
                p.wait("vector", sm, vv2_)
            vzh = mms8p([(hs(B["Zt"], h), gG(H_bf, h)) for h in range(8)])
            p.wait("vector", ts, vzh)
            hv2 = None
            for bk in range(2):
                p.op("vector", lambda e, o=houts[bk], i_=pair[bk]:
                     e.scalar_tensor_tensor(
                    out=o, in0=i_[:, :], scalar=-1.0,
                    in1=o, op0=ALU.mult, op1=ALU.add), sem=vs)
                p.chain("vector")
                hv2 = p.op("vector", lambda e, o=houts[bk],
                           r=B["R"][:, bk * 512:(bk + 1) * 512]:
                           e.tensor_add(out=o, in0=o, in1=r), sem=vs)
            pend["ev"] = ((vs, hv2),)
            p.wait("scalar", vs, hv2)
            sl8 = slice(H0 * 128, (H0 + 8) * 128)
            cv = p.op("scalar", lambda e, sl8=sl8: e.copy(
                out=G_bf[:, sl8], in_=G[:, sl8]), sem=ss)
            cv = p.op("scalar", lambda e, sl8=sl8: e.copy(
                out=H_bf[:, sl8], in_=Hs[:, sl8]), sem=ss)
            p.wait("tensor", ss, cv)
            wave_last_ts[w] = vzh

        def wave_job(c, w):
            qv = max(ev_q[(s, c)] for s in range(4))
            kv_ = max(ev_k[(s, c)] for s in range(4))
            vv_ = max(ev_v[(s, c)] for s in range(4))
            p.wait("tensor", ss, qv)
            p.wait("tensor", vs, max(kv_, vv_, beta_v))
            p.wait("scalar", vs, max(kv_, vv_, beta_v))
            pairs = ((pb[0][:, :], pb[1][:, :]), (pb[2][:, :], pb[3][:, :]))
            yield from wave_gen(c, w, WB[w], pairs[w])

        jobs = [(c, w) for c in range(TCH) for w in range(2)]
        gens = []
        ji = 0
        primed = False
        while ji < len(jobs) or gens:
            while len(gens) < 2 and ji < len(jobs):
                g2 = wave_job(*jobs[ji])
                ji += 1
                gens.append(g2)
                if not primed:
                    primed = True
                    for _ in range(12):
                        try:
                            next(g2)
                        except StopIteration:
                            break
            for g2 in list(gens):
                try:
                    next(g2)
                except StopIteration:
                    gens.remove(g2)

    return {"spl": p.cnt["spl"], "sphi0": p.cnt["sphi0"], "sphi1": p.cnt["sphi1"],
            "solo0": p.cnt["solo0"], "solo1": p.cnt["solo1"], "ts": p.cnt[ts]}



def relay_send(p, aps):
    ts, vs, ss = p.sem("ts"), p.sem("vs"), p.sem("ss")
    ccs = p.sem("ccs")
    snd = p.sem("snd")
    p.wait("gpsimd", vs, p.cnt[vs])
    v1 = p.dma("gpsimd", aps["ccin"][0].rearrange("p (a n) -> p a n", n=512),
               aps["Gt"][:, :].rearrange("p (a n) -> p a n", n=512), snd)
    v2 = p.dma("gpsimd", aps["ccin"][1].rearrange("p (a n) -> p a n", n=512),
               aps["Hs"][:, :].rearrange("p (a n) -> p a n", n=512), snd)
    p.wait("gpsimd", snd, v2)
    cv = p.inc(ccs, 1)
    p.emit("gpsimd", lambda e: e.collective_compute(
        "AllGather", ALU.bypass, replica_groups=[list(range(8))],
        ins=[aps["ccin"].opt()], outs=[aps["ccout"].opt()],
    ).then_inc(p.handles[ccs], 1))
    return cv


def compose_phase(p, ctx, aps, cc_v):
    nc = p.nc
    NH = 16
    ts, ss, vs, gs = p.sem("ts"), p.sem("ss"), p.sem("vs"), p.sem("gs")
    ccs, cld = p.sem("ccs"), p.sem("cld")

    S_in = ctx.enter_context(nc.sbuf_tensor("S_in", [128, NH * 128], F32))
    S_in_bf = aps["S_in_bf_outer"]
    tstate = ctx.enter_context(nc.sbuf_tensor("tstate", [128, NH * 128], F32))
    t_bf = ctx.enter_context(nc.sbuf_tensor("t_bf", [128, NH * 128], BF16))
    gtj_bf = ctx.enter_context(nc.sbuf_tensor("gtj_bf", [128, NH * 128], BF16))
    hj = [ctx.enter_context(nc.sbuf_tensor(f"hj{i}", [128, NH * 128], F32))
          for i in range(2)]
    gtj_f = [ctx.enter_context(nc.sbuf_tensor(f"gtj_f{i}", [128, NH * 128], F32))
             for i in range(2)]
    aps["S_in_bf"] = S_in_bf

    pb = aps["pb"]
    sel = aps["sel_sb"]

    p.wait("gpsimd", ccs, cc_v)
    p.wait("sync", ccs, cc_v)
    p.wait("tensor", ss, p.cnt[ss])
    p.wait("tensor", vs, p.cnt[vs])
    mz = p.op("vector", lambda e: e.memset(tstate[:, :], 0.0), sem=vs)
    mz = p.op("vector", lambda e: e.memset(S_in[:, :], 0.0), sem=vs)
    mz = p.op("vector", lambda e: e.memset(t_bf[:, :], 0.0), sem=vs)
    cld2 = p.sem("cld2")
    cgt_prev, hj_prev = {}, {}
    gvals = {}
    clds = [p.sem(f"cldr{i}") for i in range(2)]
    cld2s = [p.sem(f"cld2r{i}") for i in range(2)]

    def cload(j):
        jb = j % 2
        p.wait("sync", ss, cgt_prev.get(j - 2, 0))
        p.wait("sync", vs, hj_prev.get(j - 2, 0))
        g1 = p.dma("sync", gtj_f[jb][:, :].rearrange("p (a n) -> p a n", n=512),
                   aps["ccout"][2 * j].rearrange("p (a n) -> p a n", n=512), clds[jb])
        g2 = p.dma("sync", hj[jb][:, :].rearrange("p (a n) -> p a n", n=512),
                   aps["ccout"][2 * j + 1].rearrange("p (a n) -> p a n", n=512),
                   cld2s[jb])
        gvals[j] = (g1, g2)

    cload(0)
    for j in range(7):
        jb = j % 2
        if j < 6:
            cload(j + 1)
        g1, g2 = gvals[j]
        p.wait("scalar", clds[jb], g1)
        cgt = p.op("scalar", lambda e, jb=jb: e.copy(
            out=gtj_bf[:, :], in_=gtj_f[jb][:, :]), sem=ss)
        cgt_prev[j] = cgt
        p.wait("tensor", ss, cgt)
        p.wait("tensor", vs, mz)
        v = None
        for h in range(NH):
            bank = pb[h // 4]
            v = p.op("tensor", lambda e, h=h, bank=bank: e.matmul(
                bank[:, (h % 4) * 128:(h % 4 + 1) * 128],
                gtj_bf[:, h * 128:(h + 1) * 128],
                t_bf[:, h * 128:(h + 1) * 128],
                start=True, stop=True), sem=(ts if h % 4 == 3 else None)) or v
        p.wait("vector", ts, v)
        p.wait("vector", cld2s[jb], g2)
        mz = None
        for b4 in range(4):
            sl_ = slice(b4 * 512, (b4 + 1) * 512)
            p.op("vector", lambda e, b4=b4, sl_=sl_, jb=jb: e.tensor_add(
                out=tstate[:, sl_], in0=pb[b4][:, :], in1=hj[jb][:, sl_]), sem=vs)
            p.chain("vector")
            mz = p.op("vector", lambda e, b4=b4, sl_=sl_, j=j: e.scalar_tensor_tensor(
                out=S_in[:, sl_], in0=tstate[:, sl_], scalar=sel[:, j + 1:j + 2],
                in1=S_in[:, sl_], op0=ALU.mult, op1=ALU.add), sem=vs)
            hj_prev[j] = mz
        p.wait("scalar", vs, mz)
        ctv = p.op("scalar", lambda e: e.copy(out=t_bf[:, :], in_=tstate[:, :]), sem=ss)
        p.wait("tensor", ss, ctv)
    p.wait("scalar", vs, mz)
    sbv = p.op("scalar", lambda e: e.copy(out=S_in_bf[:, :], in_=S_in[:, :]), sem=ss)
    return sbv



def attnpost_phase(p, ctx, aps, cfg, sin_v):
    nc = p.nc
    NTOK = cfg["ntok"]
    NCH, KD, NH = NTOK // 128, 16, 16
    ts, ss, vs, gs = p.sem("ts"), p.sem("ss"), p.sem("vs"), p.sem("gs")
    ost = p.sem("ost")

    pb, pbbf, ident = aps["pb"], aps["pbbf"], aps["ident_sb"]
    phisc, olocsc, xnsc, gsc = aps["phisc"], aps["olocsc"], aps["xnsc"], aps["gsc"]
    x_ap, out_ap = aps["x"], aps["out"]
    S_in_bf = aps["S_in_bf"]

    woT_sb = ctx.enter_context(nc.sbuf_tensor("woT_sb", [128, KD * 2048], BF16))
    wo_f = [ctx.enter_context(nc.sbuf_tensor(f"wo_f{i}", [128, 2048], F32)) for i in range(2)]
    wg1_sb = ctx.enter_context(nc.sbuf_tensor("wg1_sb", [128, 2048], BF16))
    wg2_sb = ctx.enter_context(nc.sbuf_tensor("wg2_sb", [128, 2048], BF16))
    wgf = ctx.enter_context(nc.sbuf_tensor("wgf", [128, 2048], F32))
    gn_sb = ctx.enter_context(nc.sbuf_tensor("gn_sb", [128, 16], F32))
    B2 = []
    for i in range(2):
        d2 = {}
        d2["phi"] = ctx.enter_context(nc.sbuf_tensor(f"phi_b{i}", [128, 2048], BF16))
        d2["oloc"] = ctx.enter_context(nc.sbuf_tensor(f"oloc_b{i}", [128, 2048], BF16))
        d2["xn"] = ctx.enter_context(nc.sbuf_tensor(f"xn_c{i}", [128, 2048], BF16))
        d2["z"] = ctx.enter_context(nc.sbuf_tensor(f"z_c{i}", [128, 128], BF16))
        d2["gate"] = ctx.enter_context(nc.sbuf_tensor(f"gate_c{i}", [128, 2048], BF16))
        d2["og"] = ctx.enter_context(nc.sbuf_tensor(f"og_c{i}", [128, 2048], F32))
        d2["ogn"] = ctx.enter_context(nc.sbuf_tensor(f"ogn_c{i}", [128, 2048], BF16))
        d2["ogf"] = ctx.enter_context(nc.sbuf_tensor(f"ogf_c{i}", [128, 2048], BF16))
        d2["ssq"] = ctx.enter_context(nc.sbuf_tensor(f"ssq2_{i}", [128, 16], F32))
        d2["junk"] = d2["ogn"]
        d2["s"] = ctx.enter_context(nc.sbuf_tensor(f"s_c{i}", [128, 2048], BF16))
        d2["g"] = ctx.enter_context(nc.sbuf_tensor(f"g_c{i}", [128, 2048], BF16))
        d2["x"] = ctx.enter_context(nc.sbuf_tensor(f"x_c{i}", [128, 2048], F32))
        d2["sg"] = d2["og"]
        B2.append(d2)
    wld2 = [p.sem(f"wld2_{i}") for i in range(2)]
    xcp = [p.sem(f"xcp{i}") for i in range(2)]
    xco = [p.sem(f"xco{i}") for i in range(2)]
    xcx = [p.sem(f"xcx{i}") for i in range(2)]
    ostp = [p.sem(f"ost{i}") for i in range(2)]
    xgl = [p.sem(f"xgl{i}") for i in range(2)]
    if "gst_final" in cfg:
        p.wait("scalar", "gst", cfg["gst_final"])
        p.wait("sync", "spl", cfg["spl_final"])
        for nm in ("sphi0", "sphi1", "solo0", "solo1"):
            p.wait("sync", nm, cfg[nm])

    pox = {}
    gx = {}

    def issue_pox(c):
        b = c % 2
        B = B2[b]
        p.dma("sync", B["phi"][:, :1024].rearrange("p (a n) -> p a n", n=512),
              phisc[c, 0].rearrange("p (a n) -> p a n", n=512), xcp[b])
        pv_p = p.dma("sync", B["phi"][:, 1024:].rearrange("p (a n) -> p a n", n=512),
                     phisc[c, 1].rearrange("p (a n) -> p a n", n=512), xcp[b])
        p.dma("sync", B["oloc"][:, :1024].rearrange("p (a n) -> p a n", n=512),
              olocsc[c, 0].rearrange("p (a n) -> p a n", n=512), xco[b])
        pv_o = p.dma("sync", B["oloc"][:, 1024:].rearrange("p (a n) -> p a n", n=512),
                     olocsc[c, 1].rearrange("p (a n) -> p a n", n=512), xco[b])
        pv_t = p.dma("sync", B["xn"][:, :].rearrange("p (k t) -> p k t", k=KD),
                     xnsc[:, :, c * 128:(c + 1) * 128].rearrange("k p t -> p k t"),
                     xcx[b])
        pox[c] = (pv_p, pv_o, pv_t)

    def issue_gx(c):
        b = c % 2
        B = B2[b]
        p.dma("scalar", B["g"][:, :].rearrange("p (a n) -> p a n", n=1024),
              gsc[c * 128:(c + 1) * 128, :].rearrange("p (a n) -> p a n", n=1024),
              xgl[b])
        gx[c] = p.dma("scalar", B["x"][:, :].rearrange("p (a n) -> p a n", n=512),
                      x_ap[c * 128:(c + 1) * 128, :].rearrange("p (a n) -> p a n", n=512),
                      xgl[b])

    issue_pox(0)
    issue_pox(1)
    issue_gx(0)
    issue_gx(1)

    gl = p.dma("sync", gn_sb[:, :], aps["gn16"], p.sem("gnl"))
    wo_c = {}
    for dd in range(16):
        b = dd % 2
        if dd >= 2:
            p.wait("sync", vs, wo_c[dd - 2])
        lv = p.dma("sync", wo_f[b][:, :].rearrange("p (a n) -> p a n", n=512),
                   aps["woT"][dd * 128:(dd + 1) * 128, :]
                   .rearrange("p (a n) -> p a n", n=512), wld2[b])
        p.wait("vector", wld2[b], lv)
        p.wait("vector", "gnl", gl)
        wo_c[dd] = p.op("vector", lambda e, dd=dd, b=b: e.tensor_scalar(
            out=woT_sb[:, dd * 2048:(dd + 1) * 2048], in0=wo_f[b][:, :],
            scalar1=gn_sb[:, dd:dd + 1], scalar2=None, op0=ALU.mult), sem=vs)
    l1 = p.dma("sync", wgf[:, :].rearrange("p (a n) -> p a n", n=512),
               aps["wg1T4"].rearrange("p (a n) -> p a n", n=512), p.sem("wgl1"))
    p.wait("scalar", "wgl1", l1)
    c1 = p.op("scalar", lambda e: e.copy(out=wg1_sb[:, :], in_=wgf[:, :]), sem=ss)
    p.wait("sync", ss, c1)
    l2 = p.dma("sync", wgf[:, :].rearrange("p (a n) -> p a n", n=512),
               aps["wg2T"].rearrange("p (a n) -> p a n", n=512), p.sem("wgl2"))
    p.wait("scalar", "wgl2", l2)
    c2 = p.op("scalar", lambda e: e.copy(out=wg2_sb[:, :], in_=wgf[:, :]), sem=ss)
    p.wait("tensor", ss, c2)
    p.wait("tensor", ss, sin_v)

    ev_og = {}

    def chunk_gen(c):
        b = c % 2
        B = B2[b]
        pairA = (pb[2 * b][:, :], pb[2 * b + 1][:, :])
        zb = pb[4 + b][:, :]
        if c >= 2:
            p.wait("vector", ostp[b], ev_og.get(("ost", c - 2), 0))
        pv_p, pv_o, pv_t = pox[c]
        yield
        p.wait("tensor", xcp[b], pv_p)
        for w in range(2):
            H0 = 8 * w
            if w == 1:
                p.wait("tensor", vs, ev_og[("oadd", c, 0)])
            v = None
            for h in range(8):
                bank = pairA[h // 4]
                v = p.op("tensor", lambda e, h=h, H0=H0, bank=bank: e.matmul(
                    bank[:, (h % 4) * 128:(h % 4 + 1) * 128],
                    B["phi"][:, (H0 + h) * 128:(H0 + h + 1) * 128],
                    S_in_bf[:, (H0 + h) * 128:(H0 + h + 1) * 128],
                    start=True, stop=True), sem=(ts if h == 7 else None)) or v
            p.wait("vector", ts, v)
            p.wait("vector", xco[b], pv_o)
            oaddv = None
            for b2 in range(2):
                sl_ = slice((H0 + b2 * 4) * 128, (H0 + b2 * 4 + 4) * 128)
                oaddv = p.op("vector", lambda e, b2=b2, sl_=sl_: e.tensor_add(
                    out=B["og"][:, sl_], in0=pairA[b2][:, :],
                    in1=B["oloc"][:, sl_]), sem=vs)
            ev_og[("oadd", c, w)] = oaddv
            yield
        p.wait("tensor", xcx[b], pv_t)
        zv = None
        for kk in range(KD):
            zv = p.op("tensor", lambda e, kk=kk: e.matmul(
                zb[:, :128], wg1_sb[:, kk * 128:(kk + 1) * 128],
                B["xn"][:, kk * 128:(kk + 1) * 128],
                start=(kk == 0), stop=(kk == KD - 1)),
                sem=(ts if kk == KD - 1 else None)) or zv
        ev_og[("zdone", c)] = zv
        p.wait("scalar", ts, zv)
        zc = p.op("scalar", lambda e: e.copy(out=B["z"][:, :], in_=zb[:, :128]), sem=ss)
        p.wait("tensor", ss, zc)
        if c + 2 < NCH:
            p.wait("sync", ts, zv)
            p.wait("sync", vs, ev_og[("oadd", c, 1)])
            issue_pox(c + 2)
        yield
        gt_v = None
        for dt_ in range(4):
            mv = p.op("tensor", lambda e, dt_=dt_: e.matmul(
                zb[:, :], B["z"][:, :], wg2_sb[:, dt_ * 512:(dt_ + 1) * 512],
                start=True, stop=True), sem=ts)
            p.wait("scalar", ts, mv)
            gt_v = p.op("scalar", lambda e, dt_=dt_: e.activation(
                out=B["gate"][:, dt_ * 512:(dt_ + 1) * 512], in_=zb[:, :],
                func=AF.Sigmoid), sem=ss)
            if dt_ < 3:
                p.wait("tensor", ss, gt_v)
            yield
        p.wait("vector", ss, gt_v)
        p.wait("vector", vs, ev_og[("oadd", c, 1)])
        p.chain("vector")
        for b4 in range(4):
            sl_ = slice(b4 * 512, (b4 + 1) * 512)
            p.op("vector", lambda e, sl_=sl_: e.tensor_mul(
                out=B["og"][:, sl_], in0=B["og"][:, sl_],
                in1=B["gate"][:, sl_]), sem=vs)
            p.chain("vector")
        sqv = None
        for hh in range(16):
            sqv = p.op("vector", lambda e, hh=hh: e.scalar_tensor_tensor(
                out=B["junk"][:, hh * 128:(hh + 1) * 128],
                in0=B["og"][:, hh * 128:(hh + 1) * 128],
                scalar=1.0, in1=B["og"][:, hh * 128:(hh + 1) * 128],
                op0=ALU.mult, op1=ALU.mult,
                accum_out=B["ssq"][:, hh:hh + 1]), sem=vs)
        yield
        p.wait("scalar", vs, sqv)
        rs = p.op("scalar", lambda e: e.activation(
            out=B["ssq"][:, :], in_=B["ssq"][:, :], func=AF.Sqrt, bias=1e-6,
            scale=1.0 / 128.0), sem=ss)
        p.wait("vector", ss, rs)
        p.op("vector", lambda e: e.reciprocal(out=B["ssq"][:, :], in_=B["ssq"][:, :]),
             sem=vs)
        p.chain("vector")
        nv = None
        for h in range(16):
            nv = p.op("vector", lambda e, h=h: e.tensor_scalar(
                out=B["ogn"][:, h * 128:(h + 1) * 128],
                in0=B["og"][:, h * 128:(h + 1) * 128],
                scalar1=B["ssq"][:, h:h + 1], scalar2=None, op0=ALU.mult), sem=vs)
        p.wait("tensor", vs, nv)
        yield
        tse = {}
        for kb in range(4):
            out_tp = (B["ogf"][:, :].rearrange("p (a n) -> p a n", a=16)
                      [:, kb * 4:(kb + 1) * 4, :])
            tse[kb] = transpose_like(p, aps,
                                     [B["ogn"][:, (kb * 4 + j) * 128:(kb * 4 + j + 1) * 128]
                                      for j in range(4)],
                                     out_tp, "scalar" if kb % 2 else "vector")
            yield
        for kb in range(4):
            p.wait("tensor", tse[kb][0], tse[kb][1])
        p.wait("tensor", vs, ev_og[("oadd", c, 1)])
        p.wait("vector", xgl[b], gx[c])
        sv = None
        cb = None
        for dt_ in range(4):
            bank = pairA[dt_ % 2]
            if dt_ >= 2:
                p.wait("tensor", vs, ev_og[("sev", c, dt_ - 2)])
            for dd in range(KD):
                sv = p.op("tensor", lambda e, dt_=dt_, dd=dd, bank=bank: e.matmul(
                    bank[:, :], B["ogf"][:, dd * 128:(dd + 1) * 128],
                    woT_sb[:, dd * 2048 + dt_ * 512:dd * 2048 + (dt_ + 1) * 512],
                    start=(dd == 0), stop=(dd == KD - 1)),
                    sem=(ts if dd == KD - 1 else None)) or sv
            p.wait("vector", ts, sv)
            sl4 = slice(dt_ * 512, (dt_ + 1) * 512)
            ev_og[("sev", c, dt_)] = p.op("vector", lambda e, sl4=sl4, bank=bank:
                                          e.scalar_tensor_tensor(
                out=B["sg"][:, sl4], in0=bank[:, :], scalar=1.0,
                in1=B["g"][:, sl4], op0=ALU.mult, op1=ALU.mult), sem=vs)
            p.chain("vector")
            cb = p.op("vector", lambda e, sl4=sl4: e.tensor_add(
                out=B["sg"][:, sl4], in0=B["sg"][:, sl4], in1=B["x"][:, sl4]),
                sem=vs)
            p.wait("gpsimd", vs, cb)
            ev_og[("ost", c)] = p.dma(
                "gpsimd", out_ap[c * 128:(c + 1) * 128, dt_ * 512:(dt_ + 1) * 512],
                B["sg"][:, sl4], ostp[b])
            yield
        ev_og[("mmdone", c)] = sv
        ev_og[("sgdone", c)] = cb
        ev_og[("comb", c)] = cb
        if c + 2 < NCH:
            p.wait("scalar", vs, cb)
            issue_gx(c + 2)

    gens = []
    next_c = 0
    primed = False
    while next_c < NCH or gens:
        while len(gens) < 2 and next_c < NCH:
            g_ = chunk_gen(next_c)
            next_c += 1
            gens.append(g_)
            if not primed:
                primed = True
                for _ in range(9):
                    try:
                        next(g_)
                    except StopIteration:
                        break
        for g_ in list(gens):
            try:
                next(g_)
            except StopIteration:
                gens.remove(g_)
    return {"ost0": p.cnt["ost0"], "ost1": p.cnt["ost1"]}


def transpose_like(p, aps, srcs, out_ap, eng):
    st = p.__dict__.setdefault("_tpstate", {"i": 0, "ev": {}})
    ts, ss, vs = p.sem("ts"), p.sem("ss"), p.sem("vs")
    i = st["i"] % 2
    st["i"] += 1
    bank = aps["pbbf"][i]
    pe = st["ev"].get(i)
    if pe is not None:
        p.wait("tensor", pe[0], pe[1])
    v = None
    for t, s in enumerate(srcs):
        v = p.op("tensor", lambda e, s=s, t=t, bank=bank: e.transpose(
            bank[:, t * 128:(t + 1) * 128], s, aps["ident_sb"][:, :]),
            sem=(ts if t == len(srcs) - 1 else None)) or v
    p.wait(eng, ts, v)
    sem = ss if eng == "scalar" else vs
    if eng == "scalar":
        ev = p.op(eng, lambda e, o=out_ap, bank=bank: e.copy(
            out=o, in_=bank[:, :512]), sem=sem)
    else:
        ev = p.op(eng, lambda e, o=out_ap, bank=bank: e.tensor_copy(
            out=o, in_=bank[:, :512]), sem=sem)
    st["ev"][i] = (sem, ev)
    return sem, ev



def build_kernel(ntok, grp=512):
    from contextlib import ExitStack
    nc = bass.Bass("TRN2", target_bir_lowering=False, debug=False)
    NCH = ntok // 128
    d = {}
    def din(name, shape, dt=F32):
        d[name] = nc.dram_tensor(name, list(shape), dt, kind="ExternalInput").ap()
        return d[name]
    x = din("x", (ntok, 2048))
    out = nc.dram_tensor("out", [ntok, 2048], F32, kind="ExternalOutput").ap()
    din("wqT", (2048, 2048)); din("wkT", (2048, 2048)); din("wvT", (2048, 2048))
    din("wbeta16", (128, 256)); din("wg1T4", (128, 2048)); din("wg2T", (128, 2048))
    din("woT", (2048, 2048)); din("gn16", (128, 16))
    din("wgu4", (128, 128, 16, 128)); din("wdn4", (16, 128, 64, 128))
    din("eye16", (128, 4, 512))
    din("ident", (128, 128), BF16); din("ident4", (128, 512), BF16)
    din("maskD", (128, 128)); din("maskOff", (128, 128))
    din("triu4", (128, 512)); din("sel", (128, 8))
    din("cpool", (128, 3))

    xnsc = nc.dram_tensor("xnsc", [16, 128, ntok], BF16).ap()
    phisc = nc.dram_tensor("phisc", [NCH, 2, 128, 1024], BF16).ap()
    olocsc = nc.dram_tensor("olocsc", [NCH, 2, 128, 1024], BF16).ap()
    gsc = nc.dram_tensor("gsc", [ntok, 2048], BF16).ap()
    wqkvbf = nc.dram_tensor("wqkvbf", [3, 2048, 2048], BF16).ap()
    ccin = nc.dram_tensor("ccin", [2, 128, 2048], F32).ap()
    ccout = nc.dram_tensor("ccout", [16, 128, 2048], F32, addr_space="Shared").ap()

    p = Prog(nc)
    import os
    cfg = {"ntok": ntok, "grp": grp, "row0": 0}
    dbg_on = os.environ.get("KDBG") == "1"
    dbgsc = nc.dram_tensor("dbgsc", [40, 128, 1024], F32).ap() if dbg_on else None
    with ExitStack() as ctx0:
        pb = [ctx0.enter_context(nc.psum_tensor(f"pb{i}", [128, 512], F32))
              for i in range(6)]
        pbbf = [ctx0.enter_context(nc.psum_tensor(f"pbbf{i}", [128, 1024], BF16))
                for i in range(2)]
        ident_sb = ctx0.enter_context(nc.sbuf_tensor("ident_sb", [128, 128], BF16))
        ident4_sb = ctx0.enter_context(nc.sbuf_tensor("ident4_sb", [128, 512], BF16))
        maskD_sb = ctx0.enter_context(nc.sbuf_tensor("maskD_sb", [128, 128], F32))
        maskOff_sb = ctx0.enter_context(nc.sbuf_tensor("maskOff_sb", [128, 128], F32))
        triu4_sb = ctx0.enter_context(nc.sbuf_tensor("triu4_sb", [128, 512], F32))
        sel_sb = ctx0.enter_context(nc.sbuf_tensor("sel_sb", [128, 8], F32))
        aps = {"x": x, "out": out, "xnsc": xnsc, "phisc": phisc, "olocsc": olocsc,
               "gsc": gsc, "ccin": ccin, "ccout": ccout,
               "wqT": d["wqT"], "wkT": d["wkT"], "wvT": d["wvT"],
               "wbeta16_dram": d["wbeta16"], "wg1T4": d["wg1T4"],
               "wg2T": d["wg2T"], "woT": d["woT"], "gn16": d["gn16"],
               "wgu4": d["wgu4"], "wdn4": d["wdn4"], "eye16": d["eye16"],
               "wqkvbf": wqkvbf,
               **({"dbgsc": dbgsc} if dbg_on else {}),
               "pb": pb, "pbbf": pbbf, "ident_sb": ident_sb,
               "ident4_sb": ident4_sb, "maskD_sb": maskD_sb,
               "maskOff_sb": maskOff_sb,
               "triu4_sb": triu4_sb, "sel_sb": sel_sb}
        cpool_sb = ctx0.enter_context(nc.sbuf_tensor("cpool_sb", [128, 3], F32))
        nc.const_aps.register(F32, 1e-6, cpool_sb[:, 0:1])
        nc.const_aps.register(F32, 1.0 / 128.0, cpool_sb[:, 1:2])
        nc.const_aps.register(F32, 1e-5, cpool_sb[:, 2:3])
        cl = p.sem("cl")
        v = p.dma("sync", cpool_sb[:, :], d["cpool"], cl)
        v = p.dma("sync", ident_sb[:, :], d["ident"], cl)
        v = p.dma("sync", ident4_sb[:, :], d["ident4"], cl)
        v = p.dma("sync", maskD_sb[:, :], d["maskD"], cl)
        v = p.dma("sync", maskOff_sb[:, :], d["maskOff"], cl)
        v = p.dma("sync", triu4_sb[:, :], d["triu4"], cl)
        v = p.dma("sync", sel_sb[:, :], d["sel"], cl)
        for e in ENGINES:
            p.wait(e, cl, v)

        with ExitStack() as c1:
            sv = sweep_phase(p, c1, aps, cfg)
            cc_v = relay_send(p, aps)
            for i in range(6):
                p.wait("sync", f"wsl_ld{i}", p.cnt[f"wsl_ld{i}"])
            p.wait("gpsimd", "spl", sv["spl"])
            for nm in ("sphi0", "sphi1", "solo0", "solo1"):
                p.wait("gpsimd", nm, sv[nm])
            for i in range(2):
                p.wait("sync", f"xch{i}", p.cnt[f"xch{i}"])
            p.barrier()
        with ExitStack() as c2:
            aps["xn_ready"] = ("spl", sv["spl"])
            fv = ffn_phase(p, c2, aps, cfg)
            for i in range(4):
                p.wait("sync", f"wgu_ld{i}", p.cnt[f"wgu_ld{i}"])
            for i in range(3):
                p.wait("sync", f"wdn_ld{i}", p.cnt[f"wdn_ld{i}"])
            for i in range(2):
                p.wait("scalar", f"fxld{i}", p.cnt[f"fxld{i}"])
            p.wait("gpsimd", "gst", fv["gst"])
            p.barrier()
        aps["S_in_bf_outer"] = ctx0.enter_context(
            nc.sbuf_tensor("S_in_bf_o", [128, 16 * 128], BF16))
        with ExitStack() as c3a:
            sin_v = compose_phase(p, c3a, aps, cc_v)
            for nm in ("cldr0", "cldr1", "cld2r0", "cld2r1"):
                p.wait("sync", nm, p.cnt[nm])
            p.barrier()
        with ExitStack() as c3:
            cfg2 = dict(cfg)
            cfg2["gst_final"] = fv["gst"]
            cfg2["spl_final"] = sv["spl"]
            for nm in ("sphi0", "sphi1", "solo0", "solo1"):
                cfg2[nm] = sv[nm]
            av = attnpost_phase(p, c3, aps, cfg2, sin_v)
            p.wait("gpsimd", "ost0", av["ost0"])
            p.wait("gpsimd", "ost1", av["ost1"])
        p.build_sems(ctx0)
        with nc.Block() as block:
            p.replay(block)
    return nc


def host_prep(inputs, ntok_core=2048):
    import ml_dtypes
    f32 = np.float32
    asc = np.ascontiguousarray
    x = np.asarray(inputs["L0_attn_x"], f32).reshape(-1, 2048)
    wq = np.asarray(inputs["L0_attn_Wq"], f32)
    wk = np.asarray(inputs["L0_attn_Wk"], f32)
    wv = np.asarray(inputs["L0_attn_Wv"], f32)
    wbeta = np.asarray(inputs["L0_attn_Wbeta"], f32)
    wg1 = np.asarray(inputs["L0_attn_Wg1"], f32)
    wg2 = np.asarray(inputs["L0_attn_Wg2"], f32)
    gn = np.asarray(inputs["L0_attn_g_norm"], f32)
    wo = np.asarray(inputs["L0_attn_Wo"], f32)
    wgu = np.asarray(inputs["L0_ffn_w_gate_up"], f32)
    wdn = np.asarray(inputs["L0_ffn_w_down"], f32)

    shared = {
        "wqT": asc(wq.T), "wkT": asc(wk.T), "wvT": asc(wv.T),
        "wbeta16": asc(wbeta.T.reshape(16, 128, 16).transpose(1, 0, 2)
                       .reshape(128, 256)),
        "wg1T4": asc(wg1.T.reshape(16, 128, 128).transpose(1, 0, 2)
                     .reshape(128, 2048)),
        "wg2T": asc(wg2.T),
        "woT": asc(wo.T),
        "gn16": asc(gn.reshape(16, 128).T),
        "wgu4": asc(wgu.T.reshape(16, 128, 128, 128).transpose(2, 1, 0, 3)),
        "wdn4": asc(wdn.T.reshape(64, 128, 16, 128).transpose(2, 1, 0, 3)),
        "eye16": asc(np.tile(np.eye(128, dtype=f32), 16).reshape(128, 4, 512)),
        "ident": np.eye(128, dtype=ml_dtypes.bfloat16),
        "ident4": asc(np.tile(np.eye(128, dtype=np.float32), 4)
                      .astype(ml_dtypes.bfloat16)),
        "maskD": asc(-np.tril(np.ones((128, 128), f32), -1)
                     * np.kron(np.eye(4, dtype=f32), np.ones((32, 32), f32))),
        "maskOff": asc(-np.tril(np.ones((128, 128), f32), -1)
                       * (1.0 - np.kron(np.eye(4, dtype=f32), np.ones((32, 32), f32)))),
        "triu4": asc(np.tile(np.triu(np.ones((128, 128), f32), 0), 4)),
        "cpool": asc(np.tile(np.array([[1e-6, 1.0 / 128.0, 1e-5]], f32), (128, 1))),
    }
    per_core = []
    ncores = x.shape[0] // ntok_core
    for c in range(ncores):
        sel = np.zeros((128, 8), f32)
        sel[:, c] = 1.0
        m = {"x": asc(x[c * ntok_core:(c + 1) * ntok_core]), "sel": sel}
        m.update(shared)
        per_core.append(m)
    return per_core



NTOK_CORE = 2048

def kernel(**inputs):
    from concourse.bass_utils import run_bass_kernel_spmd
    nc = build_kernel(NTOK_CORE, grp=512)
    per_core = host_prep(inputs, NTOK_CORE)
    res = run_bass_kernel_spmd(nc, per_core, core_ids=list(range(8)))
    out = np.concatenate([np.asarray(res.results[c]["out"]) for c in range(8)], axis=0)
    return out.reshape(1, 16384, 2048).astype(np.float32)


# version 21
# speedup vs baseline: 1.0026x; 1.0026x over previous
import numpy as np
import concourse.bass as bass
import concourse.mybir as mybir

F32 = mybir.dt.float32
BF16 = mybir.dt.bfloat16
AF = mybir.ActivationFunctionType
ALU = mybir.AluOpType
AX = mybir.AxisListType

ENGINES = ("sync", "scalar", "vector", "tensor", "gpsimd")


class Prog:
    def __init__(self, nc):
        self.nc = nc
        self.q = {e: [] for e in ENGINES}
        self.cnt = {}
        self.waited = {e: {} for e in ENGINES}
        self.handles = {}

    def sem(self, name):
        if name not in self.cnt:
            self.cnt[name] = 0
        return name

    def emit(self, eng, fn):
        self.q[eng].append(fn)

    def inc(self, name, by=1):
        self.sem(name)
        self.cnt[name] += by
        return self.cnt[name]

    def wait(self, eng, sem, val):
        if val is None or val <= 0:
            return
        w = self.waited[eng].get(sem, 0)
        if val <= w:
            return
        self.waited[eng][sem] = val
        self.emit(eng, lambda e, s=sem, v=val: e.wait_ge(self.handles[s], v))

    def dma(self, eng, out, in_, sem):
        assert tuple(out.shape) == tuple(in_.shape), (
            f"dma shape mismatch {tuple(out.shape)} vs {tuple(in_.shape)}")
        c = self.inc(sem, 16)
        self.emit(eng, lambda e, o=out, i=in_, s=sem: e.dma_start(out=o, in_=i)
                  .then_inc(self.handles[s], 16))
        return c

    def op(self, eng, fn, sem=None):
        if sem is None:
            self.emit(eng, fn)
            return None
        c = self.inc(sem, 1)
        self.emit(eng, lambda e, f=fn, s=sem: f(e).then_inc(self.handles[s], 1))
        return c

    DEFSEM = {"vector": "vs", "scalar": "ss", "gpsimd": "gs", "tensor": "ts"}

    def chain(self, eng):
        self.wait(eng, self.DEFSEM[eng], self.cnt.get(self.DEFSEM[eng], 0))

    def barrier(self):
        b = self.sem("bar")
        target = self.cnt["bar"] + len(ENGINES)
        self.cnt["bar"] = target
        for e in ENGINES:
            if e == "sync":
                self.emit(e, lambda en: en.nop().then_inc(self.handles["bar"], 1))
            else:
                self.emit(e, lambda en: en.drain().then_inc(self.handles["bar"], 1))
        for e in ENGINES:
            self.waited[e]["bar"] = target
            self.emit(e, lambda en, v=target: en.wait_ge(self.handles["bar"], v))

    def replay(self, block):
        for e in ENGINES:
            fns = self.q[e]
            getattr(block, e)(lambda eng, fns=fns: [f(eng) for f in fns])

    def build_sems(self, ctx):
        for name in self.cnt:
            self.handles[name] = ctx.enter_context(self.nc.semaphore(name))



class WStream:
    def __init__(self, p, ctx, name, shape, nbuf, dma_eng="sync",
                 cast_engines=("scalar", "vector"), rel_sem="ts"):
        self.p, self.name, self.nbuf = p, name, nbuf
        nc = p.nc
        self.f32 = [ctx.enter_context(nc.sbuf_tensor(f"{name}_f{i}", list(shape), F32))
                    for i in range(nbuf)]
        self.bf = [ctx.enter_context(nc.sbuf_tensor(f"{name}_b{i}", list(shape), BF16))
                   for i in range(nbuf)]
        self.ld = [p.sem(f"{name}_ld{i}") for i in range(nbuf)]
        self.cast_engines = cast_engines
        self.cast_sems = {e: p.sem(f"{name}_c_{e}") for e in cast_engines}
        self.dma_eng = dma_eng
        self.rel_sem = rel_sem
        self.i = 0
        self.tok = {}
        self.rel = {}

    def load(self, in_ap):
        p, i = self.p, self.i
        b = i % self.nbuf
        old = i - self.nbuf
        if old >= 0:
            assert old in self.rel, f"{self.name}: rel({old}) not set before load({i})"
            p.wait(self.dma_eng, self.rel_sem, self.rel[old])
        out_ap = self.f32[b][:, :]
        if len(in_ap.shape) == 3:
            out_ap = out_ap.rearrange("p (k c) -> p k c", k=in_ap.shape[1])
        ldv = p.dma(self.dma_eng, out_ap, in_ap, self.ld[b])
        ce = self.cast_engines[i % len(self.cast_engines)]
        p.wait(ce, self.ld[b], ldv)
        cs = self.cast_sems[ce]
        cv = p.inc(cs, 1)
        if ce == "scalar":
            p.emit(ce, lambda e, b=b, s=cs: e.copy(out=self.bf[b][:, :],
                                                   in_=self.f32[b][:, :])
                   .then_inc(p.handles[s], 1))
        else:
            p.emit(ce, lambda e, b=b, s=cs: e.tensor_copy(out=self.bf[b][:, :],
                                                          in_=self.f32[b][:, :])
                   .then_inc(p.handles[s], 1))
        self.tok[i] = (b, ce, cv)
        self.i += 1
        return i

    def wait_bf(self, eng, tok):
        b, ce, cv = self.tok[tok]
        self.p.wait(eng, self.cast_sems[ce], cv)
        return self.bf[b]

    def set_rel(self, tok, val):
        assert val is not None
        self.rel[tok] = val


class BfRing:
    def __init__(self, p, ctx, name, shape, nbuf, dma_eng="sync"):
        self.p, self.name, self.nbuf = p, name, nbuf
        nc = p.nc
        self.bf = [ctx.enter_context(nc.sbuf_tensor(f"{name}_b{i}", list(shape), BF16))
                   for i in range(nbuf)]
        self.ld = [p.sem(f"{name}_ld{i}") for i in range(nbuf)]
        self.dma_eng = dma_eng
        self.i = 0
        self.tok = {}
        self.rel = {}
        self.rel_sem = "ts"

    def load(self, in_ap):
        p, i = self.p, self.i
        b = i % self.nbuf
        old = i - self.nbuf
        if old >= 0:
            p.wait(self.dma_eng, self.rel_sem, self.rel[old])
        out_ap = self.bf[b][:, :]
        if len(in_ap.shape) == 3:
            out_ap = out_ap.rearrange("p (k c) -> p k c", k=in_ap.shape[1])
        ldv = p.dma(self.dma_eng, out_ap, in_ap, self.ld[b])
        self.tok[i] = (b, self.ld[b], ldv)
        self.i += 1
        return i

    def wait_bf(self, eng, tok):
        b, sem, v = self.tok[tok]
        self.p.wait(eng, sem, v)
        return self.bf[b]

    def set_rel(self, tok, val):
        self.rel[tok] = val



def ffn_phase(p, ctx, aps, cfg):
    nc = p.nc
    NTOK, GRP = cfg["ntok"], cfg["grp"]
    ngrp = NTOK // GRP
    TCH = GRP // 128
    NI, KD = 64, 16
    ts, ss, vs = p.sem("ts"), p.sem("ss"), p.sem("vs")
    gst = p.sem("gst")
    xld2 = [p.sem(f"fxld{i}") for i in range(2)]

    xn_g = ctx.enter_context(nc.sbuf_tensor("xn_g", [128, KD * GRP], BF16))
    h = ctx.enter_context(nc.sbuf_tensor("h", [128, NI * GRP], BF16))
    sg = [ctx.enter_context(nc.sbuf_tensor(f"sg{j}", [128, GRP], BF16)) for j in range(2)]
    gfm = [ctx.enter_context(nc.sbuf_tensor(f"gfm{j}", [128, GRP], BF16)) for j in range(2)]
    g_tm = ctx.enter_context(nc.sbuf_tensor("g_tm", [128, TCH * 2048], BF16))

    st_gu = WStream(p, ctx, "wgu", (128, KD * 128), nbuf=4)
    st_dn = WStream(p, ctx, "wdn", (128, 16 * 128), nbuf=4)

    pb, pbbf, ident = aps["pb"], aps["pbbf"], aps["ident_sb"]
    wgu4, wdn4, xnsc, gsc = aps["wgu4"], aps["wdn4"], aps["xnsc"], aps["gsc"]

    LA = 3
    LA_DN = LA
    assert LA < st_gu.nbuf and LA_DN < st_dn.nbuf

    silu_v = {}
    hmul_v = {}
    dsilu_v = {}
    dlast_v = {}
    tp_v = {}
    ev_v = {}
    gu_last = 0
    gu_last2 = {}
    dn_last = 0
    gst_prev = 0

    for g in range(ngrp):
        if "xn_ready" in aps:
            p.wait("scalar", aps["xn_ready"][0], aps["xn_ready"][1])
        p.wait("scalar", ts, gu_last)
        xv = p.dma("scalar", xn_g[:, :].rearrange("p (k t) -> p k t", k=KD),
                   xnsc[:, :, g * GRP:(g + 1) * GRP].rearrange("k p t -> p k t"),
                   xld2[g % 2])
        p.wait("tensor", xld2[g % 2], xv)

        toks = {}
        def issue_gu(j):
            if 0 <= j < 2 * NI:
                i, is_up = j // 2, j % 2
                toks[j] = st_gu.load(wgu4[64 + i] if is_up else wgu4[i])
        for j in range(LA):
            issue_gu(j)
        p.wait("vector", ts, dn_last)
        for i in range(NI):
            for half in (0, 1):
                j = 2 * i + half
                issue_gu(j + LA)
                bank = pb[2 * half + (i % 2)]
                buf = st_gu.wait_bf("tensor", toks[j])
                if i >= 2:
                    p.wait("tensor", ss if half == 0 else vs,
                           silu_v[g, i - 2] if half == 0 else hmul_v[g, i - 2])
                v = None
                for k in range(KD):
                    last = (k == KD - 1)
                    v = p.op("tensor",
                             lambda e, b=buf, k=k, bank=bank, last=last:
                             e.matmul(bank[:, :GRP], b[:, k * 128:(k + 1) * 128],
                                      xn_g[:, k * GRP:(k + 1) * GRP],
                                      start=(k == 0), stop=last),
                             sem=(ts if last else None)) or v
                st_gu.set_rel(toks[j], v)
            gu_last = v
            gu_last2[g] = v
            p.wait("scalar", ts, st_gu.rel[toks[2 * i]])
            if i >= 2:
                p.wait("scalar", vs, hmul_v[g, i - 2])
            silu_v[g, i] = p.op("scalar", lambda e, i=i: e.activation(
                out=sg[i % 2][:, :], in_=pb[0 + (i % 2)][:, :GRP], func=AF.Silu), sem=ss)
            p.wait("vector", ts, st_gu.rel[toks[2 * i + 1]])
            p.wait("vector", ss, silu_v[g, i])
            hmul_v[g, i] = p.op("vector", lambda e, i=i: e.tensor_mul(
                out=h[:, i * GRP:(i + 1) * GRP], in0=sg[i % 2][:, :],
                in1=pb[2 + (i % 2)][:, :GRP]), sem=vs)

        dtoks = {}
        def issue_dn(jj):
            if 0 <= jj < 16 * 4:
                d, qtr = jj // 4, jj % 4
                dtoks[jj] = st_dn.load(wdn4[d, :, qtr * 16:(qtr + 1) * 16, :])
        for jj in range(LA_DN):
            issue_dn(jj)

        def transpose_batch(d):
            p.wait("tensor", ss, dsilu_v[g, d])
            if (g, d - 2) in ev_v:
                p.wait("tensor", vs, ev_v[g, d - 2])
            v = None
            for t in range(TCH):
                v = p.op("tensor", lambda e, d=d, t=t: e.transpose(
                    pbbf[d % 2][:, t * 128:(t + 1) * 128],
                    gfm[d % 2][:, t * 128:(t + 1) * 128], ident[:, :]),
                    sem=(ts if t == TCH - 1 else None)) or v
            tp_v[g, d] = v
            p.wait("vector", ts, v)
            p.wait("vector", gst, gst_prev)
            out_ap = g_tm[:, :].rearrange("p (t x) -> p t x", t=TCH)[:, :, d * 128:(d + 1) * 128]
            in_ap = pbbf[d % 2][:, :TCH * 128].rearrange("p (t c) -> p t c", t=TCH)
            ev_v[g, d] = p.op("vector", lambda e, o=out_ap, i=in_ap:
                              e.tensor_copy(out=o, in_=i), sem=vs)

        p.wait("tensor", vs, hmul_v[g, NI - 1])
        for d in range(16):
            bank = pb[4 + (d % 2)]
            if d >= 2:
                p.wait("tensor", ss, dsilu_v[g, d - 2])
            v = None
            for qtr in range(4):
                jj = 4 * d + qtr
                issue_dn(jj + LA_DN)
                buf = st_dn.wait_bf("tensor", dtoks[jj])
                for kk in range(16):
                    i = qtr * 16 + kk
                    v = p.op("tensor",
                             lambda e, b=buf, kk=kk, i=i, bank=bank:
                             e.matmul(bank[:, :GRP], b[:, kk * 128:(kk + 1) * 128],
                                      h[:, i * GRP:(i + 1) * GRP],
                                      start=(i == 0), stop=(i == NI - 1)),
                             sem=(ts if kk == 15 else None)) or v
                st_dn.set_rel(dtoks[jj], v)
            dlast_v[g, d] = v
            p.wait("scalar", ts, v)
            if (g, d - 2) in tp_v:
                p.wait("scalar", ts, tp_v[g, d - 2])
            dsilu_v[g, d] = p.op("scalar", lambda e, d=d: e.activation(
                out=gfm[d % 2][:, :], in_=pb[4 + (d % 2)][:, :GRP], func=AF.Silu), sem=ss)
            if d >= 1:
                transpose_batch(d - 1)
        transpose_batch(15)
        dn_last = dlast_v[g, 15]

        for t in range(TCH):
            p.wait("gpsimd", vs, ev_v[g, 15])
            row0 = cfg["row0"] + g * GRP + t * 128
            gst_prev = p.dma("gpsimd", gsc[row0:row0 + 128, :],
                             g_tm[:, t * 2048:(t + 1) * 2048], gst)

    return {"gst": p.cnt["gst"], "ts": p.cnt["ts"]}



class PairPsum:
    def __init__(self, p, banks):
        self.p = p
        self.pairs = [(banks[0], banks[1]), (banks[2], banks[3])]
        self.j = 0
        self.ev = {}

    def next(self):
        i = self.j % 2
        self.j += 1
        for sem, val in self.ev.get(i, ()):
            self.p.wait("tensor", sem, val)
        return i

    def set_ev(self, i, evs):
        self.ev[i] = evs


def sweep_phase(p, ctx, aps, cfg):
    nc = p.nc
    NTOK, GRP = cfg["ntok"], cfg["grp"]
    ngrp, TCH, KD, NH = NTOK // GRP, GRP // 128, 16, 16
    ts, ss, vs, gs = p.sem("ts"), p.sem("ss"), p.sem("vs"), p.sem("gs")
    spl = p.sem("spl")
    sphi, solo = p.sem("sphi"), p.sem("solo")

    x_ap, wqT, wkT, wvT = aps["x"], aps["wqT"], aps["wkT"], aps["wvT"]
    xnsc, phisc, olocsc = aps["xnsc"], aps["phisc"], aps["olocsc"]
    pb, pbbf = aps["pb"], aps["pbbf"]
    ident, ident4 = aps["ident_sb"], aps["ident4_sb"]
    triu4, eye16 = aps["triu4_sb"], aps["eye16"]
    maskD, maskOff = aps["maskD_sb"], aps["maskOff_sb"]

    G = ctx.enter_context(nc.sbuf_tensor("G", [128, NH * 128], F32))
    Hs = ctx.enter_context(nc.sbuf_tensor("Hs", [128, NH * 128], F32))
    Gt = ctx.enter_context(nc.sbuf_tensor("Gt", [128, NH * 128], F32))
    G_bf = ctx.enter_context(nc.sbuf_tensor("G_bf", [128, NH * 128], BF16))
    H_bf = ctx.enter_context(nc.sbuf_tensor("H_bf", [128, NH * 128], BF16))
    aps["G"], aps["Hs"], aps["Gt"] = G, Hs, Gt

    xb = [ctx.enter_context(nc.sbuf_tensor(f"xb{i}", [128, 2048], F32)) for i in range(2)]
    xn_tm = [ctx.enter_context(nc.sbuf_tensor(f"xntm{i}", [128, 2048], BF16)) for i in range(2)]
    xn_fm = ctx.enter_context(nc.sbuf_tensor("xn_fm", [128, KD * GRP], BF16))
    q_fm = ctx.enter_context(nc.sbuf_tensor("q_fm", [128, 16 * GRP], BF16))
    k_tm = ctx.enter_context(nc.sbuf_tensor("k_tm", [128, TCH * 2048], BF16))
    v_tm = ctx.enter_context(nc.sbuf_tensor("v_tm", [128, TCH * 2048], BF16))
    ksl = [ctx.enter_context(nc.sbuf_tensor(f"ksl{i}", [128, 512], BF16)) for i in range(4)]
    ssq = ctx.enter_context(nc.sbuf_tensor("ssq", [128, 64], F32))
    junk = ctx.enter_context(nc.sbuf_tensor("junk", [128, 512], BF16))
    lnst = ctx.enter_context(nc.sbuf_tensor("lnst", [128, 8], F32))
    beta_g = ctx.enter_context(nc.sbuf_tensor("beta_g", [128, TCH * 16], F32))
    wb_sb = ctx.enter_context(nc.sbuf_tensor("wb_sb", [128, 256], BF16))

    def wb(name, dt=BF16):
        return ctx.enter_context(nc.sbuf_tensor(name, [128, 1024], dt))
    WB = []
    for wv in range(2):
        B = {}
        for nm in ("Kf", "NT", "Da", "Dat", "No", "C0K", "C0V",
                   "Pb", "Pbt", "X0", "BK", "BV", "Wp", "U0"):
            B[nm] = wb(f"{nm}_{wv}")
        B["R"] = wb(f"R_{wv}", F32)
        B["Zt"] = B["BK"]
        B["Mk"] = B["BV"]
        B["Qe"] = B["Da"]
        B["phi"] = B["Dat"]
        B["ol"] = B["X0"]
        WB.append(B)
    spill_g = {"phi": {}, "oloc": {}}
    wave_last_ts = {}

    st_w = BfRing(p, ctx, "wsl", (128, 512), nbuf=6)

    dbg_slot = [0]
    def dbg(buf, width=1024):
        if "dbgsc" not in aps:
            return
        i = dbg_slot[0]
        dbg_slot[0] += 1
        for e in ("ss", "vs", "gs", "ts"):
            p.wait("gpsimd", e, p.cnt[e])
        dv = p.dma("gpsimd", aps["dbgsc"][i][:, :width].rearrange("p (a n) -> p a n", n=512 if width >= 512 else width),
              buf[:, :width].rearrange("p (a n) -> p a n", n=512 if width >= 512 else width),
              p.sem("dbgs"))
        for e in ENGINES:
            p.wait(e, "dbgs", dv)

    ild = p.sem("ild")
    e1 = p.dma("gpsimd", G[:, :].rearrange("p (a b) -> p a b", b=512), eye16, ild)
    e2 = p.dma("gpsimd", Gt[:, :].rearrange("p (a b) -> p a b", b=512), eye16, ild)
    b1 = p.dma("gpsimd", xb[0][:, :256], aps["wbeta16_dram"], ild)
    p.wait("vector", ild, b1)
    p.op("vector", lambda e: e.memset(Hs[:, :], 0.0))
    p.op("vector", lambda e: e.memset(H_bf[:, :], 0.0))
    iv = p.op("vector", lambda e: e.tensor_copy(out=G_bf[:, :], in_=G[:, :]), sem=vs)
    p.wait("scalar", ild, b1)
    is_ = p.op("scalar", lambda e: e.copy(out=wb_sb[:, :], in_=xb[0][:, :256]), sem=ss)
    p.wait("tensor", vs, iv)
    p.wait("tensor", ss, is_)
    p.wait("gpsimd", vs, iv)

    wpst = [p.sem(f"wpst{i}") for i in range(2)]
    wpld = [p.sem(f"wpld{i}") for i in range(2)]
    prep_tiles = [(wi, r) for wi in range(3) for r in range(16)]
    wTs = [wqT, wkT, wvT]
    p.wait("scalar", ss, p.cnt[ss])
    prep_last_cast = {}
    for i, (wi, r) in enumerate(prep_tiles):
        bb = i % 2
        if i >= 2:
            p.wait("scalar", wpst[bb], 16 * (i // 2))
        lv = p.dma("scalar", xb[bb][:, :].rearrange("p (a n) -> p a n", n=512),
                   wTs[wi][r * 128:(r + 1) * 128, :]
                   .rearrange("p (a n) -> p a n", n=512), wpld[bb])
        ce = "vector" if i % 2 else "scalar"
        p.wait(ce, wpld[bb], lv)
        if i >= 2:
            p.wait(ce, wpst[bb], 16 * (i // 2))
        sem = vs if ce == "vector" else ss
        if ce == "vector":
            cv2 = p.op(ce, lambda e, bb=bb: e.tensor_copy(
                out=xn_tm[bb][:, :], in_=xb[bb][:, :]), sem=sem)
        else:
            cv2 = p.op(ce, lambda e, bb=bb: e.copy(
                out=xn_tm[bb][:, :], in_=xb[bb][:, :]), sem=sem)
        prep_last_cast[bb] = (sem, cv2)
        p.wait("gpsimd", sem, cv2)
        p.dma("gpsimd", aps["wqkvbf"][wi][r * 128:(r + 1) * 128, :]
              .rearrange("p (a n) -> p a n", n=512),
              xn_tm[bb][:, :].rearrange("p (a n) -> p a n", n=512), wpst[bb])
    wpst_fin = [p.cnt["wpst0"], p.cnt["wpst1"]]

    xld = [p.sem(f"xch{i}") for i in range(2)]
    p.wait("sync", ss, p.cnt[ss])
    ln_v = {}
    tpev = {}
    pbbf_state = {"i": 0, "ev": {}, }

    def transpose_batch4(srcs, out_ap, eng):
        i = pbbf_state["i"] % 2
        pbbf_state["i"] += 1
        bank = pbbf[i]
        pe = pbbf_state["ev"].get(i)
        if pe is not None:
            p.wait("tensor", pe[0], pe[1])
        v = None
        for t, s in enumerate(srcs):
            v = p.op("tensor", lambda e, s=s, t=t, bank=bank: e.transpose(
                bank[:, t * 128:(t + 1) * 128], s, ident[:, :]),
                sem=(ts if t == len(srcs) - 1 else None)) or v
        p.wait(eng, ts, v)
        sem = ss if eng == "scalar" else vs
        if eng == "scalar":
            ev = p.op(eng, lambda e, o=out_ap, bank=bank: e.copy(
                out=o, in_=bank[:, :512]), sem=sem)
        else:
            ev = p.op(eng, lambda e, o=out_ap, bank=bank: e.tensor_copy(
                out=o, in_=bank[:, :512]), sem=sem)
        pbbf_state["ev"][i] = (sem, ev)
        return sem, ev

    pp = PairPsum(p, [pb[0][:, :], pb[1][:, :], pb[2][:, :], pb[3][:, :]])

    def slot(pair_i, h):
        bank = pp.pairs[pair_i][h // 4]
        return bank[:, (h % 4) * 128:(h % 4 + 1) * 128]

    def mms8(ops, accs=None):
        i = pp.next()
        v = None
        for h, (l, r) in enumerate(ops):
            if accs is None:
                v = p.op("tensor", lambda e, s=slot(i, h), l=l, r=r:
                         e.matmul(s, l, r, start=True, stop=True),
                         sem=(ts if h == 7 else None)) or v
            else:
                p.op("tensor", lambda e, s=slot(i, h), l=l, r=r:
                     e.matmul(s, l, r, start=True, stop=False))
                v = p.op("tensor", lambda e, s=slot(i, h), l=accs[h][0], r=accs[h][1]:
                         e.matmul(s, l, r, start=False, stop=True),
                         sem=(ts if h == 7 else None)) or v
        return i, v

    def evac8(pair_i, mm_v, outs, engs=("vector", "scalar"), fns=(None, None)):
        res = []
        for b in range(2):
            eng = engs[b]
            sem = vs if eng == "vector" else ss
            p.wait(eng, ts, mm_v)
            bank = pp.pairs[pair_i][b]
            fn = fns[b]
            if fn is None:
                if eng == "scalar":
                    val = p.op(eng, lambda e, o=outs[b], i_=bank: e.copy(
                        out=o, in_=i_[:, :]), sem=sem)
                else:
                    val = p.op(eng, lambda e, o=outs[b], i_=bank: e.tensor_copy(
                        out=o, in_=i_[:, :]), sem=sem)
            else:
                val = p.op(eng, lambda e, o=outs[b], i_=bank, f=fn: f(e, o, i_),
                           sem=sem)
            res.append((sem, val))
        pp.set_ev(pair_i, res)
        return res

    ev_q, ev_k, ev_v = {}, {}, {}
    beta_v = 0
    spill_seq = [0]

    spl_group = 0
    phi_spl = 0
    oloc_spl = 0
    for g in range(ngrp):
        p.wait("tensor", spl, spl_group)
        for c in range(TCH):
            gc = g * TCH + c
            b = gc % 2
            p.wait("sync", vs, ln_v.get(gc - 2, 0))
            if gc < 2:
                sem_c, v_c = prep_last_cast[b]
                p.wait("sync", sem_c, v_c)
                p.wait("scalar", wpst[b], wpst_fin[b])
                p.wait("vector", wpst[b], wpst_fin[b])
            xv = p.dma("sync", xb[b][:, :].rearrange("p (a n) -> p a n", n=512),
                       x_ap[gc * 128:(gc + 1) * 128, :]
                       .rearrange("p (a n) -> p a n", n=512), xld[b])
            p.wait("vector", xld[b], xv)
            p.wait("scalar", xld[b], xv)
            s1v = p.op("vector", lambda e, b=b: e.reduce_sum(
                out=lnst[:, 0:1], in_=xb[b][:, :], axis=AX.X), sem=vs)
            s2v = p.op("scalar", lambda e, b=b: e.activation(
                out=xn_tm[b][:, :], in_=xb[b][:, :], func=AF.Square,
                accum_out=lnst[:, 1:2]), sem=ss)
            p.wait("vector", ss, s2v)
            p.chain("vector")
            p.op("vector", lambda e: e.tensor_scalar_mul(
                out=lnst[:, 2:3], in0=lnst[:, 0:1], scalar1=1.0 / 2048.0), sem=vs)
            p.chain("vector")
            p.op("vector", lambda e: e.tensor_mul(
                out=lnst[:, 3:4], in0=lnst[:, 2:3], in1=lnst[:, 2:3]), sem=vs)
            p.op("vector", lambda e: e.tensor_scalar_mul(
                out=lnst[:, 4:5], in0=lnst[:, 1:2], scalar1=1.0 / 2048.0), sem=vs)
            p.chain("vector")
            vv2 = p.op("vector", lambda e: e.tensor_sub(
                out=lnst[:, 4:5], in0=lnst[:, 4:5], in1=lnst[:, 3:4]), sem=vs)
            p.wait("scalar", vs, vv2)
            sq2 = p.op("scalar", lambda e: e.activation(
                out=lnst[:, 5:6], in_=lnst[:, 4:5], func=AF.Sqrt, bias=1e-5), sem=ss)
            p.wait("vector", ss, sq2)
            rc2 = p.op("vector", lambda e: e.reciprocal(
                out=lnst[:, 6:7], in_=lnst[:, 5:6]), sem=vs)
            p.chain("vector")
            rc2 = p.op("vector", lambda e: e.scalar_tensor_tensor(
                out=lnst[:, 7:8], in0=lnst[:, 2:3], scalar=-1.0,
                in1=lnst[:, 6:7], op0=ALU.mult, op1=ALU.mult), sem=vs)
            tp = tpev.get(gc - 2)
            if tp is not None:
                p.wait("vector", tp[0], tp[1])
            p.chain("vector")
            ln_v[gc] = p.op("vector", lambda e, b=b: e.tensor_scalar(
                out=xn_tm[b][:, :], in0=xb[b][:, :], scalar1=lnst[:, 6:7],
                scalar2=lnst[:, 7:8], op0=ALU.mult, op1=ALU.add), sem=vs)
            p.wait("tensor", vs, ln_v[gc])
            for kb in range(4):
                out_ap = (xn_fm[:, :].rearrange("p (k t) -> p k t", k=KD)
                          [:, kb * 4:(kb + 1) * 4, c * 128:(c + 1) * 128])
                tpev[gc] = transpose_batch4(
                    [xn_tm[b][:, (kb * 4 + j) * 128:(kb * 4 + j + 1) * 128]
                     for j in range(4)],
                    out_ap, "scalar" if kb % 2 else "vector")
        p.wait("gpsimd", ss, p.cnt[ss])
        p.wait("gpsimd", vs, p.cnt[vs])
        nsp = min(512, GRP)
        for k in range(KD):
            p.dma("gpsimd", xnsc[k][:, g * GRP:(g + 1) * GRP]
                  .rearrange("p (a n) -> p a n", n=nsp),
                  xn_fm[:, k * GRP:(k + 1) * GRP]
                  .rearrange("p (a n) -> p a n", n=nsp), spl)
        spl_group = p.cnt[spl]

        p.wait("tensor", ss, p.cnt[ss])
        p.wait("tensor", vs, p.cnt[vs])
        bank_reader = {}
        tiles = [(s, wn, kk) for s in range(4) for wn in ("q", "k", "v")
                 for kk in range(KD)]
        toks = {}
        LA_C = 4

        def issue_c(i):
            if 0 <= i < len(tiles):
                s_, wn, kk = tiles[i]
                wi = {"q": 0, "k": 1, "v": 2}[wn]
                pi = wi * 16 + kk
                p.wait("sync", wpst[pi % 2], 16 * (pi // 2 + 1))
                toks[i] = st_w.load(aps["wqkvbf"][wi][kk * 128:(kk + 1) * 128,
                                                      s_ * 512:(s_ + 1) * 512])

        for i in range(LA_C):
            issue_c(i)
        for i, (s, wn, kk) in enumerate(tiles):
            issue_c(i + LA_C)
            buf = st_w.wait_bf("tensor", toks[i])
            if kk == 0:
                for t in range(TCH):
                    br = bank_reader.get(t)
                    if br:
                        p.wait("tensor", br[0], br[1])
            v = None
            if wn == "q":
                for j in range(4):
                    v = p.op("tensor", lambda e, kk=kk, j=j, buf=buf:
                             e.matmul(pb[j][:, :GRP],
                                      buf[:, j * 128:(j + 1) * 128],
                                      xn_fm[:, kk * GRP:(kk + 1) * GRP],
                                      start=(kk == 0), stop=(kk == KD - 1)),
                             sem=(ts if j == 3 else None)) or v
            else:
                for t in range(TCH):
                    v = p.op("tensor", lambda e, kk=kk, t=t, buf=buf:
                             e.matmul(pb[t][:, :],
                                      xn_fm[:, kk * GRP + t * 128:kk * GRP + (t + 1) * 128],
                                      buf[:, :],
                                      start=(kk == 0), stop=(kk == KD - 1)),
                             sem=(ts if t == TCH - 1 else None)) or v
            st_w.set_rel(toks[i], v)
            if kk != KD - 1:
                continue
            mm_done = v
            for t in range(TCH):
                if wn == "q":
                    p.wait("scalar", ts, mm_done)
                    h_ = s * 4 + t
                    ev_q[(s, t)] = p.op("scalar", lambda e, h_=h_, t=t: e.activation(
                        out=q_fm[:, h_ * GRP:(h_ + 1) * GRP],
                        in_=pb[t][:, :GRP], func=AF.Silu), sem=ss)
                    bank_reader[t] = (ss, ev_q[(s, t)])
                elif wn == "v":
                    p.wait("vector", ts, mm_done)
                    ev_v[(s, t)] = p.op("vector", lambda e, s=s, t=t: e.tensor_copy(
                        out=v_tm[:, t * 2048 + s * 512:t * 2048 + (s + 1) * 512],
                        in_=pb[t][:, :]), sem=vs)
                    bank_reader[t] = (vs, ev_v[(s, t)])
                else:
                    p.wait("scalar", ts, mm_done)
                    pv = ev_k.get((s - 1, t))
                    if pv is not None:
                        p.wait("scalar", vs, pv)
                    sl_v = p.op("scalar", lambda e, t=t: e.activation(
                        out=ksl[t][:, :], in_=pb[t][:, :], func=AF.Silu), sem=ss)
                    bank_reader[t] = (ss, sl_v)
                    p.wait("vector", ss, sl_v)
                    c0_ = t * 16 + s * 4
                    sq_v = None
                    for hh in range(4):
                        sq_v = p.op("vector", lambda e, t=t, hh=hh, c0_=c0_:
                                    e.scalar_tensor_tensor(
                            out=junk[:, hh * 128:(hh + 1) * 128],
                            in0=ksl[t][:, hh * 128:(hh + 1) * 128],
                            scalar=1.0,
                            in1=ksl[t][:, hh * 128:(hh + 1) * 128],
                            op0=ALU.mult, op1=ALU.mult,
                            accum_out=ssq[:, c0_ + hh:c0_ + hh + 1]), sem=vs)
                    p.wait("scalar", vs, sq_v)
                    sq_s = p.op("scalar", lambda e, c0_=c0_: e.activation(
                        out=ssq[:, c0_:c0_ + 4], in_=ssq[:, c0_:c0_ + 4],
                        func=AF.Sqrt, bias=1e-6), sem=ss)
                    p.wait("vector", ss, sq_s)
                    p.op("vector", lambda e, c0_=c0_: e.reciprocal(
                        out=ssq[:, c0_:c0_ + 4], in_=ssq[:, c0_:c0_ + 4]), sem=vs)
                    p.chain("vector")
                    nv = None
                    for hh in range(4):
                        nv = p.op("vector", lambda e, t=t, s=s, hh=hh, c0_=c0_:
                                  e.tensor_scalar(
                            out=k_tm[:, t * 2048 + (s * 4 + hh) * 128:
                                     t * 2048 + (s * 4 + hh + 1) * 128],
                            in0=ksl[t][:, hh * 128:(hh + 1) * 128],
                            scalar1=ssq[:, c0_ + hh:c0_ + hh + 1],
                            scalar2=None, op0=ALU.mult), sem=vs)
                    ev_k[(s, t)] = nv
        bv = None
        for t in range(TCH):
            for kk in range(KD):
                bv = p.op("tensor", lambda e, t=t, kk=kk: e.matmul(
                    pb[4][:, t * 16:(t + 1) * 16],
                    xn_fm[:, kk * GRP + t * 128:kk * GRP + (t + 1) * 128],
                    wb_sb[:, kk * 16:(kk + 1) * 16],
                    start=(kk == 0), stop=(kk == KD - 1)),
                    sem=(ts if (t == TCH - 1 and kk == KD - 1) else None)) or bv
        p.wait("scalar", ts, bv)
        sg_v = p.op("scalar", lambda e: e.activation(
            out=beta_g[:, :], in_=pb[4][:, :TCH * 16], func=AF.Sigmoid), sem=ss)
        p.wait("vector", ss, sg_v)
        beta_v = p.op("vector", lambda e: e.tensor_scalar_mul(
            out=beta_g[:, :], in0=beta_g[:, :], scalar1=2.0), sem=vs)

        p.wait("tensor", ss, p.cnt[ss])
        p.wait("tensor", vs, p.cnt[vs])

        def wave_gen(c, w, B, pair):
            gc_ = g * TCH + c
            H0 = 8 * w
            kc = lambda h: k_tm[:, c * 2048 + (H0 + h) * 128:c * 2048 + (H0 + h + 1) * 128]
            vc = lambda h: v_tm[:, c * 2048 + (H0 + h) * 128:c * 2048 + (H0 + h + 1) * 128]
            qf = lambda h: q_fm[:, (H0 + h) * GRP + c * 128:(H0 + h) * GRP + (c + 1) * 128]
            bcol = lambda h: beta_g[:, c * 16 + H0 + h:c * 16 + H0 + h + 1]
            gG = lambda buf, h: buf[:, (H0 + h) * 128:(H0 + h + 1) * 128]
            hs = lambda buf, h: buf[:, h * 128:(h + 1) * 128]
            half2 = lambda buf: (buf[:, :512], buf[:, 512:])

            pend = {"ev": ()}

            def mms8p(ops, accs=None):
                for sem, val in pend["ev"]:
                    p.wait("tensor", sem, val)
                v = None
                for h, (l, r) in enumerate(ops):
                    s_ = pair[h // 4][:, (h % 4) * 128:(h % 4 + 1) * 128]
                    if accs is None:
                        v = p.op("tensor", lambda e, s=s_, l=l, r=r:
                                 e.matmul(s, l, r, start=True, stop=True),
                                 sem=(ts if h == 7 else None)) or v
                    else:
                        p.op("tensor", lambda e, s=s_, l=l, r=r:
                             e.matmul(s, l, r, start=True, stop=False))
                        v = p.op("tensor", lambda e, s=s_, l=accs[h][0], r=accs[h][1]:
                                 e.matmul(s, l, r, start=False, stop=True),
                                 sem=(ts if h == 7 else None)) or v
                return v

            def evac8p(mm_v, outs, engs=("vector", "scalar"), fns=(None, None)):
                res = []
                for b_ in range(2):
                    eng = engs[b_]
                    sem = vs if eng == "vector" else ss
                    p.wait(eng, ts, mm_v)
                    bank = pair[b_]
                    fn = fns[b_]
                    if fn is None:
                        if eng == "scalar":
                            val = p.op(eng, lambda e, o=outs[b_], i_=bank: e.copy(
                                out=o, in_=i_[:, :]), sem=sem)
                        else:
                            val = p.op(eng, lambda e, o=outs[b_], i_=bank: e.tensor_copy(
                                out=o, in_=i_[:, :]), sem=sem)
                    else:
                        val = p.op(eng, lambda e, o=outs[b_], i_=bank, f=fn: f(e, o, i_),
                                   sem=sem)
                    res.append((sem, val))
                pend["ev"] = tuple(res)
                return res

            p.wait("gpsimd", ts, wave_last_ts.get(w, 0))
            p.wait("gpsimd", vs, max(ev_k[(s2, c)] for s2 in range(4)))
            p.wait("gpsimd", vs, max(ev_v[(s2, c)] for s2 in range(4)))
            p.wait("gpsimd", vs, beta_v)
            p.wait("scalar", f"sphi{w}", spill_g["phi"].get(w, 0))
            p.wait("vector", f"sphi{w}", spill_g["phi"].get(w, 0))
            p.wait("scalar", f"solo{w}", spill_g["oloc"].get(w, 0))
            p.wait("vector", f"solo{w}", spill_g["oloc"].get(w, 0))
            gv_ = None
            for h in range(8):
                p.op("gpsimd", lambda e, o=B["BK"][:, h * 128:(h + 1) * 128],
                     i_=kc(h), sc=bcol(h):
                     e.tensor_scalar(out=o, in0=i_, scalar1=sc,
                                     scalar2=None, op0=ALU.mult), sem=gs)
                gv_ = p.op("gpsimd", lambda e, o=B["BV"][:, h * 128:(h + 1) * 128],
                           i_=vc(h), sc=bcol(h):
                           e.tensor_scalar(out=o, in0=i_, scalar1=sc,
                                           scalar2=None, op0=ALU.mult), sem=gs)
            bkbv = ((gs, gv_),)
            yield

            tev = []
            for half in range(2):
                tev.append(transpose_batch4([kc(half * 4 + j) for j in range(4)],
                                            B["Kf"][:, half * 512:(half + 1) * 512],
                                            "scalar"))
            yield
            for sem, val in tev:
                p.wait("tensor", sem, val)
            v1 = mms8p([(hs(B["Kf"], h), hs(B["Kf"], h)) for h in range(8)])
            p.wait("vector", ts, v1)
            dv = None
            for h in range(8):
                i_ = pair[h // 4][:, (h % 4) * 128:(h % 4 + 1) * 128]
                p.op("vector", lambda e, o=hs(B["Da"], h), i_=i_,
                     sc=bcol(h): e.scalar_tensor_tensor(
                    out=o, in0=i_, scalar=sc,
                    in1=maskD[:, :], op0=ALU.mult, op1=ALU.mult), sem=vs)
                dv = p.op("vector", lambda e, o=hs(B["No"], h), i_=i_,
                          sc=bcol(h): e.scalar_tensor_tensor(
                    out=o, in0=i_, scalar=sc,
                    in1=maskOff[:, :], op0=ALU.mult, op1=ALU.mult), sem=vs)
            pend["ev"] = ((vs, dv),)
            yield
            x0 = None
            p.wait("tensor", vs, dv)
            dat_ev = []
            for half in range(2):
                sem, ev = transpose_batch4([hs(B["Da"], half * 4 + j) for j in range(4)],
                                           B["Dat"][:, half * 512:(half + 1) * 512],
                                           "scalar")
                dat_ev.append((sem, ev))
                p.wait("vector", sem, ev)
                x0 = p.op("vector", lambda e, half=half: e.tensor_add(
                    out=B["X0"][:, half * 512:(half + 1) * 512],
                    in0=B["Dat"][:, half * 512:(half + 1) * 512],
                    in1=ident4[:, :]), sem=vs)
            yield
            X = B["X0"]
            Pc, Pct, Pn, Pnt = B["Da"], B["Dat"], B["Pb"], B["Pbt"]
            xready = ((vs, x0),)
            for sem, val in dat_ev:
                p.wait("tensor", sem, val)
            prev_ept = None
            for lev in range(1, 5):
                if prev_ept:
                    for sem, val in prev_ept:
                        p.wait("tensor", sem, val)
                vp = mms8p([(hs(Pct, h), hs(Pc, h)) for h in range(8)])
                e_p = evac8p(vp, half2(Pn), engs=("vector", "scalar"))
                yield
                if lev < 4:
                    vpt = mms8p([(hs(Pc, h), hs(Pct, h)) for h in range(8)])
                    prev_ept = evac8p(vpt, half2(Pnt), engs=("scalar", "vector"))
                    yield
                else:
                    prev_ept = None
                for sem, val in e_p:
                    p.wait("tensor", sem, val)
                for sem, val in xready:
                    p.wait("tensor", sem, val)
                vx = mms8p([(hs(Pn, h), hs(X, h)) for h in range(8)],
                           accs=[(ident[:, :], hs(X, h)) for h in range(8)])
                exv = evac8p(vx, half2(X), engs=("vector", "scalar"))
                pend["ev"] = tuple(exv)
                xready = exv
                Pc, Pn = Pn, Pc
                Pct, Pnt = Pnt, Pct
                yield
            XD = X
            for sem, val in xready:
                p.wait("tensor", sem, val)
            for sm, vv_ in bkbv:
                p.wait("tensor", sm, vv_)
            vc_ = mms8p([(hs(XD, h), hs(B["BK"], h)) for h in range(8)])
            e_ck = evac8p(vc_, half2(B["C0K"]), engs=("scalar", "vector"))
            yield
            vc2 = mms8p([(hs(XD, h), hs(B["BV"], h)) for h in range(8)])
            e_cv = evac8p(vc2, half2(B["C0V"]), engs=("vector", "scalar"))
            yield
            vn_ = mms8p([(hs(B["No"], h), hs(XD, h)) for h in range(8)])
            e_nt = evac8p(vn_, half2(B["NT"]), engs=("scalar", "vector"))
            yield
            for evs in (e_ck, e_cv, e_nt):
                for sem, val in evs:
                    p.wait("tensor", sem, val)
            SK, SV = B["C0K"], B["C0V"]
            for j in range(3):
                lastj = (j == 2)
                okK = B["Wp"] if lastj else B["X0"]
                okV = B["U0"] if lastj else B["Pb"]
                vhk = mms8p([(hs(B["NT"], h), hs(SK, h)) for h in range(8)],
                            accs=[(ident[:, :], hs(B["C0K"], h)) for h in range(8)])
                e_k = evac8p(vhk, half2(okK), engs=("vector", "scalar"))
                yield
                vhv = mms8p([(hs(B["NT"], h), hs(SV, h)) for h in range(8)],
                            accs=[(ident[:, :], hs(B["C0V"], h)) for h in range(8)])
                e_v = evac8p(vhv, half2(okV), engs=("scalar", "vector"))
                for sm, vv2_ in list(e_k) + list(e_v):
                    p.wait("tensor", sm, vv2_)
                SK, SV = okK, okV
                yield
            vz = mms8p([(hs(B["Wp"], h), kc(h)) for h in range(8)])
            e_z = evac8p(vz, half2(B["Zt"]), engs=("scalar", "vector"))
            yield
            vr = mms8p([(kc(h), hs(B["U0"], h)) for h in range(8)])
            e_r = evac8p(vr, half2(B["R"]), engs=("vector", "scalar"))
            yield
            va = mms8p([(hs(B["Kf"], h), qf(h)) for h in range(8)])
            mkfn = lambda e, o, i_: e.tensor_mul(out=o, in0=i_[:, :], in1=triu4[:, :])
            e_a = evac8p(va, half2(B["Mk"]), engs=("vector", "vector"),
                         fns=(mkfn, mkfn))
            yield
            for sm, vv2_ in list(e_z) + list(e_a):
                p.wait("tensor", sm, vv2_)
            vf = mms8p([(hs(B["Wp"], h), hs(B["Mk"], h)) for h in range(8)])
            qefn = lambda bk: (lambda e, o, i_, bk=bk, c=c, H0=H0: e.tensor_sub(
                out=o.rearrange("p (hh t) -> p hh t", hh=4),
                in0=q_fm[:, :].rearrange("p (hh t) -> p hh t", t=GRP)
                [:, H0 + bk * 4:H0 + bk * 4 + 4, c * 128:(c + 1) * 128],
                in1=i_[:, :].rearrange("p (hh t) -> p hh t", hh=4)))
            e_q = evac8p(vf, half2(B["Qe"]), engs=("vector", "vector"),
                         fns=(qefn(0), qefn(1)))
            yield
            for sm, vv2_ in e_q:
                p.wait("tensor", sm, vv2_)
            vg = mms8p([(gG(G_bf, h), hs(B["Qe"], h)) for h in range(8)])
            e_phi = evac8p(vg, half2(B["phi"]), engs=("scalar", "vector"))
            yield
            vo = mms8p([(hs(B["Mk"], h), hs(B["U0"], h)) for h in range(8)],
                       accs=[(hs(B["Qe"], h), gG(H_bf, h)) for h in range(8)])
            e_ol = evac8p(vo, half2(B["ol"]), engs=("vector", "scalar"))
            yield
            for sm, v_ in e_phi:
                p.wait("gpsimd", sm, v_)
            for sm, v_ in e_ol:
                p.wait("gpsimd", sm, v_)
            spill_g["phi"][w] = p.dma(
                "gpsimd", phisc[gc_, w][:, :].rearrange("p (a n) -> p a n", n=512),
                B["phi"][:, :].rearrange("p (a n) -> p a n", n=512), p.sem(f"sphi{w}"))
            spill_g["oloc"][w] = p.dma(
                "gpsimd", olocsc[gc_, w][:, :].rearrange("p (a n) -> p a n", n=512),
                B["ol"][:, :].rearrange("p (a n) -> p a n", n=512), p.sem(f"solo{w}"))
            gouts = (G[:, (H0 + 0) * 128:(H0 + 4) * 128],
                     G[:, (H0 + 4) * 128:(H0 + 8) * 128])
            gtouts = (Gt[:, (H0 + 0) * 128:(H0 + 4) * 128],
                      Gt[:, (H0 + 4) * 128:(H0 + 8) * 128])
            houts = (Hs[:, (H0 + 0) * 128:(H0 + 4) * 128],
                     Hs[:, (H0 + 4) * 128:(H0 + 8) * 128])
            vgt = mms8p([(gG(G_bf, h), hs(B["Zt"], h)) for h in range(8)])
            p.wait("vector", ts, vgt)
            gt2 = None
            for bk in range(2):
                gt2 = p.op("vector", lambda e, o=gtouts[bk], i_=pair[bk]:
                           e.scalar_tensor_tensor(
                    out=o, in0=i_[:, :], scalar=-1.0,
                    in1=o, op0=ALU.mult, op1=ALU.add), sem=vs)
            pend["ev"] = ((vs, gt2),)
            yield
            vzg = mms8p([(hs(B["Zt"], h), gG(G_bf, h)) for h in range(8)])
            p.wait("vector", ts, vzg)
            gv2 = None
            for bk in range(2):
                gv2 = p.op("vector", lambda e, o=gouts[bk], i_=pair[bk]:
                           e.scalar_tensor_tensor(
                    out=o, in0=i_[:, :], scalar=-1.0,
                    in1=o, op0=ALU.mult, op1=ALU.add), sem=vs)
            pend["ev"] = ((vs, gv2),)
            yield
            for sm, vv2_ in e_r:
                p.wait("vector", sm, vv2_)
            vzh = mms8p([(hs(B["Zt"], h), gG(H_bf, h)) for h in range(8)])
            p.wait("vector", ts, vzh)
            hv2 = None
            for bk in range(2):
                p.op("vector", lambda e, o=houts[bk], i_=pair[bk]:
                     e.scalar_tensor_tensor(
                    out=o, in0=i_[:, :], scalar=-1.0,
                    in1=o, op0=ALU.mult, op1=ALU.add), sem=vs)
                p.chain("vector")
                hv2 = p.op("vector", lambda e, o=houts[bk],
                           r=B["R"][:, bk * 512:(bk + 1) * 512]:
                           e.tensor_add(out=o, in0=o, in1=r), sem=vs)
            pend["ev"] = ((vs, hv2),)
            p.wait("scalar", vs, hv2)
            sl8 = slice(H0 * 128, (H0 + 8) * 128)
            cv = p.op("scalar", lambda e, sl8=sl8: e.copy(
                out=G_bf[:, sl8], in_=G[:, sl8]), sem=ss)
            cv = p.op("scalar", lambda e, sl8=sl8: e.copy(
                out=H_bf[:, sl8], in_=Hs[:, sl8]), sem=ss)
            p.wait("tensor", ss, cv)
            wave_last_ts[w] = vzh

        def wave_job(c, w):
            qv = max(ev_q[(s, c)] for s in range(4))
            kv_ = max(ev_k[(s, c)] for s in range(4))
            vv_ = max(ev_v[(s, c)] for s in range(4))
            p.wait("tensor", ss, qv)
            p.wait("tensor", vs, max(kv_, vv_, beta_v))
            p.wait("scalar", vs, max(kv_, vv_, beta_v))
            pairs = ((pb[0][:, :], pb[1][:, :]), (pb[2][:, :], pb[3][:, :]))
            yield from wave_gen(c, w, WB[w], pairs[w])

        jobs = [(c, w) for c in range(TCH) for w in range(2)]
        gens = []
        ji = 0
        primed = False
        while ji < len(jobs) or gens:
            while len(gens) < 2 and ji < len(jobs):
                g2 = wave_job(*jobs[ji])
                ji += 1
                gens.append(g2)
                if not primed:
                    primed = True
                    for _ in range(12):
                        try:
                            next(g2)
                        except StopIteration:
                            break
            for g2 in list(gens):
                try:
                    next(g2)
                except StopIteration:
                    gens.remove(g2)

    return {"spl": p.cnt["spl"], "sphi0": p.cnt["sphi0"], "sphi1": p.cnt["sphi1"],
            "solo0": p.cnt["solo0"], "solo1": p.cnt["solo1"], "ts": p.cnt[ts]}



def relay_send(p, aps):
    ts, vs, ss = p.sem("ts"), p.sem("vs"), p.sem("ss")
    ccs = p.sem("ccs")
    snd = p.sem("snd")
    p.wait("gpsimd", vs, p.cnt[vs])
    v1 = p.dma("gpsimd", aps["ccin"][0].rearrange("p (a n) -> p a n", n=512),
               aps["Gt"][:, :].rearrange("p (a n) -> p a n", n=512), snd)
    v2 = p.dma("gpsimd", aps["ccin"][1].rearrange("p (a n) -> p a n", n=512),
               aps["Hs"][:, :].rearrange("p (a n) -> p a n", n=512), snd)
    p.wait("gpsimd", snd, v2)
    cv = p.inc(ccs, 1)
    p.emit("gpsimd", lambda e: e.collective_compute(
        "AllGather", ALU.bypass, replica_groups=[list(range(8))],
        ins=[aps["ccin"].opt()], outs=[aps["ccout"].opt()],
    ).then_inc(p.handles[ccs], 1))
    return cv


def compose_phase(p, ctx, aps, cc_v):
    nc = p.nc
    NH = 16
    ts, ss, vs, gs = p.sem("ts"), p.sem("ss"), p.sem("vs"), p.sem("gs")
    ccs, cld = p.sem("ccs"), p.sem("cld")

    S_in = ctx.enter_context(nc.sbuf_tensor("S_in", [128, NH * 128], F32))
    S_in_bf = aps["S_in_bf_outer"]
    tstate = ctx.enter_context(nc.sbuf_tensor("tstate", [128, NH * 128], F32))
    t_bf = ctx.enter_context(nc.sbuf_tensor("t_bf", [128, NH * 128], BF16))
    gtj_bf = ctx.enter_context(nc.sbuf_tensor("gtj_bf", [128, NH * 128], BF16))
    hj = [ctx.enter_context(nc.sbuf_tensor(f"hj{i}", [128, NH * 128], F32))
          for i in range(2)]
    gtj_f = [ctx.enter_context(nc.sbuf_tensor(f"gtj_f{i}", [128, NH * 128], F32))
             for i in range(2)]
    aps["S_in_bf"] = S_in_bf

    pb = aps["pb"]
    sel = aps["sel_sb"]

    p.wait("gpsimd", ccs, cc_v)
    p.wait("sync", ccs, cc_v)
    p.wait("tensor", ss, p.cnt[ss])
    p.wait("tensor", vs, p.cnt[vs])
    mz = p.op("vector", lambda e: e.memset(tstate[:, :], 0.0), sem=vs)
    mz = p.op("vector", lambda e: e.memset(S_in[:, :], 0.0), sem=vs)
    mz = p.op("vector", lambda e: e.memset(t_bf[:, :], 0.0), sem=vs)
    cld2 = p.sem("cld2")
    cgt_prev, hj_prev = {}, {}
    gvals = {}
    clds = [p.sem(f"cldr{i}") for i in range(2)]
    cld2s = [p.sem(f"cld2r{i}") for i in range(2)]

    def cload(j):
        jb = j % 2
        p.wait("sync", ss, cgt_prev.get(j - 2, 0))
        p.wait("sync", vs, hj_prev.get(j - 2, 0))
        g1 = p.dma("sync", gtj_f[jb][:, :].rearrange("p (a n) -> p a n", n=512),
                   aps["ccout"][2 * j].rearrange("p (a n) -> p a n", n=512), clds[jb])
        g2 = p.dma("sync", hj[jb][:, :].rearrange("p (a n) -> p a n", n=512),
                   aps["ccout"][2 * j + 1].rearrange("p (a n) -> p a n", n=512),
                   cld2s[jb])
        gvals[j] = (g1, g2)

    cload(0)
    for j in range(7):
        jb = j % 2
        if j < 6:
            cload(j + 1)
        g1, g2 = gvals[j]
        p.wait("scalar", clds[jb], g1)
        cgt = p.op("scalar", lambda e, jb=jb: e.copy(
            out=gtj_bf[:, :], in_=gtj_f[jb][:, :]), sem=ss)
        cgt_prev[j] = cgt
        p.wait("tensor", ss, cgt)
        p.wait("tensor", vs, mz)
        v = None
        for h in range(NH):
            bank = pb[h // 4]
            v = p.op("tensor", lambda e, h=h, bank=bank: e.matmul(
                bank[:, (h % 4) * 128:(h % 4 + 1) * 128],
                gtj_bf[:, h * 128:(h + 1) * 128],
                t_bf[:, h * 128:(h + 1) * 128],
                start=True, stop=True), sem=(ts if h % 4 == 3 else None)) or v
        p.wait("vector", ts, v)
        p.wait("vector", cld2s[jb], g2)
        mz = None
        for b4 in range(4):
            sl_ = slice(b4 * 512, (b4 + 1) * 512)
            p.op("vector", lambda e, b4=b4, sl_=sl_, jb=jb: e.tensor_add(
                out=tstate[:, sl_], in0=pb[b4][:, :], in1=hj[jb][:, sl_]), sem=vs)
            p.chain("vector")
            mz = p.op("vector", lambda e, b4=b4, sl_=sl_, j=j: e.scalar_tensor_tensor(
                out=S_in[:, sl_], in0=tstate[:, sl_], scalar=sel[:, j + 1:j + 2],
                in1=S_in[:, sl_], op0=ALU.mult, op1=ALU.add), sem=vs)
            hj_prev[j] = mz
        p.wait("scalar", vs, mz)
        ctv = p.op("scalar", lambda e: e.copy(out=t_bf[:, :], in_=tstate[:, :]), sem=ss)
        p.wait("tensor", ss, ctv)
    p.wait("scalar", vs, mz)
    sbv = p.op("scalar", lambda e: e.copy(out=S_in_bf[:, :], in_=S_in[:, :]), sem=ss)
    return sbv



def attnpost_phase(p, ctx, aps, cfg, sin_v):
    nc = p.nc
    NTOK = cfg["ntok"]
    NCH, KD, NH = NTOK // 128, 16, 16
    ts, ss, vs, gs = p.sem("ts"), p.sem("ss"), p.sem("vs"), p.sem("gs")
    ost = p.sem("ost")

    pb, pbbf, ident = aps["pb"], aps["pbbf"], aps["ident_sb"]
    phisc, olocsc, xnsc, gsc = aps["phisc"], aps["olocsc"], aps["xnsc"], aps["gsc"]
    x_ap, out_ap = aps["x"], aps["out"]
    S_in_bf = aps["S_in_bf"]

    woT_sb = ctx.enter_context(nc.sbuf_tensor("woT_sb", [128, KD * 2048], BF16))
    wo_f = [ctx.enter_context(nc.sbuf_tensor(f"wo_f{i}", [128, 2048], F32)) for i in range(2)]
    wg1_sb = ctx.enter_context(nc.sbuf_tensor("wg1_sb", [128, 2048], BF16))
    wg2_sb = ctx.enter_context(nc.sbuf_tensor("wg2_sb", [128, 2048], BF16))
    wgf = ctx.enter_context(nc.sbuf_tensor("wgf", [128, 2048], F32))
    gn_sb = ctx.enter_context(nc.sbuf_tensor("gn_sb", [128, 16], F32))
    B2 = []
    for i in range(2):
        d2 = {}
        d2["phi"] = ctx.enter_context(nc.sbuf_tensor(f"phi_b{i}", [128, 2048], BF16))
        d2["oloc"] = ctx.enter_context(nc.sbuf_tensor(f"oloc_b{i}", [128, 2048], BF16))
        d2["xn"] = ctx.enter_context(nc.sbuf_tensor(f"xn_c{i}", [128, 2048], BF16))
        d2["z"] = ctx.enter_context(nc.sbuf_tensor(f"z_c{i}", [128, 128], BF16))
        d2["gate"] = ctx.enter_context(nc.sbuf_tensor(f"gate_c{i}", [128, 2048], BF16))
        d2["og"] = ctx.enter_context(nc.sbuf_tensor(f"og_c{i}", [128, 2048], F32))
        d2["ogn"] = ctx.enter_context(nc.sbuf_tensor(f"ogn_c{i}", [128, 2048], BF16))
        d2["ogf"] = ctx.enter_context(nc.sbuf_tensor(f"ogf_c{i}", [128, 2048], BF16))
        d2["ssq"] = ctx.enter_context(nc.sbuf_tensor(f"ssq2_{i}", [128, 16], F32))
        d2["junk"] = d2["ogn"]
        d2["s"] = ctx.enter_context(nc.sbuf_tensor(f"s_c{i}", [128, 2048], BF16))
        d2["g"] = ctx.enter_context(nc.sbuf_tensor(f"g_c{i}", [128, 2048], BF16))
        d2["x"] = ctx.enter_context(nc.sbuf_tensor(f"x_c{i}", [128, 2048], F32))
        d2["sg"] = d2["og"]
        B2.append(d2)
    wld2 = [p.sem(f"wld2_{i}") for i in range(2)]
    xcp = [p.sem(f"xcp{i}") for i in range(2)]
    xco = [p.sem(f"xco{i}") for i in range(2)]
    xcx = [p.sem(f"xcx{i}") for i in range(2)]
    ostp = [p.sem(f"ost{i}") for i in range(2)]
    xgl = [p.sem(f"xgl{i}") for i in range(2)]
    if "gst_final" in cfg:
        p.wait("scalar", "gst", cfg["gst_final"])
        p.wait("sync", "spl", cfg["spl_final"])
        for nm in ("sphi0", "sphi1", "solo0", "solo1"):
            p.wait("sync", nm, cfg[nm])

    pox = {}
    gx = {}

    def issue_pox(c):
        b = c % 2
        B = B2[b]
        p.dma("sync", B["phi"][:, :1024].rearrange("p (a n) -> p a n", n=512),
              phisc[c, 0].rearrange("p (a n) -> p a n", n=512), xcp[b])
        pv_p = p.dma("sync", B["phi"][:, 1024:].rearrange("p (a n) -> p a n", n=512),
                     phisc[c, 1].rearrange("p (a n) -> p a n", n=512), xcp[b])
        p.dma("sync", B["oloc"][:, :1024].rearrange("p (a n) -> p a n", n=512),
              olocsc[c, 0].rearrange("p (a n) -> p a n", n=512), xco[b])
        pv_o = p.dma("sync", B["oloc"][:, 1024:].rearrange("p (a n) -> p a n", n=512),
                     olocsc[c, 1].rearrange("p (a n) -> p a n", n=512), xco[b])
        pv_t = p.dma("sync", B["xn"][:, :].rearrange("p (k t) -> p k t", k=KD),
                     xnsc[:, :, c * 128:(c + 1) * 128].rearrange("k p t -> p k t"),
                     xcx[b])
        pox[c] = (pv_p, pv_o, pv_t)

    def issue_gx(c):
        b = c % 2
        B = B2[b]
        p.dma("scalar", B["g"][:, :].rearrange("p (a n) -> p a n", n=1024),
              gsc[c * 128:(c + 1) * 128, :].rearrange("p (a n) -> p a n", n=1024),
              xgl[b])
        gx[c] = p.dma("scalar", B["x"][:, :].rearrange("p (a n) -> p a n", n=512),
                      x_ap[c * 128:(c + 1) * 128, :].rearrange("p (a n) -> p a n", n=512),
                      xgl[b])

    issue_pox(0)
    issue_pox(1)
    issue_gx(0)
    issue_gx(1)

    gl = p.dma("sync", gn_sb[:, :], aps["gn16"], p.sem("gnl"))
    wo_c = {}
    for dd in range(16):
        b = dd % 2
        if dd >= 2:
            p.wait("sync", vs, wo_c[dd - 2])
        lv = p.dma("sync", wo_f[b][:, :].rearrange("p (a n) -> p a n", n=512),
                   aps["woT"][dd * 128:(dd + 1) * 128, :]
                   .rearrange("p (a n) -> p a n", n=512), wld2[b])
        p.wait("vector", wld2[b], lv)
        p.wait("vector", "gnl", gl)
        wo_c[dd] = p.op("vector", lambda e, dd=dd, b=b: e.tensor_scalar(
            out=woT_sb[:, dd * 2048:(dd + 1) * 2048], in0=wo_f[b][:, :],
            scalar1=gn_sb[:, dd:dd + 1], scalar2=None, op0=ALU.mult), sem=vs)
    l1 = p.dma("sync", wgf[:, :].rearrange("p (a n) -> p a n", n=512),
               aps["wg1T4"].rearrange("p (a n) -> p a n", n=512), p.sem("wgl1"))
    p.wait("scalar", "wgl1", l1)
    c1 = p.op("scalar", lambda e: e.copy(out=wg1_sb[:, :], in_=wgf[:, :]), sem=ss)
    p.wait("sync", ss, c1)
    l2 = p.dma("sync", wgf[:, :].rearrange("p (a n) -> p a n", n=512),
               aps["wg2T"].rearrange("p (a n) -> p a n", n=512), p.sem("wgl2"))
    p.wait("scalar", "wgl2", l2)
    c2 = p.op("scalar", lambda e: e.copy(out=wg2_sb[:, :], in_=wgf[:, :]), sem=ss)
    p.wait("tensor", ss, c2)
    p.wait("tensor", ss, sin_v)

    ev_og = {}

    def chunk_gen(c):
        b = c % 2
        B = B2[b]
        pairA = (pb[2 * b][:, :], pb[2 * b + 1][:, :])
        zb = pb[4 + b][:, :]
        if c >= 2:
            p.wait("vector", ostp[b], ev_og.get(("ost", c - 2), 0))
        pv_p, pv_o, pv_t = pox[c]
        yield
        p.wait("tensor", vs, ev_og.get(("sev", c - 2, 3), 0))
        p.wait("tensor", xcp[b], pv_p)
        for w in range(2):
            H0 = 8 * w
            if w == 1:
                p.wait("tensor", vs, ev_og[("oadd", c, 0)])
            v = None
            for h in range(8):
                bank = pairA[h // 4]
                v = p.op("tensor", lambda e, h=h, H0=H0, bank=bank: e.matmul(
                    bank[:, (h % 4) * 128:(h % 4 + 1) * 128],
                    B["phi"][:, (H0 + h) * 128:(H0 + h + 1) * 128],
                    S_in_bf[:, (H0 + h) * 128:(H0 + h + 1) * 128],
                    start=True, stop=True), sem=(ts if h == 7 else None)) or v
            p.wait("vector", ts, v)
            p.wait("vector", xco[b], pv_o)
            oaddv = None
            for b2 in range(2):
                sl_ = slice((H0 + b2 * 4) * 128, (H0 + b2 * 4 + 4) * 128)
                oaddv = p.op("vector", lambda e, b2=b2, sl_=sl_: e.tensor_add(
                    out=B["og"][:, sl_], in0=pairA[b2][:, :],
                    in1=B["oloc"][:, sl_]), sem=vs)
            ev_og[("oadd", c, w)] = oaddv
            yield
        p.wait("tensor", xcx[b], pv_t)
        zv = None
        for kk in range(KD):
            zv = p.op("tensor", lambda e, kk=kk: e.matmul(
                zb[:, :128], wg1_sb[:, kk * 128:(kk + 1) * 128],
                B["xn"][:, kk * 128:(kk + 1) * 128],
                start=(kk == 0), stop=(kk == KD - 1)),
                sem=(ts if kk == KD - 1 else None)) or zv
        ev_og[("zdone", c)] = zv
        p.wait("scalar", ts, zv)
        zc = p.op("scalar", lambda e: e.copy(out=B["z"][:, :], in_=zb[:, :128]), sem=ss)
        p.wait("tensor", ss, zc)
        if c + 2 < NCH:
            p.wait("sync", ts, zv)
            p.wait("sync", vs, ev_og[("oadd", c, 1)])
            issue_pox(c + 2)
        yield
        gt_v = None
        for dt_ in range(4):
            mv = p.op("tensor", lambda e, dt_=dt_: e.matmul(
                zb[:, :], B["z"][:, :], wg2_sb[:, dt_ * 512:(dt_ + 1) * 512],
                start=True, stop=True), sem=ts)
            p.wait("scalar", ts, mv)
            gt_v = p.op("scalar", lambda e, dt_=dt_: e.activation(
                out=B["gate"][:, dt_ * 512:(dt_ + 1) * 512], in_=zb[:, :],
                func=AF.Sigmoid), sem=ss)
            if dt_ < 3:
                p.wait("tensor", ss, gt_v)
            yield
        p.wait("vector", ss, gt_v)
        p.wait("vector", vs, ev_og[("oadd", c, 1)])
        p.chain("vector")
        for b4 in range(4):
            sl_ = slice(b4 * 512, (b4 + 1) * 512)
            p.op("vector", lambda e, sl_=sl_: e.tensor_mul(
                out=B["og"][:, sl_], in0=B["og"][:, sl_],
                in1=B["gate"][:, sl_]), sem=vs)
            p.chain("vector")
        sqv = None
        for hh in range(16):
            sqv = p.op("vector", lambda e, hh=hh: e.scalar_tensor_tensor(
                out=B["junk"][:, hh * 128:(hh + 1) * 128],
                in0=B["og"][:, hh * 128:(hh + 1) * 128],
                scalar=1.0, in1=B["og"][:, hh * 128:(hh + 1) * 128],
                op0=ALU.mult, op1=ALU.mult,
                accum_out=B["ssq"][:, hh:hh + 1]), sem=vs)
        yield
        p.wait("scalar", vs, sqv)
        rs = p.op("scalar", lambda e: e.activation(
            out=B["ssq"][:, :], in_=B["ssq"][:, :], func=AF.Sqrt, bias=1e-6,
            scale=1.0 / 128.0), sem=ss)
        p.wait("vector", ss, rs)
        p.op("vector", lambda e: e.reciprocal(out=B["ssq"][:, :], in_=B["ssq"][:, :]),
             sem=vs)
        p.chain("vector")
        nv = None
        for h in range(16):
            nv = p.op("vector", lambda e, h=h: e.tensor_scalar(
                out=B["ogn"][:, h * 128:(h + 1) * 128],
                in0=B["og"][:, h * 128:(h + 1) * 128],
                scalar1=B["ssq"][:, h:h + 1], scalar2=None, op0=ALU.mult), sem=vs)
        yield
        p.wait("tensor", vs, nv)
        tse = {}
        for kb in range(4):
            out_tp = (B["ogf"][:, :].rearrange("p (a n) -> p a n", a=16)
                      [:, kb * 4:(kb + 1) * 4, :])
            tse[kb] = transpose_like(p, aps,
                                     [B["ogn"][:, (kb * 4 + j) * 128:(kb * 4 + j + 1) * 128]
                                      for j in range(4)],
                                     out_tp, "scalar" if kb % 2 else "vector")
            yield
        for kb in range(4):
            p.wait("tensor", tse[kb][0], tse[kb][1])
        p.wait("tensor", vs, ev_og[("oadd", c, 1)])
        p.wait("vector", xgl[b], gx[c])
        sv = None
        cb = None
        for dt_ in range(4):
            bank = pairA[dt_ % 2]
            if dt_ >= 2:
                p.wait("tensor", vs, ev_og[("sev", c, dt_ - 2)])
            for dd in range(KD):
                sv = p.op("tensor", lambda e, dt_=dt_, dd=dd, bank=bank: e.matmul(
                    bank[:, :], B["ogf"][:, dd * 128:(dd + 1) * 128],
                    woT_sb[:, dd * 2048 + dt_ * 512:dd * 2048 + (dt_ + 1) * 512],
                    start=(dd == 0), stop=(dd == KD - 1)),
                    sem=(ts if dd == KD - 1 else None)) or sv
            p.wait("vector", ts, sv)
            sl4 = slice(dt_ * 512, (dt_ + 1) * 512)
            ev_og[("sev", c, dt_)] = p.op("vector", lambda e, sl4=sl4, bank=bank:
                                          e.scalar_tensor_tensor(
                out=B["sg"][:, sl4], in0=bank[:, :], scalar=1.0,
                in1=B["g"][:, sl4], op0=ALU.mult, op1=ALU.mult), sem=vs)
            p.chain("vector")
            cb = p.op("vector", lambda e, sl4=sl4: e.tensor_add(
                out=B["sg"][:, sl4], in0=B["sg"][:, sl4], in1=B["x"][:, sl4]),
                sem=vs)
            p.wait("gpsimd", vs, cb)
            ev_og[("ost", c)] = p.dma(
                "gpsimd", out_ap[c * 128:(c + 1) * 128, dt_ * 512:(dt_ + 1) * 512],
                B["sg"][:, sl4], ostp[b])
            yield
        ev_og[("mmdone", c)] = sv
        ev_og[("sgdone", c)] = cb
        ev_og[("comb", c)] = cb
        if c + 2 < NCH:
            p.wait("scalar", vs, cb)
            issue_gx(c + 2)

    gens = []
    next_c = 0
    primed = False
    while next_c < NCH or gens:
        while len(gens) < 2 and next_c < NCH:
            g_ = chunk_gen(next_c)
            next_c += 1
            gens.append(g_)
            for _ in range(9 if not primed else 2):
                try:
                    next(g_)
                except StopIteration:
                    break
            primed = True
        for g_ in list(gens):
            try:
                next(g_)
            except StopIteration:
                gens.remove(g_)
    return {"ost0": p.cnt["ost0"], "ost1": p.cnt["ost1"]}


def transpose_like(p, aps, srcs, out_ap, eng):
    st = p.__dict__.setdefault("_tpstate", {"i": 0, "ev": {}})
    ts, ss, vs = p.sem("ts"), p.sem("ss"), p.sem("vs")
    i = st["i"] % 2
    st["i"] += 1
    bank = aps["pbbf"][i]
    pe = st["ev"].get(i)
    if pe is not None:
        p.wait("tensor", pe[0], pe[1])
    v = None
    for t, s in enumerate(srcs):
        v = p.op("tensor", lambda e, s=s, t=t, bank=bank: e.transpose(
            bank[:, t * 128:(t + 1) * 128], s, aps["ident_sb"][:, :]),
            sem=(ts if t == len(srcs) - 1 else None)) or v
    p.wait(eng, ts, v)
    sem = ss if eng == "scalar" else vs
    if eng == "scalar":
        ev = p.op(eng, lambda e, o=out_ap, bank=bank: e.copy(
            out=o, in_=bank[:, :512]), sem=sem)
    else:
        ev = p.op(eng, lambda e, o=out_ap, bank=bank: e.tensor_copy(
            out=o, in_=bank[:, :512]), sem=sem)
    st["ev"][i] = (sem, ev)
    return sem, ev



def build_kernel(ntok, grp=512):
    from contextlib import ExitStack
    nc = bass.Bass("TRN2", target_bir_lowering=False, debug=False)
    NCH = ntok // 128
    d = {}
    def din(name, shape, dt=F32):
        d[name] = nc.dram_tensor(name, list(shape), dt, kind="ExternalInput").ap()
        return d[name]
    x = din("x", (ntok, 2048))
    out = nc.dram_tensor("out", [ntok, 2048], F32, kind="ExternalOutput").ap()
    din("wqT", (2048, 2048)); din("wkT", (2048, 2048)); din("wvT", (2048, 2048))
    din("wbeta16", (128, 256)); din("wg1T4", (128, 2048)); din("wg2T", (128, 2048))
    din("woT", (2048, 2048)); din("gn16", (128, 16))
    din("wgu4", (128, 128, 16, 128)); din("wdn4", (16, 128, 64, 128))
    din("eye16", (128, 4, 512))
    din("ident", (128, 128), BF16); din("ident4", (128, 512), BF16)
    din("maskD", (128, 128)); din("maskOff", (128, 128))
    din("triu4", (128, 512)); din("sel", (128, 8))
    din("cpool", (128, 3))

    xnsc = nc.dram_tensor("xnsc", [16, 128, ntok], BF16).ap()
    phisc = nc.dram_tensor("phisc", [NCH, 2, 128, 1024], BF16).ap()
    olocsc = nc.dram_tensor("olocsc", [NCH, 2, 128, 1024], BF16).ap()
    gsc = nc.dram_tensor("gsc", [ntok, 2048], BF16).ap()
    wqkvbf = nc.dram_tensor("wqkvbf", [3, 2048, 2048], BF16).ap()
    ccin = nc.dram_tensor("ccin", [2, 128, 2048], F32).ap()
    ccout = nc.dram_tensor("ccout", [16, 128, 2048], F32, addr_space="Shared").ap()

    p = Prog(nc)
    import os
    cfg = {"ntok": ntok, "grp": grp, "row0": 0}
    dbg_on = os.environ.get("KDBG") == "1"
    dbgsc = nc.dram_tensor("dbgsc", [40, 128, 1024], F32).ap() if dbg_on else None
    with ExitStack() as ctx0:
        pb = [ctx0.enter_context(nc.psum_tensor(f"pb{i}", [128, 512], F32))
              for i in range(6)]
        pbbf = [ctx0.enter_context(nc.psum_tensor(f"pbbf{i}", [128, 1024], BF16))
                for i in range(2)]
        ident_sb = ctx0.enter_context(nc.sbuf_tensor("ident_sb", [128, 128], BF16))
        ident4_sb = ctx0.enter_context(nc.sbuf_tensor("ident4_sb", [128, 512], BF16))
        maskD_sb = ctx0.enter_context(nc.sbuf_tensor("maskD_sb", [128, 128], F32))
        maskOff_sb = ctx0.enter_context(nc.sbuf_tensor("maskOff_sb", [128, 128], F32))
        triu4_sb = ctx0.enter_context(nc.sbuf_tensor("triu4_sb", [128, 512], F32))
        sel_sb = ctx0.enter_context(nc.sbuf_tensor("sel_sb", [128, 8], F32))
        aps = {"x": x, "out": out, "xnsc": xnsc, "phisc": phisc, "olocsc": olocsc,
               "gsc": gsc, "ccin": ccin, "ccout": ccout,
               "wqT": d["wqT"], "wkT": d["wkT"], "wvT": d["wvT"],
               "wbeta16_dram": d["wbeta16"], "wg1T4": d["wg1T4"],
               "wg2T": d["wg2T"], "woT": d["woT"], "gn16": d["gn16"],
               "wgu4": d["wgu4"], "wdn4": d["wdn4"], "eye16": d["eye16"],
               "wqkvbf": wqkvbf,
               **({"dbgsc": dbgsc} if dbg_on else {}),
               "pb": pb, "pbbf": pbbf, "ident_sb": ident_sb,
               "ident4_sb": ident4_sb, "maskD_sb": maskD_sb,
               "maskOff_sb": maskOff_sb,
               "triu4_sb": triu4_sb, "sel_sb": sel_sb}
        cpool_sb = ctx0.enter_context(nc.sbuf_tensor("cpool_sb", [128, 3], F32))
        nc.const_aps.register(F32, 1e-6, cpool_sb[:, 0:1])
        nc.const_aps.register(F32, 1.0 / 128.0, cpool_sb[:, 1:2])
        nc.const_aps.register(F32, 1e-5, cpool_sb[:, 2:3])
        cl = p.sem("cl")
        v = p.dma("sync", cpool_sb[:, :], d["cpool"], cl)
        v = p.dma("sync", ident_sb[:, :], d["ident"], cl)
        v = p.dma("sync", ident4_sb[:, :], d["ident4"], cl)
        v = p.dma("sync", maskD_sb[:, :], d["maskD"], cl)
        v = p.dma("sync", maskOff_sb[:, :], d["maskOff"], cl)
        v = p.dma("sync", triu4_sb[:, :], d["triu4"], cl)
        v = p.dma("sync", sel_sb[:, :], d["sel"], cl)
        for e in ENGINES:
            p.wait(e, cl, v)

        with ExitStack() as c1:
            sv = sweep_phase(p, c1, aps, cfg)
            cc_v = relay_send(p, aps)
            for i in range(6):
                p.wait("sync", f"wsl_ld{i}", p.cnt[f"wsl_ld{i}"])
            p.wait("gpsimd", "spl", sv["spl"])
            for nm in ("sphi0", "sphi1", "solo0", "solo1"):
                p.wait("gpsimd", nm, sv[nm])
            for i in range(2):
                p.wait("sync", f"xch{i}", p.cnt[f"xch{i}"])
            p.barrier()
        with ExitStack() as c2:
            aps["xn_ready"] = ("spl", sv["spl"])
            fv = ffn_phase(p, c2, aps, cfg)
            for i in range(4):
                p.wait("sync", f"wgu_ld{i}", p.cnt[f"wgu_ld{i}"])
            for i in range(3):
                p.wait("sync", f"wdn_ld{i}", p.cnt[f"wdn_ld{i}"])
            for i in range(2):
                p.wait("scalar", f"fxld{i}", p.cnt[f"fxld{i}"])
            p.wait("gpsimd", "gst", fv["gst"])
            p.barrier()
        aps["S_in_bf_outer"] = ctx0.enter_context(
            nc.sbuf_tensor("S_in_bf_o", [128, 16 * 128], BF16))
        with ExitStack() as c3a:
            sin_v = compose_phase(p, c3a, aps, cc_v)
            for nm in ("cldr0", "cldr1", "cld2r0", "cld2r1"):
                p.wait("sync", nm, p.cnt[nm])
            p.barrier()
        with ExitStack() as c3:
            cfg2 = dict(cfg)
            cfg2["gst_final"] = fv["gst"]
            cfg2["spl_final"] = sv["spl"]
            for nm in ("sphi0", "sphi1", "solo0", "solo1"):
                cfg2[nm] = sv[nm]
            av = attnpost_phase(p, c3, aps, cfg2, sin_v)
            p.wait("gpsimd", "ost0", av["ost0"])
            p.wait("gpsimd", "ost1", av["ost1"])
        p.build_sems(ctx0)
        with nc.Block() as block:
            p.replay(block)
    return nc


def host_prep(inputs, ntok_core=2048):
    import ml_dtypes
    f32 = np.float32
    asc = np.ascontiguousarray
    x = np.asarray(inputs["L0_attn_x"], f32).reshape(-1, 2048)
    wq = np.asarray(inputs["L0_attn_Wq"], f32)
    wk = np.asarray(inputs["L0_attn_Wk"], f32)
    wv = np.asarray(inputs["L0_attn_Wv"], f32)
    wbeta = np.asarray(inputs["L0_attn_Wbeta"], f32)
    wg1 = np.asarray(inputs["L0_attn_Wg1"], f32)
    wg2 = np.asarray(inputs["L0_attn_Wg2"], f32)
    gn = np.asarray(inputs["L0_attn_g_norm"], f32)
    wo = np.asarray(inputs["L0_attn_Wo"], f32)
    wgu = np.asarray(inputs["L0_ffn_w_gate_up"], f32)
    wdn = np.asarray(inputs["L0_ffn_w_down"], f32)

    shared = {
        "wqT": asc(wq.T), "wkT": asc(wk.T), "wvT": asc(wv.T),
        "wbeta16": asc(wbeta.T.reshape(16, 128, 16).transpose(1, 0, 2)
                       .reshape(128, 256)),
        "wg1T4": asc(wg1.T.reshape(16, 128, 128).transpose(1, 0, 2)
                     .reshape(128, 2048)),
        "wg2T": asc(wg2.T),
        "woT": asc(wo.T),
        "gn16": asc(gn.reshape(16, 128).T),
        "wgu4": asc(wgu.T.reshape(16, 128, 128, 128).transpose(2, 1, 0, 3)),
        "wdn4": asc(wdn.T.reshape(64, 128, 16, 128).transpose(2, 1, 0, 3)),
        "eye16": asc(np.tile(np.eye(128, dtype=f32), 16).reshape(128, 4, 512)),
        "ident": np.eye(128, dtype=ml_dtypes.bfloat16),
        "ident4": asc(np.tile(np.eye(128, dtype=np.float32), 4)
                      .astype(ml_dtypes.bfloat16)),
        "maskD": asc(-np.tril(np.ones((128, 128), f32), -1)
                     * np.kron(np.eye(4, dtype=f32), np.ones((32, 32), f32))),
        "maskOff": asc(-np.tril(np.ones((128, 128), f32), -1)
                       * (1.0 - np.kron(np.eye(4, dtype=f32), np.ones((32, 32), f32)))),
        "triu4": asc(np.tile(np.triu(np.ones((128, 128), f32), 0), 4)),
        "cpool": asc(np.tile(np.array([[1e-6, 1.0 / 128.0, 1e-5]], f32), (128, 1))),
    }
    per_core = []
    ncores = x.shape[0] // ntok_core
    for c in range(ncores):
        sel = np.zeros((128, 8), f32)
        sel[:, c] = 1.0
        m = {"x": asc(x[c * ntok_core:(c + 1) * ntok_core]), "sel": sel}
        m.update(shared)
        per_core.append(m)
    return per_core



NTOK_CORE = 2048

def kernel(**inputs):
    from concourse.bass_utils import run_bass_kernel_spmd
    nc = build_kernel(NTOK_CORE, grp=512)
    per_core = host_prep(inputs, NTOK_CORE)
    res = run_bass_kernel_spmd(nc, per_core, core_ids=list(range(8)))
    out = np.concatenate([np.asarray(res.results[c]["out"]) for c in range(8)], axis=0)
    return out.reshape(1, 16384, 2048).astype(np.float32)


# version 22
# speedup vs baseline: 1.0216x; 1.0189x over previous
import numpy as np
import concourse.bass as bass
import concourse.mybir as mybir

F32 = mybir.dt.float32
BF16 = mybir.dt.bfloat16
AF = mybir.ActivationFunctionType
ALU = mybir.AluOpType
AX = mybir.AxisListType

ENGINES = ("sync", "scalar", "vector", "tensor", "gpsimd")


class Prog:
    def __init__(self, nc):
        self.nc = nc
        self.q = {e: [] for e in ENGINES}
        self.cnt = {}
        self.waited = {e: {} for e in ENGINES}
        self.handles = {}

    def sem(self, name):
        if name not in self.cnt:
            self.cnt[name] = 0
        return name

    def emit(self, eng, fn):
        self.q[eng].append(fn)

    def inc(self, name, by=1):
        self.sem(name)
        self.cnt[name] += by
        return self.cnt[name]

    def wait(self, eng, sem, val):
        if val is None or val <= 0:
            return
        w = self.waited[eng].get(sem, 0)
        if val <= w:
            return
        self.waited[eng][sem] = val
        self.emit(eng, lambda e, s=sem, v=val: e.wait_ge(self.handles[s], v))

    def dma(self, eng, out, in_, sem):
        assert tuple(out.shape) == tuple(in_.shape), (
            f"dma shape mismatch {tuple(out.shape)} vs {tuple(in_.shape)}")
        c = self.inc(sem, 16)
        self.emit(eng, lambda e, o=out, i=in_, s=sem: e.dma_start(out=o, in_=i)
                  .then_inc(self.handles[s], 16))
        return c

    def op(self, eng, fn, sem=None):
        if sem is None:
            self.emit(eng, fn)
            return None
        c = self.inc(sem, 1)
        self.emit(eng, lambda e, f=fn, s=sem: f(e).then_inc(self.handles[s], 1))
        return c

    DEFSEM = {"vector": "vs", "scalar": "ss", "gpsimd": "gs", "tensor": "ts"}

    def chain(self, eng):
        self.wait(eng, self.DEFSEM[eng], self.cnt.get(self.DEFSEM[eng], 0))

    def barrier(self):
        b = self.sem("bar")
        target = self.cnt["bar"] + len(ENGINES)
        self.cnt["bar"] = target
        for e in ENGINES:
            if e == "sync":
                self.emit(e, lambda en: en.nop().then_inc(self.handles["bar"], 1))
            else:
                self.emit(e, lambda en: en.drain().then_inc(self.handles["bar"], 1))
        for e in ENGINES:
            self.waited[e]["bar"] = target
            self.emit(e, lambda en, v=target: en.wait_ge(self.handles["bar"], v))

    def replay(self, block):
        for e in ENGINES:
            fns = self.q[e]
            getattr(block, e)(lambda eng, fns=fns: [f(eng) for f in fns])

    def build_sems(self, ctx):
        for name in self.cnt:
            self.handles[name] = ctx.enter_context(self.nc.semaphore(name))



class WStream:
    def __init__(self, p, ctx, name, shape, nbuf, dma_eng="sync",
                 cast_engines=("scalar", "vector"), rel_sem="ts"):
        self.p, self.name, self.nbuf = p, name, nbuf
        nc = p.nc
        self.f32 = [ctx.enter_context(nc.sbuf_tensor(f"{name}_f{i}", list(shape), F32))
                    for i in range(nbuf)]
        self.bf = [ctx.enter_context(nc.sbuf_tensor(f"{name}_b{i}", list(shape), BF16))
                   for i in range(nbuf)]
        self.ld = [p.sem(f"{name}_ld{i}") for i in range(nbuf)]
        self.cast_engines = cast_engines
        self.cast_sems = {e: p.sem(f"{name}_c_{e}") for e in cast_engines}
        self.dma_eng = dma_eng
        self.rel_sem = rel_sem
        self.i = 0
        self.tok = {}
        self.rel = {}

    def load(self, in_ap):
        p, i = self.p, self.i
        b = i % self.nbuf
        old = i - self.nbuf
        if old >= 0:
            assert old in self.rel, f"{self.name}: rel({old}) not set before load({i})"
            p.wait(self.dma_eng, self.rel_sem, self.rel[old])
        out_ap = self.f32[b][:, :]
        if len(in_ap.shape) == 3:
            out_ap = out_ap.rearrange("p (k c) -> p k c", k=in_ap.shape[1])
        ldv = p.dma(self.dma_eng, out_ap, in_ap, self.ld[b])
        ce = self.cast_engines[i % len(self.cast_engines)]
        p.wait(ce, self.ld[b], ldv)
        cs = self.cast_sems[ce]
        cv = p.inc(cs, 1)
        if ce == "scalar":
            p.emit(ce, lambda e, b=b, s=cs: e.copy(out=self.bf[b][:, :],
                                                   in_=self.f32[b][:, :])
                   .then_inc(p.handles[s], 1))
        else:
            p.emit(ce, lambda e, b=b, s=cs: e.tensor_copy(out=self.bf[b][:, :],
                                                          in_=self.f32[b][:, :])
                   .then_inc(p.handles[s], 1))
        self.tok[i] = (b, ce, cv)
        self.i += 1
        return i

    def wait_bf(self, eng, tok):
        b, ce, cv = self.tok[tok]
        self.p.wait(eng, self.cast_sems[ce], cv)
        return self.bf[b]

    def set_rel(self, tok, val):
        assert val is not None
        self.rel[tok] = val


class BfRing:
    def __init__(self, p, ctx, name, shape, nbuf, dma_eng="sync"):
        self.p, self.name, self.nbuf = p, name, nbuf
        nc = p.nc
        self.bf = [ctx.enter_context(nc.sbuf_tensor(f"{name}_b{i}", list(shape), BF16))
                   for i in range(nbuf)]
        self.ld = [p.sem(f"{name}_ld{i}") for i in range(nbuf)]
        self.dma_eng = dma_eng
        self.i = 0
        self.tok = {}
        self.rel = {}
        self.rel_sem = "ts"

    def load(self, in_ap):
        p, i = self.p, self.i
        b = i % self.nbuf
        old = i - self.nbuf
        if old >= 0:
            p.wait(self.dma_eng, self.rel_sem, self.rel[old])
        out_ap = self.bf[b][:, :]
        if len(in_ap.shape) == 3:
            out_ap = out_ap.rearrange("p (k c) -> p k c", k=in_ap.shape[1])
        ldv = p.dma(self.dma_eng, out_ap, in_ap, self.ld[b])
        self.tok[i] = (b, self.ld[b], ldv)
        self.i += 1
        return i

    def wait_bf(self, eng, tok):
        b, sem, v = self.tok[tok]
        self.p.wait(eng, sem, v)
        return self.bf[b]

    def set_rel(self, tok, val):
        self.rel[tok] = val



def ffn_phase(p, ctx, aps, cfg):
    nc = p.nc
    NTOK, GRP = cfg["ntok"], cfg["grp"]
    ngrp = NTOK // GRP
    TCH = GRP // 128
    NI, KD = 64, 16
    ts, ss, vs = p.sem("ts"), p.sem("ss"), p.sem("vs")
    gst = p.sem("gst")
    xld2 = [p.sem(f"fxld{i}") for i in range(2)]

    xn_g = ctx.enter_context(nc.sbuf_tensor("xn_g", [128, KD * GRP], BF16))
    h = ctx.enter_context(nc.sbuf_tensor("h", [128, NI * GRP], BF16))
    sg = [ctx.enter_context(nc.sbuf_tensor(f"sg{j}", [128, GRP], BF16)) for j in range(2)]
    gfm = [ctx.enter_context(nc.sbuf_tensor(f"gfm{j}", [128, GRP], BF16)) for j in range(2)]
    g_tm = ctx.enter_context(nc.sbuf_tensor("g_tm", [128, TCH * 2048], BF16))

    st_gu = WStream(p, ctx, "wgu", (128, KD * 128), nbuf=4)
    st_dn = WStream(p, ctx, "wdn", (128, 16 * 128), nbuf=4)

    pb, pbbf, ident = aps["pb"], aps["pbbf"], aps["ident_sb"]
    wgu4, wdn4, xnsc, gsc = aps["wgu4"], aps["wdn4"], aps["xnsc"], aps["gsc"]

    LA = 3
    LA_DN = LA
    assert LA < st_gu.nbuf and LA_DN < st_dn.nbuf

    silu_v = {}
    hmul_v = {}
    dsilu_v = {}
    dlast_v = {}
    tp_v = {}
    ev_v = {}
    gu_last = 0
    gu_last2 = {}
    dn_last = 0
    gst_prev = 0

    for g in range(ngrp):
        if "xn_ready" in aps:
            p.wait("scalar", aps["xn_ready"][0], aps["xn_ready"][1])
        p.wait("scalar", ts, gu_last)
        xv = p.dma("scalar", xn_g[:, :].rearrange("p (k t) -> p k t", k=KD),
                   xnsc[:, :, g * GRP:(g + 1) * GRP].rearrange("k p t -> p k t"),
                   xld2[g % 2])
        p.wait("tensor", xld2[g % 2], xv)

        toks = {}
        def issue_gu(j):
            if 0 <= j < 2 * NI:
                i, is_up = j // 2, j % 2
                toks[j] = st_gu.load(wgu4[64 + i] if is_up else wgu4[i])
        for j in range(LA):
            issue_gu(j)
        p.wait("vector", ts, dn_last)
        for i in range(NI):
            for half in (0, 1):
                j = 2 * i + half
                issue_gu(j + LA)
                bank = pb[2 * half + (i % 2)]
                buf = st_gu.wait_bf("tensor", toks[j])
                if i >= 2:
                    p.wait("tensor", ss if half == 0 else vs,
                           silu_v[g, i - 2] if half == 0 else hmul_v[g, i - 2])
                v = None
                for k in range(KD):
                    last = (k == KD - 1)
                    v = p.op("tensor",
                             lambda e, b=buf, k=k, bank=bank, last=last:
                             e.matmul(bank[:, :GRP], b[:, k * 128:(k + 1) * 128],
                                      xn_g[:, k * GRP:(k + 1) * GRP],
                                      start=(k == 0), stop=last),
                             sem=(ts if last else None)) or v
                st_gu.set_rel(toks[j], v)
            gu_last = v
            gu_last2[g] = v
            p.wait("scalar", ts, st_gu.rel[toks[2 * i]])
            if i >= 2:
                p.wait("scalar", vs, hmul_v[g, i - 2])
            silu_v[g, i] = p.op("scalar", lambda e, i=i: e.activation(
                out=sg[i % 2][:, :], in_=pb[0 + (i % 2)][:, :GRP], func=AF.Silu), sem=ss)
            p.wait("vector", ts, st_gu.rel[toks[2 * i + 1]])
            p.wait("vector", ss, silu_v[g, i])
            hmul_v[g, i] = p.op("vector", lambda e, i=i: e.tensor_mul(
                out=h[:, i * GRP:(i + 1) * GRP], in0=sg[i % 2][:, :],
                in1=pb[2 + (i % 2)][:, :GRP]), sem=vs)

        dtoks = {}
        def issue_dn(jj):
            if 0 <= jj < 16 * 4:
                d, qtr = jj // 4, jj % 4
                dtoks[jj] = st_dn.load(wdn4[d, :, qtr * 16:(qtr + 1) * 16, :])
        for jj in range(LA_DN):
            issue_dn(jj)

        def transpose_batch(d):
            p.wait("tensor", ss, dsilu_v[g, d])
            if (g, d - 2) in ev_v:
                p.wait("tensor", vs, ev_v[g, d - 2])
            v = None
            for t in range(TCH):
                v = p.op("tensor", lambda e, d=d, t=t: e.transpose(
                    pbbf[d % 2][:, t * 128:(t + 1) * 128],
                    gfm[d % 2][:, t * 128:(t + 1) * 128], ident[:, :]),
                    sem=(ts if t == TCH - 1 else None)) or v
            tp_v[g, d] = v
            p.wait("vector", ts, v)
            p.wait("vector", gst, gst_prev)
            out_ap = g_tm[:, :].rearrange("p (t x) -> p t x", t=TCH)[:, :, d * 128:(d + 1) * 128]
            in_ap = pbbf[d % 2][:, :TCH * 128].rearrange("p (t c) -> p t c", t=TCH)
            ev_v[g, d] = p.op("vector", lambda e, o=out_ap, i=in_ap:
                              e.tensor_copy(out=o, in_=i), sem=vs)

        p.wait("tensor", vs, hmul_v[g, NI - 1])
        for d in range(16):
            bank = pb[4 + (d % 2)]
            if d >= 2:
                p.wait("tensor", ss, dsilu_v[g, d - 2])
            v = None
            for qtr in range(4):
                jj = 4 * d + qtr
                issue_dn(jj + LA_DN)
                buf = st_dn.wait_bf("tensor", dtoks[jj])
                for kk in range(16):
                    i = qtr * 16 + kk
                    v = p.op("tensor",
                             lambda e, b=buf, kk=kk, i=i, bank=bank:
                             e.matmul(bank[:, :GRP], b[:, kk * 128:(kk + 1) * 128],
                                      h[:, i * GRP:(i + 1) * GRP],
                                      start=(i == 0), stop=(i == NI - 1)),
                             sem=(ts if kk == 15 else None)) or v
                st_dn.set_rel(dtoks[jj], v)
            dlast_v[g, d] = v
            p.wait("scalar", ts, v)
            if (g, d - 2) in tp_v:
                p.wait("scalar", ts, tp_v[g, d - 2])
            dsilu_v[g, d] = p.op("scalar", lambda e, d=d: e.activation(
                out=gfm[d % 2][:, :], in_=pb[4 + (d % 2)][:, :GRP], func=AF.Silu), sem=ss)
            if d >= 1:
                transpose_batch(d - 1)
        transpose_batch(15)
        dn_last = dlast_v[g, 15]

        for t in range(TCH):
            p.wait("gpsimd", vs, ev_v[g, 15])
            row0 = cfg["row0"] + g * GRP + t * 128
            gst_prev = p.dma("gpsimd", gsc[row0:row0 + 128, :],
                             g_tm[:, t * 2048:(t + 1) * 2048], gst)

    return {"gst": p.cnt["gst"], "ts": p.cnt["ts"]}



class PairPsum:
    def __init__(self, p, banks):
        self.p = p
        self.pairs = [(banks[0], banks[1]), (banks[2], banks[3])]
        self.j = 0
        self.ev = {}

    def next(self):
        i = self.j % 2
        self.j += 1
        for sem, val in self.ev.get(i, ()):
            self.p.wait("tensor", sem, val)
        return i

    def set_ev(self, i, evs):
        self.ev[i] = evs


def sweep_phase(p, ctx, aps, cfg):
    nc = p.nc
    NTOK, GRP = cfg["ntok"], cfg["grp"]
    ngrp, TCH, KD, NH = NTOK // GRP, GRP // 128, 16, 16
    ts, ss, vs, gs = p.sem("ts"), p.sem("ss"), p.sem("vs"), p.sem("gs")
    spl = p.sem("spl")
    sphi, solo = p.sem("sphi"), p.sem("solo")

    x_ap, wqT, wkT, wvT = aps["x"], aps["wqT"], aps["wkT"], aps["wvT"]
    xnsc, phisc, olocsc = aps["xnsc"], aps["phisc"], aps["olocsc"]
    pb, pbbf = aps["pb"], aps["pbbf"]
    ident, ident4 = aps["ident_sb"], aps["ident4_sb"]
    triu4, eye16 = aps["triu4_sb"], aps["eye16"]
    maskD, maskOff = aps["maskD_sb"], aps["maskOff_sb"]

    G = ctx.enter_context(nc.sbuf_tensor("G", [128, NH * 128], F32))
    Hs = ctx.enter_context(nc.sbuf_tensor("Hs", [128, NH * 128], F32))
    Gt = ctx.enter_context(nc.sbuf_tensor("Gt", [128, NH * 128], F32))
    G_bf = ctx.enter_context(nc.sbuf_tensor("G_bf", [128, NH * 128], BF16))
    H_bf = ctx.enter_context(nc.sbuf_tensor("H_bf", [128, NH * 128], BF16))
    aps["G"], aps["Hs"], aps["Gt"] = G, Hs, Gt

    xb = [ctx.enter_context(nc.sbuf_tensor(f"xb{i}", [128, 2048], F32)) for i in range(2)]
    xn_tm = [ctx.enter_context(nc.sbuf_tensor(f"xntm{i}", [128, 2048], BF16)) for i in range(2)]
    xn_fm = ctx.enter_context(nc.sbuf_tensor("xn_fm", [128, KD * GRP], BF16))
    q_fm = ctx.enter_context(nc.sbuf_tensor("q_fm", [128, 16 * GRP], BF16))
    k_tm = ctx.enter_context(nc.sbuf_tensor("k_tm", [128, TCH * 2048], BF16))
    v_tm = ctx.enter_context(nc.sbuf_tensor("v_tm", [128, TCH * 2048], BF16))
    ksl = [ctx.enter_context(nc.sbuf_tensor(f"ksl{i}", [128, 512], BF16)) for i in range(4)]
    ssq = ctx.enter_context(nc.sbuf_tensor("ssq", [128, 64], F32))
    junk = ctx.enter_context(nc.sbuf_tensor("junk", [128, 512], BF16))
    lnst = ctx.enter_context(nc.sbuf_tensor("lnst", [128, 8], F32))
    beta_g = ctx.enter_context(nc.sbuf_tensor("beta_g", [128, TCH * 16], F32))
    wb_sb = ctx.enter_context(nc.sbuf_tensor("wb_sb", [128, 256], BF16))

    def wb(name, dt=BF16):
        return ctx.enter_context(nc.sbuf_tensor(name, [128, 1024], dt))
    WB = []
    for wv in range(2):
        B = {}
        for nm in ("Kf", "NT", "Da", "Dat", "No", "C0K", "C0V",
                   "Pb", "Pbt", "X0", "BK", "BV", "Wp", "U0"):
            B[nm] = wb(f"{nm}_{wv}")
        B["R"] = wb(f"R_{wv}", F32)
        B["Zt"] = B["BK"]
        B["Mk"] = B["BV"]
        B["Qe"] = B["Da"]
        B["phi"] = B["Dat"]
        B["ol"] = B["X0"]
        WB.append(B)
    spill_g = {"phi": {}, "oloc": {}}
    wave_last_ts = {}

    st_w = BfRing(p, ctx, "wsl", (128, 512), nbuf=6)

    dbg_slot = [0]
    def dbg(buf, width=1024):
        if "dbgsc" not in aps:
            return
        i = dbg_slot[0]
        dbg_slot[0] += 1
        for e in ("ss", "vs", "gs", "ts"):
            p.wait("gpsimd", e, p.cnt[e])
        dv = p.dma("gpsimd", aps["dbgsc"][i][:, :width].rearrange("p (a n) -> p a n", n=512 if width >= 512 else width),
              buf[:, :width].rearrange("p (a n) -> p a n", n=512 if width >= 512 else width),
              p.sem("dbgs"))
        for e in ENGINES:
            p.wait(e, "dbgs", dv)

    ild = p.sem("ild")
    e1 = p.dma("gpsimd", G[:, :].rearrange("p (a b) -> p a b", b=512), eye16, ild)
    e2 = p.dma("gpsimd", Gt[:, :].rearrange("p (a b) -> p a b", b=512), eye16, ild)
    b1 = p.dma("gpsimd", xb[0][:, :256], aps["wbeta16_dram"], ild)
    p.wait("vector", ild, b1)
    p.op("vector", lambda e: e.memset(Hs[:, :], 0.0))
    p.op("vector", lambda e: e.memset(H_bf[:, :], 0.0))
    iv = p.op("vector", lambda e: e.tensor_copy(out=G_bf[:, :], in_=G[:, :]), sem=vs)
    p.wait("scalar", ild, b1)
    is_ = p.op("scalar", lambda e: e.copy(out=wb_sb[:, :], in_=xb[0][:, :256]), sem=ss)
    p.wait("tensor", vs, iv)
    p.wait("tensor", ss, is_)
    p.wait("gpsimd", vs, iv)

    wpst = [p.sem(f"wpst{i}") for i in range(2)]
    wpld = [p.sem(f"wpld{i}") for i in range(2)]
    prep_tiles = [(wi, r) for wi in range(3) for r in range(16)]
    wTs = [wqT, wkT, wvT]
    p.wait("scalar", ss, p.cnt[ss])
    prep_last_cast = {}
    for i, (wi, r) in enumerate(prep_tiles):
        bb = i % 2
        if i >= 2:
            p.wait("scalar", wpst[bb], 16 * (i // 2))
        lv = p.dma("scalar", xb[bb][:, :].rearrange("p (a n) -> p a n", n=512),
                   wTs[wi][r * 128:(r + 1) * 128, :]
                   .rearrange("p (a n) -> p a n", n=512), wpld[bb])
        ce = "vector" if i % 2 else "scalar"
        p.wait(ce, wpld[bb], lv)
        if i >= 2:
            p.wait(ce, wpst[bb], 16 * (i // 2))
        sem = vs if ce == "vector" else ss
        if ce == "vector":
            cv2 = p.op(ce, lambda e, bb=bb: e.tensor_copy(
                out=xn_tm[bb][:, :], in_=xb[bb][:, :]), sem=sem)
        else:
            cv2 = p.op(ce, lambda e, bb=bb: e.copy(
                out=xn_tm[bb][:, :], in_=xb[bb][:, :]), sem=sem)
        prep_last_cast[bb] = (sem, cv2)
        p.wait("gpsimd", sem, cv2)
        p.dma("gpsimd", aps["wqkvbf"][wi][r * 128:(r + 1) * 128, :]
              .rearrange("p (a n) -> p a n", n=512),
              xn_tm[bb][:, :].rearrange("p (a n) -> p a n", n=512), wpst[bb])
    wpst_fin = [p.cnt["wpst0"], p.cnt["wpst1"]]

    xld = [p.sem(f"xch{i}") for i in range(2)]
    p.wait("sync", ss, p.cnt[ss])
    ln_v = {}
    tpev = {}
    pbbf_state = {"i": 0, "ev": {}, }

    def transpose_batch4(srcs, out_ap, eng):
        i = pbbf_state["i"] % 2
        pbbf_state["i"] += 1
        bank = pbbf[i]
        pe = pbbf_state["ev"].get(i)
        if pe is not None:
            p.wait("tensor", pe[0], pe[1])
        v = None
        for t, s in enumerate(srcs):
            v = p.op("tensor", lambda e, s=s, t=t, bank=bank: e.transpose(
                bank[:, t * 128:(t + 1) * 128], s, ident[:, :]),
                sem=(ts if t == len(srcs) - 1 else None)) or v
        p.wait(eng, ts, v)
        sem = ss if eng == "scalar" else vs
        if eng == "scalar":
            ev = p.op(eng, lambda e, o=out_ap, bank=bank: e.copy(
                out=o, in_=bank[:, :512]), sem=sem)
        else:
            ev = p.op(eng, lambda e, o=out_ap, bank=bank: e.tensor_copy(
                out=o, in_=bank[:, :512]), sem=sem)
        pbbf_state["ev"][i] = (sem, ev)
        return sem, ev

    pp = PairPsum(p, [pb[0][:, :], pb[1][:, :], pb[2][:, :], pb[3][:, :]])

    def slot(pair_i, h):
        bank = pp.pairs[pair_i][h // 4]
        return bank[:, (h % 4) * 128:(h % 4 + 1) * 128]

    def mms8(ops, accs=None):
        i = pp.next()
        v = None
        for h, (l, r) in enumerate(ops):
            if accs is None:
                v = p.op("tensor", lambda e, s=slot(i, h), l=l, r=r:
                         e.matmul(s, l, r, start=True, stop=True),
                         sem=(ts if h == 7 else None)) or v
            else:
                p.op("tensor", lambda e, s=slot(i, h), l=l, r=r:
                     e.matmul(s, l, r, start=True, stop=False))
                v = p.op("tensor", lambda e, s=slot(i, h), l=accs[h][0], r=accs[h][1]:
                         e.matmul(s, l, r, start=False, stop=True),
                         sem=(ts if h == 7 else None)) or v
        return i, v

    def evac8(pair_i, mm_v, outs, engs=("vector", "scalar"), fns=(None, None)):
        res = []
        for b in range(2):
            eng = engs[b]
            sem = vs if eng == "vector" else ss
            p.wait(eng, ts, mm_v)
            bank = pp.pairs[pair_i][b]
            fn = fns[b]
            if fn is None:
                if eng == "scalar":
                    val = p.op(eng, lambda e, o=outs[b], i_=bank: e.copy(
                        out=o, in_=i_[:, :]), sem=sem)
                else:
                    val = p.op(eng, lambda e, o=outs[b], i_=bank: e.tensor_copy(
                        out=o, in_=i_[:, :]), sem=sem)
            else:
                val = p.op(eng, lambda e, o=outs[b], i_=bank, f=fn: f(e, o, i_),
                           sem=sem)
            res.append((sem, val))
        pp.set_ev(pair_i, res)
        return res

    ev_q, ev_k, ev_v = {}, {}, {}
    beta_v = 0
    spill_seq = [0]

    spl_group = 0
    phi_spl = 0
    oloc_spl = 0
    for g in range(ngrp):
        p.wait("tensor", spl, spl_group)
        for c in range(TCH):
            gc = g * TCH + c
            b = gc % 2
            p.wait("sync", vs, ln_v.get(gc - 2, 0))
            if gc < 2:
                sem_c, v_c = prep_last_cast[b]
                p.wait("sync", sem_c, v_c)
                p.wait("scalar", wpst[b], wpst_fin[b])
                p.wait("vector", wpst[b], wpst_fin[b])
            xv = p.dma("sync", xb[b][:, :].rearrange("p (a n) -> p a n", n=512),
                       x_ap[gc * 128:(gc + 1) * 128, :]
                       .rearrange("p (a n) -> p a n", n=512), xld[b])
            p.wait("vector", xld[b], xv)
            p.wait("scalar", xld[b], xv)
            s1v = p.op("vector", lambda e, b=b: e.reduce_sum(
                out=lnst[:, 0:1], in_=xb[b][:, :], axis=AX.X), sem=vs)
            s2v = p.op("scalar", lambda e, b=b: e.activation(
                out=xn_tm[b][:, :], in_=xb[b][:, :], func=AF.Square,
                accum_out=lnst[:, 1:2]), sem=ss)
            p.wait("vector", ss, s2v)
            p.chain("vector")
            p.op("vector", lambda e: e.tensor_scalar_mul(
                out=lnst[:, 2:3], in0=lnst[:, 0:1], scalar1=1.0 / 2048.0), sem=vs)
            p.chain("vector")
            p.op("vector", lambda e: e.tensor_mul(
                out=lnst[:, 3:4], in0=lnst[:, 2:3], in1=lnst[:, 2:3]), sem=vs)
            p.op("vector", lambda e: e.tensor_scalar_mul(
                out=lnst[:, 4:5], in0=lnst[:, 1:2], scalar1=1.0 / 2048.0), sem=vs)
            p.chain("vector")
            vv2 = p.op("vector", lambda e: e.tensor_sub(
                out=lnst[:, 4:5], in0=lnst[:, 4:5], in1=lnst[:, 3:4]), sem=vs)
            p.wait("scalar", vs, vv2)
            sq2 = p.op("scalar", lambda e: e.activation(
                out=lnst[:, 5:6], in_=lnst[:, 4:5], func=AF.Sqrt, bias=1e-5), sem=ss)
            p.wait("vector", ss, sq2)
            rc2 = p.op("vector", lambda e: e.reciprocal(
                out=lnst[:, 6:7], in_=lnst[:, 5:6]), sem=vs)
            p.chain("vector")
            rc2 = p.op("vector", lambda e: e.scalar_tensor_tensor(
                out=lnst[:, 7:8], in0=lnst[:, 2:3], scalar=-1.0,
                in1=lnst[:, 6:7], op0=ALU.mult, op1=ALU.mult), sem=vs)
            tp = tpev.get(gc - 2)
            if tp is not None:
                p.wait("vector", tp[0], tp[1])
            p.chain("vector")
            ln_v[gc] = p.op("vector", lambda e, b=b: e.tensor_scalar(
                out=xn_tm[b][:, :], in0=xb[b][:, :], scalar1=lnst[:, 6:7],
                scalar2=lnst[:, 7:8], op0=ALU.mult, op1=ALU.add), sem=vs)
            p.wait("tensor", vs, ln_v[gc])
            for kb in range(4):
                out_ap = (xn_fm[:, :].rearrange("p (k t) -> p k t", k=KD)
                          [:, kb * 4:(kb + 1) * 4, c * 128:(c + 1) * 128])
                tpev[gc] = transpose_batch4(
                    [xn_tm[b][:, (kb * 4 + j) * 128:(kb * 4 + j + 1) * 128]
                     for j in range(4)],
                    out_ap, "scalar" if kb % 2 else "vector")
        p.wait("gpsimd", ss, p.cnt[ss])
        p.wait("gpsimd", vs, p.cnt[vs])
        nsp = min(512, GRP)
        for k in range(KD):
            p.dma("gpsimd", xnsc[k][:, g * GRP:(g + 1) * GRP]
                  .rearrange("p (a n) -> p a n", n=nsp),
                  xn_fm[:, k * GRP:(k + 1) * GRP]
                  .rearrange("p (a n) -> p a n", n=nsp), spl)
        spl_group = p.cnt[spl]

        p.wait("tensor", ss, p.cnt[ss])
        p.wait("tensor", vs, p.cnt[vs])
        bank_reader = {}
        tiles = [(s, wn, kk) for s in range(4) for wn in ("q", "k", "v")
                 for kk in range(KD)]
        toks = {}
        LA_C = 4

        def issue_c(i):
            if 0 <= i < len(tiles):
                s_, wn, kk = tiles[i]
                wi = {"q": 0, "k": 1, "v": 2}[wn]
                pi = wi * 16 + kk
                p.wait("sync", wpst[pi % 2], 16 * (pi // 2 + 1))
                toks[i] = st_w.load(aps["wqkvbf"][wi][kk * 128:(kk + 1) * 128,
                                                      s_ * 512:(s_ + 1) * 512])

        for i in range(LA_C):
            issue_c(i)
        for i, (s, wn, kk) in enumerate(tiles):
            issue_c(i + LA_C)
            buf = st_w.wait_bf("tensor", toks[i])
            if kk == 0:
                for t in range(TCH):
                    br = bank_reader.get(t)
                    if br:
                        p.wait("tensor", br[0], br[1])
            v = None
            if wn == "q":
                for j in range(4):
                    v = p.op("tensor", lambda e, kk=kk, j=j, buf=buf:
                             e.matmul(pb[j][:, :GRP],
                                      buf[:, j * 128:(j + 1) * 128],
                                      xn_fm[:, kk * GRP:(kk + 1) * GRP],
                                      start=(kk == 0), stop=(kk == KD - 1)),
                             sem=(ts if j == 3 else None)) or v
            else:
                for t in range(TCH):
                    v = p.op("tensor", lambda e, kk=kk, t=t, buf=buf:
                             e.matmul(pb[t][:, :],
                                      xn_fm[:, kk * GRP + t * 128:kk * GRP + (t + 1) * 128],
                                      buf[:, :],
                                      start=(kk == 0), stop=(kk == KD - 1)),
                             sem=(ts if t == TCH - 1 else None)) or v
            st_w.set_rel(toks[i], v)
            if kk != KD - 1:
                continue
            mm_done = v
            for t in range(TCH):
                if wn == "q":
                    p.wait("scalar", ts, mm_done)
                    h_ = s * 4 + t
                    ev_q[(s, t)] = p.op("scalar", lambda e, h_=h_, t=t: e.activation(
                        out=q_fm[:, h_ * GRP:(h_ + 1) * GRP],
                        in_=pb[t][:, :GRP], func=AF.Silu), sem=ss)
                    bank_reader[t] = (ss, ev_q[(s, t)])
                elif wn == "v":
                    p.wait("vector", ts, mm_done)
                    ev_v[(s, t)] = p.op("vector", lambda e, s=s, t=t: e.tensor_copy(
                        out=v_tm[:, t * 2048 + s * 512:t * 2048 + (s + 1) * 512],
                        in_=pb[t][:, :]), sem=vs)
                    bank_reader[t] = (vs, ev_v[(s, t)])
                else:
                    p.wait("scalar", ts, mm_done)
                    pv = ev_k.get((s - 1, t))
                    if pv is not None:
                        p.wait("scalar", vs, pv)
                    sl_v = p.op("scalar", lambda e, t=t: e.activation(
                        out=ksl[t][:, :], in_=pb[t][:, :], func=AF.Silu), sem=ss)
                    bank_reader[t] = (ss, sl_v)
                    p.wait("vector", ss, sl_v)
                    c0_ = t * 16 + s * 4
                    sq_v = None
                    for hh in range(4):
                        sq_v = p.op("vector", lambda e, t=t, hh=hh, c0_=c0_:
                                    e.scalar_tensor_tensor(
                            out=junk[:, hh * 128:(hh + 1) * 128],
                            in0=ksl[t][:, hh * 128:(hh + 1) * 128],
                            scalar=1.0,
                            in1=ksl[t][:, hh * 128:(hh + 1) * 128],
                            op0=ALU.mult, op1=ALU.mult,
                            accum_out=ssq[:, c0_ + hh:c0_ + hh + 1]), sem=vs)
                    p.wait("scalar", vs, sq_v)
                    sq_s = p.op("scalar", lambda e, c0_=c0_: e.activation(
                        out=ssq[:, c0_:c0_ + 4], in_=ssq[:, c0_:c0_ + 4],
                        func=AF.Sqrt, bias=1e-6), sem=ss)
                    p.wait("vector", ss, sq_s)
                    p.op("vector", lambda e, c0_=c0_: e.reciprocal(
                        out=ssq[:, c0_:c0_ + 4], in_=ssq[:, c0_:c0_ + 4]), sem=vs)
                    p.chain("vector")
                    nv = None
                    for hh in range(4):
                        nv = p.op("vector", lambda e, t=t, s=s, hh=hh, c0_=c0_:
                                  e.tensor_scalar(
                            out=k_tm[:, t * 2048 + (s * 4 + hh) * 128:
                                     t * 2048 + (s * 4 + hh + 1) * 128],
                            in0=ksl[t][:, hh * 128:(hh + 1) * 128],
                            scalar1=ssq[:, c0_ + hh:c0_ + hh + 1],
                            scalar2=None, op0=ALU.mult), sem=vs)
                    ev_k[(s, t)] = nv
        bv = None
        for t in range(TCH):
            for kk in range(KD):
                bv = p.op("tensor", lambda e, t=t, kk=kk: e.matmul(
                    pb[4][:, t * 16:(t + 1) * 16],
                    xn_fm[:, kk * GRP + t * 128:kk * GRP + (t + 1) * 128],
                    wb_sb[:, kk * 16:(kk + 1) * 16],
                    start=(kk == 0), stop=(kk == KD - 1)),
                    sem=(ts if (t == TCH - 1 and kk == KD - 1) else None)) or bv
        p.wait("scalar", ts, bv)
        sg_v = p.op("scalar", lambda e: e.activation(
            out=beta_g[:, :], in_=pb[4][:, :TCH * 16], func=AF.Sigmoid), sem=ss)
        p.wait("vector", ss, sg_v)
        beta_v = p.op("vector", lambda e: e.tensor_scalar_mul(
            out=beta_g[:, :], in0=beta_g[:, :], scalar1=2.0), sem=vs)

        p.wait("tensor", ss, p.cnt[ss])
        p.wait("tensor", vs, p.cnt[vs])

        def wave_gen(c, w, B, pair):
            gc_ = g * TCH + c
            H0 = 8 * w
            kc = lambda h: k_tm[:, c * 2048 + (H0 + h) * 128:c * 2048 + (H0 + h + 1) * 128]
            vc = lambda h: v_tm[:, c * 2048 + (H0 + h) * 128:c * 2048 + (H0 + h + 1) * 128]
            qf = lambda h: q_fm[:, (H0 + h) * GRP + c * 128:(H0 + h) * GRP + (c + 1) * 128]
            bcol = lambda h: beta_g[:, c * 16 + H0 + h:c * 16 + H0 + h + 1]
            gG = lambda buf, h: buf[:, (H0 + h) * 128:(H0 + h + 1) * 128]
            hs = lambda buf, h: buf[:, h * 128:(h + 1) * 128]
            half2 = lambda buf: (buf[:, :512], buf[:, 512:])

            pend = {"ev": ()}

            def mms8p(ops, accs=None):
                for sem, val in pend["ev"]:
                    p.wait("tensor", sem, val)
                v = None
                for h, (l, r) in enumerate(ops):
                    s_ = pair[h // 4][:, (h % 4) * 128:(h % 4 + 1) * 128]
                    if accs is None:
                        v = p.op("tensor", lambda e, s=s_, l=l, r=r:
                                 e.matmul(s, l, r, start=True, stop=True),
                                 sem=(ts if h == 7 else None)) or v
                    else:
                        p.op("tensor", lambda e, s=s_, l=l, r=r:
                             e.matmul(s, l, r, start=True, stop=False))
                        v = p.op("tensor", lambda e, s=s_, l=accs[h][0], r=accs[h][1]:
                                 e.matmul(s, l, r, start=False, stop=True),
                                 sem=(ts if h == 7 else None)) or v
                return v

            def evac8p(mm_v, outs, engs=("vector", "scalar"), fns=(None, None)):
                res = []
                for b_ in range(2):
                    eng = engs[b_]
                    sem = vs if eng == "vector" else ss
                    p.wait(eng, ts, mm_v)
                    bank = pair[b_]
                    fn = fns[b_]
                    if fn is None:
                        if eng == "scalar":
                            val = p.op(eng, lambda e, o=outs[b_], i_=bank: e.copy(
                                out=o, in_=i_[:, :]), sem=sem)
                        else:
                            val = p.op(eng, lambda e, o=outs[b_], i_=bank: e.tensor_copy(
                                out=o, in_=i_[:, :]), sem=sem)
                    else:
                        val = p.op(eng, lambda e, o=outs[b_], i_=bank, f=fn: f(e, o, i_),
                                   sem=sem)
                    res.append((sem, val))
                pend["ev"] = tuple(res)
                return res

            p.wait("gpsimd", ts, wave_last_ts.get(w, 0))
            p.wait("gpsimd", vs, max(ev_k[(s2, c)] for s2 in range(4)))
            p.wait("gpsimd", vs, max(ev_v[(s2, c)] for s2 in range(4)))
            p.wait("gpsimd", vs, beta_v)
            p.wait("scalar", f"sphi{w}", spill_g["phi"].get(w, 0))
            p.wait("vector", f"sphi{w}", spill_g["phi"].get(w, 0))
            p.wait("scalar", f"solo{w}", spill_g["oloc"].get(w, 0))
            p.wait("vector", f"solo{w}", spill_g["oloc"].get(w, 0))
            gv_ = None
            for h in range(8):
                p.op("gpsimd", lambda e, o=B["BK"][:, h * 128:(h + 1) * 128],
                     i_=kc(h), sc=bcol(h):
                     e.tensor_scalar(out=o, in0=i_, scalar1=sc,
                                     scalar2=None, op0=ALU.mult), sem=gs)
                gv_ = p.op("gpsimd", lambda e, o=B["BV"][:, h * 128:(h + 1) * 128],
                           i_=vc(h), sc=bcol(h):
                           e.tensor_scalar(out=o, in0=i_, scalar1=sc,
                                           scalar2=None, op0=ALU.mult), sem=gs)
            bkbv = ((gs, gv_),)
            yield

            tev = []
            for half in range(2):
                tev.append(transpose_batch4([kc(half * 4 + j) for j in range(4)],
                                            B["Kf"][:, half * 512:(half + 1) * 512],
                                            "scalar"))
            yield
            for sem, val in tev:
                p.wait("tensor", sem, val)
            v1 = mms8p([(hs(B["Kf"], h), hs(B["Kf"], h)) for h in range(8)])
            p.wait("vector", ts, v1)
            dv = None
            for h in range(8):
                i_ = pair[h // 4][:, (h % 4) * 128:(h % 4 + 1) * 128]
                p.op("vector", lambda e, o=hs(B["Da"], h), i_=i_,
                     sc=bcol(h): e.scalar_tensor_tensor(
                    out=o, in0=i_, scalar=sc,
                    in1=maskD[:, :], op0=ALU.mult, op1=ALU.mult), sem=vs)
                dv = p.op("vector", lambda e, o=hs(B["No"], h), i_=i_,
                          sc=bcol(h): e.scalar_tensor_tensor(
                    out=o, in0=i_, scalar=sc,
                    in1=maskOff[:, :], op0=ALU.mult, op1=ALU.mult), sem=vs)
            pend["ev"] = ((vs, dv),)
            yield
            x0 = None
            p.wait("tensor", vs, dv)
            dat_ev = []
            for half in range(2):
                sem, ev = transpose_batch4([hs(B["Da"], half * 4 + j) for j in range(4)],
                                           B["Dat"][:, half * 512:(half + 1) * 512],
                                           "scalar")
                dat_ev.append((sem, ev))
                p.wait("vector", sem, ev)
                x0 = p.op("vector", lambda e, half=half: e.tensor_add(
                    out=B["X0"][:, half * 512:(half + 1) * 512],
                    in0=B["Dat"][:, half * 512:(half + 1) * 512],
                    in1=ident4[:, :]), sem=vs)
            yield
            X = B["X0"]
            Pc, Pct, Pn, Pnt = B["Da"], B["Dat"], B["Pb"], B["Pbt"]
            xready = ((vs, x0),)
            for sem, val in dat_ev:
                p.wait("tensor", sem, val)
            prev_ept = None
            for lev in range(1, 5):
                if prev_ept:
                    for sem, val in prev_ept:
                        p.wait("tensor", sem, val)
                vp = mms8p([(hs(Pct, h), hs(Pc, h)) for h in range(8)])
                e_p = evac8p(vp, half2(Pn), engs=("vector", "scalar"))
                yield
                if lev < 4:
                    vpt = mms8p([(hs(Pc, h), hs(Pct, h)) for h in range(8)])
                    prev_ept = evac8p(vpt, half2(Pnt), engs=("scalar", "vector"))
                    yield
                else:
                    prev_ept = None
                for sem, val in e_p:
                    p.wait("tensor", sem, val)
                for sem, val in xready:
                    p.wait("tensor", sem, val)
                vx = mms8p([(hs(Pn, h), hs(X, h)) for h in range(8)],
                           accs=[(ident[:, :], hs(X, h)) for h in range(8)])
                exv = evac8p(vx, half2(X), engs=("vector", "scalar"))
                pend["ev"] = tuple(exv)
                xready = exv
                Pc, Pn = Pn, Pc
                Pct, Pnt = Pnt, Pct
                yield
            XD = X
            for sem, val in xready:
                p.wait("tensor", sem, val)
            for sm, vv_ in bkbv:
                p.wait("tensor", sm, vv_)
            vc_ = mms8p([(hs(XD, h), hs(B["BK"], h)) for h in range(8)])
            e_ck = evac8p(vc_, half2(B["C0K"]), engs=("scalar", "vector"))
            yield
            vc2 = mms8p([(hs(XD, h), hs(B["BV"], h)) for h in range(8)])
            e_cv = evac8p(vc2, half2(B["C0V"]), engs=("vector", "scalar"))
            yield
            vn_ = mms8p([(hs(B["No"], h), hs(XD, h)) for h in range(8)])
            e_nt = evac8p(vn_, half2(B["NT"]), engs=("scalar", "vector"))
            yield
            for evs in (e_ck, e_cv, e_nt):
                for sem, val in evs:
                    p.wait("tensor", sem, val)
            SK, SV = B["C0K"], B["C0V"]
            for j in range(3):
                lastj = (j == 2)
                okK = B["Wp"] if lastj else B["X0"]
                okV = B["U0"] if lastj else B["Pb"]
                vhk = mms8p([(hs(B["NT"], h), hs(SK, h)) for h in range(8)],
                            accs=[(ident[:, :], hs(B["C0K"], h)) for h in range(8)])
                e_k = evac8p(vhk, half2(okK), engs=("vector", "scalar"))
                yield
                vhv = mms8p([(hs(B["NT"], h), hs(SV, h)) for h in range(8)],
                            accs=[(ident[:, :], hs(B["C0V"], h)) for h in range(8)])
                e_v = evac8p(vhv, half2(okV), engs=("scalar", "vector"))
                for sm, vv2_ in list(e_k) + list(e_v):
                    p.wait("tensor", sm, vv2_)
                SK, SV = okK, okV
                yield
            vz = mms8p([(hs(B["Wp"], h), kc(h)) for h in range(8)])
            e_z = evac8p(vz, half2(B["Zt"]), engs=("scalar", "vector"))
            yield
            vr = mms8p([(kc(h), hs(B["U0"], h)) for h in range(8)])
            e_r = evac8p(vr, half2(B["R"]), engs=("vector", "scalar"))
            yield
            va = mms8p([(hs(B["Kf"], h), qf(h)) for h in range(8)])
            mkfn = lambda e, o, i_: e.tensor_mul(out=o, in0=i_[:, :], in1=triu4[:, :])
            e_a = evac8p(va, half2(B["Mk"]), engs=("vector", "vector"),
                         fns=(mkfn, mkfn))
            yield
            for sm, vv2_ in list(e_z) + list(e_a):
                p.wait("tensor", sm, vv2_)
            vf = mms8p([(hs(B["Wp"], h), hs(B["Mk"], h)) for h in range(8)])
            qefn = lambda bk: (lambda e, o, i_, bk=bk, c=c, H0=H0: e.tensor_sub(
                out=o.rearrange("p (hh t) -> p hh t", hh=4),
                in0=q_fm[:, :].rearrange("p (hh t) -> p hh t", t=GRP)
                [:, H0 + bk * 4:H0 + bk * 4 + 4, c * 128:(c + 1) * 128],
                in1=i_[:, :].rearrange("p (hh t) -> p hh t", hh=4)))
            e_q = evac8p(vf, half2(B["Qe"]), engs=("vector", "vector"),
                         fns=(qefn(0), qefn(1)))
            yield
            for sm, vv2_ in e_q:
                p.wait("tensor", sm, vv2_)
            vg = mms8p([(gG(G_bf, h), hs(B["Qe"], h)) for h in range(8)])
            e_phi = evac8p(vg, half2(B["phi"]), engs=("scalar", "vector"))
            yield
            vo = mms8p([(hs(B["Mk"], h), hs(B["U0"], h)) for h in range(8)],
                       accs=[(hs(B["Qe"], h), gG(H_bf, h)) for h in range(8)])
            e_ol = evac8p(vo, half2(B["ol"]), engs=("vector", "scalar"))
            yield
            for sm, v_ in e_phi:
                p.wait("gpsimd", sm, v_)
            for sm, v_ in e_ol:
                p.wait("gpsimd", sm, v_)
            spill_g["phi"][w] = p.dma(
                "gpsimd", phisc[gc_, w][:, :].rearrange("p (a n) -> p a n", n=512),
                B["phi"][:, :].rearrange("p (a n) -> p a n", n=512), p.sem(f"sphi{w}"))
            spill_g["oloc"][w] = p.dma(
                "gpsimd", olocsc[gc_, w][:, :].rearrange("p (a n) -> p a n", n=512),
                B["ol"][:, :].rearrange("p (a n) -> p a n", n=512), p.sem(f"solo{w}"))
            gouts = (G[:, (H0 + 0) * 128:(H0 + 4) * 128],
                     G[:, (H0 + 4) * 128:(H0 + 8) * 128])
            gtouts = (Gt[:, (H0 + 0) * 128:(H0 + 4) * 128],
                      Gt[:, (H0 + 4) * 128:(H0 + 8) * 128])
            houts = (Hs[:, (H0 + 0) * 128:(H0 + 4) * 128],
                     Hs[:, (H0 + 4) * 128:(H0 + 8) * 128])
            vgt = mms8p([(gG(G_bf, h), hs(B["Zt"], h)) for h in range(8)])
            p.wait("vector", ts, vgt)
            gt2 = None
            for bk in range(2):
                gt2 = p.op("vector", lambda e, o=gtouts[bk], i_=pair[bk]:
                           e.scalar_tensor_tensor(
                    out=o, in0=i_[:, :], scalar=-1.0,
                    in1=o, op0=ALU.mult, op1=ALU.add), sem=vs)
            pend["ev"] = ((vs, gt2),)
            yield
            vzg = mms8p([(hs(B["Zt"], h), gG(G_bf, h)) for h in range(8)])
            p.wait("vector", ts, vzg)
            gv2 = None
            for bk in range(2):
                gv2 = p.op("vector", lambda e, o=gouts[bk], i_=pair[bk]:
                           e.scalar_tensor_tensor(
                    out=o, in0=i_[:, :], scalar=-1.0,
                    in1=o, op0=ALU.mult, op1=ALU.add), sem=vs)
            pend["ev"] = ((vs, gv2),)
            yield
            for sm, vv2_ in e_r:
                p.wait("vector", sm, vv2_)
            vzh = mms8p([(hs(B["Zt"], h), gG(H_bf, h)) for h in range(8)])
            p.wait("vector", ts, vzh)
            hv2 = None
            for bk in range(2):
                p.op("vector", lambda e, o=houts[bk], i_=pair[bk]:
                     e.scalar_tensor_tensor(
                    out=o, in0=i_[:, :], scalar=-1.0,
                    in1=o, op0=ALU.mult, op1=ALU.add), sem=vs)
                p.chain("vector")
                hv2 = p.op("vector", lambda e, o=houts[bk],
                           r=B["R"][:, bk * 512:(bk + 1) * 512]:
                           e.tensor_add(out=o, in0=o, in1=r), sem=vs)
            pend["ev"] = ((vs, hv2),)
            p.wait("scalar", vs, hv2)
            sl8 = slice(H0 * 128, (H0 + 8) * 128)
            cv = p.op("scalar", lambda e, sl8=sl8: e.copy(
                out=G_bf[:, sl8], in_=G[:, sl8]), sem=ss)
            cv = p.op("scalar", lambda e, sl8=sl8: e.copy(
                out=H_bf[:, sl8], in_=Hs[:, sl8]), sem=ss)
            p.wait("tensor", ss, cv)
            wave_last_ts[w] = vzh

        def wave_job(c, w):
            qv = max(ev_q[(s, c)] for s in range(4))
            kv_ = max(ev_k[(s, c)] for s in range(4))
            vv_ = max(ev_v[(s, c)] for s in range(4))
            p.wait("tensor", ss, qv)
            p.wait("tensor", vs, max(kv_, vv_, beta_v))
            p.wait("scalar", vs, max(kv_, vv_, beta_v))
            pairs = ((pb[0][:, :], pb[1][:, :]), (pb[2][:, :], pb[3][:, :]))
            yield from wave_gen(c, w, WB[w], pairs[w])

        jobs = [(c, w) for c in range(TCH) for w in range(2)]
        gens = []
        ji = 0
        primed = False
        while ji < len(jobs) or gens:
            while len(gens) < 2 and ji < len(jobs):
                g2 = wave_job(*jobs[ji])
                ji += 1
                gens.append(g2)
                for _ in range(12 if not primed else 2):
                    try:
                        next(g2)
                    except StopIteration:
                        break
                primed = True
            for g2 in list(gens):
                try:
                    next(g2)
                except StopIteration:
                    gens.remove(g2)

    return {"spl": p.cnt["spl"], "sphi0": p.cnt["sphi0"], "sphi1": p.cnt["sphi1"],
            "solo0": p.cnt["solo0"], "solo1": p.cnt["solo1"], "ts": p.cnt[ts]}



def relay_send(p, aps):
    ts, vs, ss = p.sem("ts"), p.sem("vs"), p.sem("ss")
    ccs = p.sem("ccs")
    snd = p.sem("snd")
    p.wait("gpsimd", vs, p.cnt[vs])
    v1 = p.dma("gpsimd", aps["ccin"][0].rearrange("p (a n) -> p a n", n=512),
               aps["Gt"][:, :].rearrange("p (a n) -> p a n", n=512), snd)
    v2 = p.dma("gpsimd", aps["ccin"][1].rearrange("p (a n) -> p a n", n=512),
               aps["Hs"][:, :].rearrange("p (a n) -> p a n", n=512), snd)
    p.wait("gpsimd", snd, v2)
    cv = p.inc(ccs, 1)
    p.emit("gpsimd", lambda e: e.collective_compute(
        "AllGather", ALU.bypass, replica_groups=[list(range(8))],
        ins=[aps["ccin"].opt()], outs=[aps["ccout"].opt()],
    ).then_inc(p.handles[ccs], 1))
    return cv


def compose_phase(p, ctx, aps, cc_v):
    nc = p.nc
    NH = 16
    ts, ss, vs, gs = p.sem("ts"), p.sem("ss"), p.sem("vs"), p.sem("gs")
    ccs, cld = p.sem("ccs"), p.sem("cld")

    S_in = ctx.enter_context(nc.sbuf_tensor("S_in", [128, NH * 128], F32))
    S_in_bf = aps["S_in_bf_outer"]
    tstate = ctx.enter_context(nc.sbuf_tensor("tstate", [128, NH * 128], F32))
    t_bf = ctx.enter_context(nc.sbuf_tensor("t_bf", [128, NH * 128], BF16))
    gtj_bf = ctx.enter_context(nc.sbuf_tensor("gtj_bf", [128, NH * 128], BF16))
    hj = [ctx.enter_context(nc.sbuf_tensor(f"hj{i}", [128, NH * 128], F32))
          for i in range(2)]
    gtj_f = [ctx.enter_context(nc.sbuf_tensor(f"gtj_f{i}", [128, NH * 128], F32))
             for i in range(2)]
    aps["S_in_bf"] = S_in_bf

    pb = aps["pb"]
    sel = aps["sel_sb"]

    p.wait("gpsimd", ccs, cc_v)
    p.wait("sync", ccs, cc_v)
    p.wait("tensor", ss, p.cnt[ss])
    p.wait("tensor", vs, p.cnt[vs])
    mz = p.op("vector", lambda e: e.memset(tstate[:, :], 0.0), sem=vs)
    mz = p.op("vector", lambda e: e.memset(S_in[:, :], 0.0), sem=vs)
    mz = p.op("vector", lambda e: e.memset(t_bf[:, :], 0.0), sem=vs)
    cld2 = p.sem("cld2")
    cgt_prev, hj_prev = {}, {}
    gvals = {}
    clds = [p.sem(f"cldr{i}") for i in range(2)]
    cld2s = [p.sem(f"cld2r{i}") for i in range(2)]

    def cload(j):
        jb = j % 2
        p.wait("sync", ss, cgt_prev.get(j - 2, 0))
        p.wait("sync", vs, hj_prev.get(j - 2, 0))
        g1 = p.dma("sync", gtj_f[jb][:, :].rearrange("p (a n) -> p a n", n=512),
                   aps["ccout"][2 * j].rearrange("p (a n) -> p a n", n=512), clds[jb])
        g2 = p.dma("sync", hj[jb][:, :].rearrange("p (a n) -> p a n", n=512),
                   aps["ccout"][2 * j + 1].rearrange("p (a n) -> p a n", n=512),
                   cld2s[jb])
        gvals[j] = (g1, g2)

    cload(0)
    for j in range(7):
        jb = j % 2
        if j < 6:
            cload(j + 1)
        g1, g2 = gvals[j]
        p.wait("scalar", clds[jb], g1)
        cgt = p.op("scalar", lambda e, jb=jb: e.copy(
            out=gtj_bf[:, :], in_=gtj_f[jb][:, :]), sem=ss)
        cgt_prev[j] = cgt
        p.wait("tensor", ss, cgt)
        p.wait("tensor", vs, mz)
        v = None
        for h in range(NH):
            bank = pb[h // 4]
            v = p.op("tensor", lambda e, h=h, bank=bank: e.matmul(
                bank[:, (h % 4) * 128:(h % 4 + 1) * 128],
                gtj_bf[:, h * 128:(h + 1) * 128],
                t_bf[:, h * 128:(h + 1) * 128],
                start=True, stop=True), sem=(ts if h % 4 == 3 else None)) or v
        p.wait("vector", ts, v)
        p.wait("vector", cld2s[jb], g2)
        mz = None
        for b4 in range(4):
            sl_ = slice(b4 * 512, (b4 + 1) * 512)
            p.op("vector", lambda e, b4=b4, sl_=sl_, jb=jb: e.tensor_add(
                out=tstate[:, sl_], in0=pb[b4][:, :], in1=hj[jb][:, sl_]), sem=vs)
            p.chain("vector")
            mz = p.op("vector", lambda e, b4=b4, sl_=sl_, j=j: e.scalar_tensor_tensor(
                out=S_in[:, sl_], in0=tstate[:, sl_], scalar=sel[:, j + 1:j + 2],
                in1=S_in[:, sl_], op0=ALU.mult, op1=ALU.add), sem=vs)
            hj_prev[j] = mz
        p.wait("scalar", vs, mz)
        ctv = p.op("scalar", lambda e: e.copy(out=t_bf[:, :], in_=tstate[:, :]), sem=ss)
        p.wait("tensor", ss, ctv)
    p.wait("scalar", vs, mz)
    sbv = p.op("scalar", lambda e: e.copy(out=S_in_bf[:, :], in_=S_in[:, :]), sem=ss)
    return sbv



def attnpost_phase(p, ctx, aps, cfg, sin_v):
    nc = p.nc
    NTOK = cfg["ntok"]
    NCH, KD, NH = NTOK // 128, 16, 16
    ts, ss, vs, gs = p.sem("ts"), p.sem("ss"), p.sem("vs"), p.sem("gs")
    ost = p.sem("ost")

    pb, pbbf, ident = aps["pb"], aps["pbbf"], aps["ident_sb"]
    phisc, olocsc, xnsc, gsc = aps["phisc"], aps["olocsc"], aps["xnsc"], aps["gsc"]
    x_ap, out_ap = aps["x"], aps["out"]
    S_in_bf = aps["S_in_bf"]

    woT_sb = ctx.enter_context(nc.sbuf_tensor("woT_sb", [128, KD * 2048], BF16))
    wo_f = [ctx.enter_context(nc.sbuf_tensor(f"wo_f{i}", [128, 2048], F32)) for i in range(2)]
    wg1_sb = ctx.enter_context(nc.sbuf_tensor("wg1_sb", [128, 2048], BF16))
    wg2_sb = ctx.enter_context(nc.sbuf_tensor("wg2_sb", [128, 2048], BF16))
    wgf = ctx.enter_context(nc.sbuf_tensor("wgf", [128, 2048], F32))
    gn_sb = ctx.enter_context(nc.sbuf_tensor("gn_sb", [128, 16], F32))
    B2 = []
    for i in range(2):
        d2 = {}
        d2["phi"] = ctx.enter_context(nc.sbuf_tensor(f"phi_b{i}", [128, 2048], BF16))
        d2["oloc"] = ctx.enter_context(nc.sbuf_tensor(f"oloc_b{i}", [128, 2048], BF16))
        d2["xn"] = ctx.enter_context(nc.sbuf_tensor(f"xn_c{i}", [128, 2048], BF16))
        d2["z"] = ctx.enter_context(nc.sbuf_tensor(f"z_c{i}", [128, 128], BF16))
        d2["gate"] = ctx.enter_context(nc.sbuf_tensor(f"gate_c{i}", [128, 2048], BF16))
        d2["og"] = ctx.enter_context(nc.sbuf_tensor(f"og_c{i}", [128, 2048], F32))
        d2["ogn"] = ctx.enter_context(nc.sbuf_tensor(f"ogn_c{i}", [128, 2048], BF16))
        d2["ogf"] = ctx.enter_context(nc.sbuf_tensor(f"ogf_c{i}", [128, 2048], BF16))
        d2["ssq"] = ctx.enter_context(nc.sbuf_tensor(f"ssq2_{i}", [128, 16], F32))
        d2["junk"] = d2["ogn"]
        d2["s"] = ctx.enter_context(nc.sbuf_tensor(f"s_c{i}", [128, 2048], BF16))
        d2["g"] = ctx.enter_context(nc.sbuf_tensor(f"g_c{i}", [128, 2048], BF16))
        d2["x"] = ctx.enter_context(nc.sbuf_tensor(f"x_c{i}", [128, 2048], F32))
        d2["sg"] = d2["og"]
        B2.append(d2)
    wld2 = [p.sem(f"wld2_{i}") for i in range(2)]
    xcp = [p.sem(f"xcp{i}") for i in range(2)]
    xco = [p.sem(f"xco{i}") for i in range(2)]
    xcx = [p.sem(f"xcx{i}") for i in range(2)]
    ostp = [p.sem(f"ost{i}") for i in range(2)]
    xgl = [p.sem(f"xgl{i}") for i in range(2)]
    if "gst_final" in cfg:
        p.wait("scalar", "gst", cfg["gst_final"])
        p.wait("sync", "spl", cfg["spl_final"])
        for nm in ("sphi0", "sphi1", "solo0", "solo1"):
            p.wait("sync", nm, cfg[nm])

    pox = {}
    gx = {}

    def issue_pox(c):
        b = c % 2
        B = B2[b]
        p.dma("sync", B["phi"][:, :1024].rearrange("p (a n) -> p a n", n=512),
              phisc[c, 0].rearrange("p (a n) -> p a n", n=512), xcp[b])
        pv_p = p.dma("sync", B["phi"][:, 1024:].rearrange("p (a n) -> p a n", n=512),
                     phisc[c, 1].rearrange("p (a n) -> p a n", n=512), xcp[b])
        p.dma("sync", B["oloc"][:, :1024].rearrange("p (a n) -> p a n", n=512),
              olocsc[c, 0].rearrange("p (a n) -> p a n", n=512), xco[b])
        pv_o = p.dma("sync", B["oloc"][:, 1024:].rearrange("p (a n) -> p a n", n=512),
                     olocsc[c, 1].rearrange("p (a n) -> p a n", n=512), xco[b])
        pv_t = p.dma("sync", B["xn"][:, :].rearrange("p (k t) -> p k t", k=KD),
                     xnsc[:, :, c * 128:(c + 1) * 128].rearrange("k p t -> p k t"),
                     xcx[b])
        pox[c] = (pv_p, pv_o, pv_t)

    def issue_gx(c):
        b = c % 2
        B = B2[b]
        p.dma("scalar", B["g"][:, :].rearrange("p (a n) -> p a n", n=1024),
              gsc[c * 128:(c + 1) * 128, :].rearrange("p (a n) -> p a n", n=1024),
              xgl[b])
        gx[c] = p.dma("scalar", B["x"][:, :].rearrange("p (a n) -> p a n", n=512),
                      x_ap[c * 128:(c + 1) * 128, :].rearrange("p (a n) -> p a n", n=512),
                      xgl[b])

    issue_pox(0)
    issue_pox(1)
    issue_gx(0)
    issue_gx(1)

    gl = p.dma("sync", gn_sb[:, :], aps["gn16"], p.sem("gnl"))
    wo_c = {}
    for dd in range(16):
        b = dd % 2
        if dd >= 2:
            p.wait("sync", vs, wo_c[dd - 2])
        lv = p.dma("sync", wo_f[b][:, :].rearrange("p (a n) -> p a n", n=512),
                   aps["woT"][dd * 128:(dd + 1) * 128, :]
                   .rearrange("p (a n) -> p a n", n=512), wld2[b])
        p.wait("vector", wld2[b], lv)
        p.wait("vector", "gnl", gl)
        wo_c[dd] = p.op("vector", lambda e, dd=dd, b=b: e.tensor_scalar(
            out=woT_sb[:, dd * 2048:(dd + 1) * 2048], in0=wo_f[b][:, :],
            scalar1=gn_sb[:, dd:dd + 1], scalar2=None, op0=ALU.mult), sem=vs)
    l1 = p.dma("sync", wgf[:, :].rearrange("p (a n) -> p a n", n=512),
               aps["wg1T4"].rearrange("p (a n) -> p a n", n=512), p.sem("wgl1"))
    p.wait("scalar", "wgl1", l1)
    c1 = p.op("scalar", lambda e: e.copy(out=wg1_sb[:, :], in_=wgf[:, :]), sem=ss)
    p.wait("sync", ss, c1)
    l2 = p.dma("sync", wgf[:, :].rearrange("p (a n) -> p a n", n=512),
               aps["wg2T"].rearrange("p (a n) -> p a n", n=512), p.sem("wgl2"))
    p.wait("scalar", "wgl2", l2)
    c2 = p.op("scalar", lambda e: e.copy(out=wg2_sb[:, :], in_=wgf[:, :]), sem=ss)
    p.wait("tensor", ss, c2)
    p.wait("tensor", ss, sin_v)

    ev_og = {}

    def chunk_gen(c):
        b = c % 2
        B = B2[b]
        pairA = (pb[2 * b][:, :], pb[2 * b + 1][:, :])
        zb = pb[4 + b][:, :]
        if c >= 2:
            p.wait("vector", ostp[b], ev_og.get(("ost", c - 2), 0))
        pv_p, pv_o, pv_t = pox[c]
        yield
        p.wait("tensor", vs, ev_og.get(("sev", c - 2, 3), 0))
        p.wait("tensor", xcp[b], pv_p)
        for w in range(2):
            H0 = 8 * w
            if w == 1:
                p.wait("tensor", vs, ev_og[("oadd", c, 0)])
            v = None
            for h in range(8):
                bank = pairA[h // 4]
                v = p.op("tensor", lambda e, h=h, H0=H0, bank=bank: e.matmul(
                    bank[:, (h % 4) * 128:(h % 4 + 1) * 128],
                    B["phi"][:, (H0 + h) * 128:(H0 + h + 1) * 128],
                    S_in_bf[:, (H0 + h) * 128:(H0 + h + 1) * 128],
                    start=True, stop=True), sem=(ts if h == 7 else None)) or v
            p.wait("vector", ts, v)
            p.wait("vector", xco[b], pv_o)
            oaddv = None
            for b2 in range(2):
                sl_ = slice((H0 + b2 * 4) * 128, (H0 + b2 * 4 + 4) * 128)
                oaddv = p.op("vector", lambda e, b2=b2, sl_=sl_: e.tensor_add(
                    out=B["og"][:, sl_], in0=pairA[b2][:, :],
                    in1=B["oloc"][:, sl_]), sem=vs)
            ev_og[("oadd", c, w)] = oaddv
            yield
        p.wait("tensor", xcx[b], pv_t)
        zv = None
        for kk in range(KD):
            zv = p.op("tensor", lambda e, kk=kk: e.matmul(
                zb[:, :128], wg1_sb[:, kk * 128:(kk + 1) * 128],
                B["xn"][:, kk * 128:(kk + 1) * 128],
                start=(kk == 0), stop=(kk == KD - 1)),
                sem=(ts if kk == KD - 1 else None)) or zv
        ev_og[("zdone", c)] = zv
        p.wait("scalar", ts, zv)
        zc = p.op("scalar", lambda e: e.copy(out=B["z"][:, :], in_=zb[:, :128]), sem=ss)
        p.wait("tensor", ss, zc)
        if c + 2 < NCH:
            p.wait("sync", ts, zv)
            p.wait("sync", vs, ev_og[("oadd", c, 1)])
            issue_pox(c + 2)
        yield
        gt_v = None
        for dt_ in range(4):
            mv = p.op("tensor", lambda e, dt_=dt_: e.matmul(
                zb[:, :], B["z"][:, :], wg2_sb[:, dt_ * 512:(dt_ + 1) * 512],
                start=True, stop=True), sem=ts)
            p.wait("scalar", ts, mv)
            gt_v = p.op("scalar", lambda e, dt_=dt_: e.activation(
                out=B["gate"][:, dt_ * 512:(dt_ + 1) * 512], in_=zb[:, :],
                func=AF.Sigmoid), sem=ss)
            if dt_ < 3:
                p.wait("tensor", ss, gt_v)
            yield
        p.wait("vector", ss, gt_v)
        p.wait("vector", vs, ev_og[("oadd", c, 1)])
        p.chain("vector")
        for b4 in range(4):
            sl_ = slice(b4 * 512, (b4 + 1) * 512)
            p.op("vector", lambda e, sl_=sl_: e.tensor_mul(
                out=B["og"][:, sl_], in0=B["og"][:, sl_],
                in1=B["gate"][:, sl_]), sem=vs)
            p.chain("vector")
        sqv = None
        for hh in range(16):
            sqv = p.op("vector", lambda e, hh=hh: e.scalar_tensor_tensor(
                out=B["junk"][:, hh * 128:(hh + 1) * 128],
                in0=B["og"][:, hh * 128:(hh + 1) * 128],
                scalar=1.0, in1=B["og"][:, hh * 128:(hh + 1) * 128],
                op0=ALU.mult, op1=ALU.mult,
                accum_out=B["ssq"][:, hh:hh + 1]), sem=vs)
        yield
        p.wait("scalar", vs, sqv)
        rs = p.op("scalar", lambda e: e.activation(
            out=B["ssq"][:, :], in_=B["ssq"][:, :], func=AF.Sqrt, bias=1e-6,
            scale=1.0 / 128.0), sem=ss)
        p.wait("vector", ss, rs)
        p.op("vector", lambda e: e.reciprocal(out=B["ssq"][:, :], in_=B["ssq"][:, :]),
             sem=vs)
        p.chain("vector")
        nv = None
        for h in range(16):
            nv = p.op("vector", lambda e, h=h: e.tensor_scalar(
                out=B["ogn"][:, h * 128:(h + 1) * 128],
                in0=B["og"][:, h * 128:(h + 1) * 128],
                scalar1=B["ssq"][:, h:h + 1], scalar2=None, op0=ALU.mult), sem=vs)
        yield
        p.wait("tensor", vs, nv)
        tse = {}
        for kb in range(4):
            out_tp = (B["ogf"][:, :].rearrange("p (a n) -> p a n", a=16)
                      [:, kb * 4:(kb + 1) * 4, :])
            tse[kb] = transpose_like(p, aps,
                                     [B["ogn"][:, (kb * 4 + j) * 128:(kb * 4 + j + 1) * 128]
                                      for j in range(4)],
                                     out_tp, "scalar" if kb % 2 else "vector")
            yield
        for kb in range(4):
            p.wait("tensor", tse[kb][0], tse[kb][1])
        p.wait("tensor", vs, ev_og[("oadd", c, 1)])
        p.wait("vector", xgl[b], gx[c])
        sv = None
        cb = None
        for dt_ in range(4):
            bank = pairA[dt_ % 2]
            if dt_ >= 2:
                p.wait("tensor", vs, ev_og[("sev", c, dt_ - 2)])
            for dd in range(KD):
                sv = p.op("tensor", lambda e, dt_=dt_, dd=dd, bank=bank: e.matmul(
                    bank[:, :], B["ogf"][:, dd * 128:(dd + 1) * 128],
                    woT_sb[:, dd * 2048 + dt_ * 512:dd * 2048 + (dt_ + 1) * 512],
                    start=(dd == 0), stop=(dd == KD - 1)),
                    sem=(ts if dd == KD - 1 else None)) or sv
            p.wait("vector", ts, sv)
            sl4 = slice(dt_ * 512, (dt_ + 1) * 512)
            ev_og[("sev", c, dt_)] = p.op("vector", lambda e, sl4=sl4, bank=bank:
                                          e.scalar_tensor_tensor(
                out=B["sg"][:, sl4], in0=bank[:, :], scalar=1.0,
                in1=B["g"][:, sl4], op0=ALU.mult, op1=ALU.mult), sem=vs)
            p.chain("vector")
            cb = p.op("vector", lambda e, sl4=sl4: e.tensor_add(
                out=B["sg"][:, sl4], in0=B["sg"][:, sl4], in1=B["x"][:, sl4]),
                sem=vs)
            p.wait("gpsimd", vs, cb)
            ev_og[("ost", c)] = p.dma(
                "gpsimd", out_ap[c * 128:(c + 1) * 128, dt_ * 512:(dt_ + 1) * 512],
                B["sg"][:, sl4], ostp[b])
            yield
        ev_og[("mmdone", c)] = sv
        ev_og[("sgdone", c)] = cb
        ev_og[("comb", c)] = cb
        if c + 2 < NCH:
            p.wait("scalar", vs, cb)
            issue_gx(c + 2)

    gens = []
    next_c = 0
    primed = False
    while next_c < NCH or gens:
        while len(gens) < 2 and next_c < NCH:
            g_ = chunk_gen(next_c)
            next_c += 1
            gens.append(g_)
            for _ in range(9 if not primed else 2):
                try:
                    next(g_)
                except StopIteration:
                    break
            primed = True
        for g_ in list(gens):
            try:
                next(g_)
            except StopIteration:
                gens.remove(g_)
    return {"ost0": p.cnt["ost0"], "ost1": p.cnt["ost1"]}


def transpose_like(p, aps, srcs, out_ap, eng):
    st = p.__dict__.setdefault("_tpstate", {"i": 0, "ev": {}})
    ts, ss, vs = p.sem("ts"), p.sem("ss"), p.sem("vs")
    i = st["i"] % 2
    st["i"] += 1
    bank = aps["pbbf"][i]
    pe = st["ev"].get(i)
    if pe is not None:
        p.wait("tensor", pe[0], pe[1])
    v = None
    for t, s in enumerate(srcs):
        v = p.op("tensor", lambda e, s=s, t=t, bank=bank: e.transpose(
            bank[:, t * 128:(t + 1) * 128], s, aps["ident_sb"][:, :]),
            sem=(ts if t == len(srcs) - 1 else None)) or v
    p.wait(eng, ts, v)
    sem = ss if eng == "scalar" else vs
    if eng == "scalar":
        ev = p.op(eng, lambda e, o=out_ap, bank=bank: e.copy(
            out=o, in_=bank[:, :512]), sem=sem)
    else:
        ev = p.op(eng, lambda e, o=out_ap, bank=bank: e.tensor_copy(
            out=o, in_=bank[:, :512]), sem=sem)
    st["ev"][i] = (sem, ev)
    return sem, ev



def build_kernel(ntok, grp=512):
    from contextlib import ExitStack
    nc = bass.Bass("TRN2", target_bir_lowering=False, debug=False)
    NCH = ntok // 128
    d = {}
    def din(name, shape, dt=F32):
        d[name] = nc.dram_tensor(name, list(shape), dt, kind="ExternalInput").ap()
        return d[name]
    x = din("x", (ntok, 2048))
    out = nc.dram_tensor("out", [ntok, 2048], F32, kind="ExternalOutput").ap()
    din("wqT", (2048, 2048)); din("wkT", (2048, 2048)); din("wvT", (2048, 2048))
    din("wbeta16", (128, 256)); din("wg1T4", (128, 2048)); din("wg2T", (128, 2048))
    din("woT", (2048, 2048)); din("gn16", (128, 16))
    din("wgu4", (128, 128, 16, 128)); din("wdn4", (16, 128, 64, 128))
    din("eye16", (128, 4, 512))
    din("ident", (128, 128), BF16); din("ident4", (128, 512), BF16)
    din("maskD", (128, 128)); din("maskOff", (128, 128))
    din("triu4", (128, 512)); din("sel", (128, 8))
    din("cpool", (128, 3))

    xnsc = nc.dram_tensor("xnsc", [16, 128, ntok], BF16).ap()
    phisc = nc.dram_tensor("phisc", [NCH, 2, 128, 1024], BF16).ap()
    olocsc = nc.dram_tensor("olocsc", [NCH, 2, 128, 1024], BF16).ap()
    gsc = nc.dram_tensor("gsc", [ntok, 2048], BF16).ap()
    wqkvbf = nc.dram_tensor("wqkvbf", [3, 2048, 2048], BF16).ap()
    ccin = nc.dram_tensor("ccin", [2, 128, 2048], F32).ap()
    ccout = nc.dram_tensor("ccout", [16, 128, 2048], F32, addr_space="Shared").ap()

    p = Prog(nc)
    import os
    cfg = {"ntok": ntok, "grp": grp, "row0": 0}
    dbg_on = os.environ.get("KDBG") == "1"
    dbgsc = nc.dram_tensor("dbgsc", [40, 128, 1024], F32).ap() if dbg_on else None
    with ExitStack() as ctx0:
        pb = [ctx0.enter_context(nc.psum_tensor(f"pb{i}", [128, 512], F32))
              for i in range(6)]
        pbbf = [ctx0.enter_context(nc.psum_tensor(f"pbbf{i}", [128, 1024], BF16))
                for i in range(2)]
        ident_sb = ctx0.enter_context(nc.sbuf_tensor("ident_sb", [128, 128], BF16))
        ident4_sb = ctx0.enter_context(nc.sbuf_tensor("ident4_sb", [128, 512], BF16))
        maskD_sb = ctx0.enter_context(nc.sbuf_tensor("maskD_sb", [128, 128], F32))
        maskOff_sb = ctx0.enter_context(nc.sbuf_tensor("maskOff_sb", [128, 128], F32))
        triu4_sb = ctx0.enter_context(nc.sbuf_tensor("triu4_sb", [128, 512], F32))
        sel_sb = ctx0.enter_context(nc.sbuf_tensor("sel_sb", [128, 8], F32))
        aps = {"x": x, "out": out, "xnsc": xnsc, "phisc": phisc, "olocsc": olocsc,
               "gsc": gsc, "ccin": ccin, "ccout": ccout,
               "wqT": d["wqT"], "wkT": d["wkT"], "wvT": d["wvT"],
               "wbeta16_dram": d["wbeta16"], "wg1T4": d["wg1T4"],
               "wg2T": d["wg2T"], "woT": d["woT"], "gn16": d["gn16"],
               "wgu4": d["wgu4"], "wdn4": d["wdn4"], "eye16": d["eye16"],
               "wqkvbf": wqkvbf,
               **({"dbgsc": dbgsc} if dbg_on else {}),
               "pb": pb, "pbbf": pbbf, "ident_sb": ident_sb,
               "ident4_sb": ident4_sb, "maskD_sb": maskD_sb,
               "maskOff_sb": maskOff_sb,
               "triu4_sb": triu4_sb, "sel_sb": sel_sb}
        cpool_sb = ctx0.enter_context(nc.sbuf_tensor("cpool_sb", [128, 3], F32))
        nc.const_aps.register(F32, 1e-6, cpool_sb[:, 0:1])
        nc.const_aps.register(F32, 1.0 / 128.0, cpool_sb[:, 1:2])
        nc.const_aps.register(F32, 1e-5, cpool_sb[:, 2:3])
        cl = p.sem("cl")
        v = p.dma("sync", cpool_sb[:, :], d["cpool"], cl)
        v = p.dma("sync", ident_sb[:, :], d["ident"], cl)
        v = p.dma("sync", ident4_sb[:, :], d["ident4"], cl)
        v = p.dma("sync", maskD_sb[:, :], d["maskD"], cl)
        v = p.dma("sync", maskOff_sb[:, :], d["maskOff"], cl)
        v = p.dma("sync", triu4_sb[:, :], d["triu4"], cl)
        v = p.dma("sync", sel_sb[:, :], d["sel"], cl)
        for e in ENGINES:
            p.wait(e, cl, v)

        with ExitStack() as c1:
            sv = sweep_phase(p, c1, aps, cfg)
            cc_v = relay_send(p, aps)
            for i in range(6):
                p.wait("sync", f"wsl_ld{i}", p.cnt[f"wsl_ld{i}"])
            p.wait("gpsimd", "spl", sv["spl"])
            for nm in ("sphi0", "sphi1", "solo0", "solo1"):
                p.wait("gpsimd", nm, sv[nm])
            for i in range(2):
                p.wait("sync", f"xch{i}", p.cnt[f"xch{i}"])
            p.barrier()
        with ExitStack() as c2:
            aps["xn_ready"] = ("spl", sv["spl"])
            fv = ffn_phase(p, c2, aps, cfg)
            for i in range(4):
                p.wait("sync", f"wgu_ld{i}", p.cnt[f"wgu_ld{i}"])
            for i in range(3):
                p.wait("sync", f"wdn_ld{i}", p.cnt[f"wdn_ld{i}"])
            for i in range(2):
                p.wait("scalar", f"fxld{i}", p.cnt[f"fxld{i}"])
            p.wait("gpsimd", "gst", fv["gst"])
            p.barrier()
        aps["S_in_bf_outer"] = ctx0.enter_context(
            nc.sbuf_tensor("S_in_bf_o", [128, 16 * 128], BF16))
        with ExitStack() as c3a:
            sin_v = compose_phase(p, c3a, aps, cc_v)
            for nm in ("cldr0", "cldr1", "cld2r0", "cld2r1"):
                p.wait("sync", nm, p.cnt[nm])
            p.barrier()
        with ExitStack() as c3:
            cfg2 = dict(cfg)
            cfg2["gst_final"] = fv["gst"]
            cfg2["spl_final"] = sv["spl"]
            for nm in ("sphi0", "sphi1", "solo0", "solo1"):
                cfg2[nm] = sv[nm]
            av = attnpost_phase(p, c3, aps, cfg2, sin_v)
            p.wait("gpsimd", "ost0", av["ost0"])
            p.wait("gpsimd", "ost1", av["ost1"])
        p.build_sems(ctx0)
        with nc.Block() as block:
            p.replay(block)
    return nc


def host_prep(inputs, ntok_core=2048):
    import ml_dtypes
    f32 = np.float32
    asc = np.ascontiguousarray
    x = np.asarray(inputs["L0_attn_x"], f32).reshape(-1, 2048)
    wq = np.asarray(inputs["L0_attn_Wq"], f32)
    wk = np.asarray(inputs["L0_attn_Wk"], f32)
    wv = np.asarray(inputs["L0_attn_Wv"], f32)
    wbeta = np.asarray(inputs["L0_attn_Wbeta"], f32)
    wg1 = np.asarray(inputs["L0_attn_Wg1"], f32)
    wg2 = np.asarray(inputs["L0_attn_Wg2"], f32)
    gn = np.asarray(inputs["L0_attn_g_norm"], f32)
    wo = np.asarray(inputs["L0_attn_Wo"], f32)
    wgu = np.asarray(inputs["L0_ffn_w_gate_up"], f32)
    wdn = np.asarray(inputs["L0_ffn_w_down"], f32)

    shared = {
        "wqT": asc(wq.T), "wkT": asc(wk.T), "wvT": asc(wv.T),
        "wbeta16": asc(wbeta.T.reshape(16, 128, 16).transpose(1, 0, 2)
                       .reshape(128, 256)),
        "wg1T4": asc(wg1.T.reshape(16, 128, 128).transpose(1, 0, 2)
                     .reshape(128, 2048)),
        "wg2T": asc(wg2.T),
        "woT": asc(wo.T),
        "gn16": asc(gn.reshape(16, 128).T),
        "wgu4": asc(wgu.T.reshape(16, 128, 128, 128).transpose(2, 1, 0, 3)),
        "wdn4": asc(wdn.T.reshape(64, 128, 16, 128).transpose(2, 1, 0, 3)),
        "eye16": asc(np.tile(np.eye(128, dtype=f32), 16).reshape(128, 4, 512)),
        "ident": np.eye(128, dtype=ml_dtypes.bfloat16),
        "ident4": asc(np.tile(np.eye(128, dtype=np.float32), 4)
                      .astype(ml_dtypes.bfloat16)),
        "maskD": asc(-np.tril(np.ones((128, 128), f32), -1)
                     * np.kron(np.eye(4, dtype=f32), np.ones((32, 32), f32))),
        "maskOff": asc(-np.tril(np.ones((128, 128), f32), -1)
                       * (1.0 - np.kron(np.eye(4, dtype=f32), np.ones((32, 32), f32)))),
        "triu4": asc(np.tile(np.triu(np.ones((128, 128), f32), 0), 4)),
        "cpool": asc(np.tile(np.array([[1e-6, 1.0 / 128.0, 1e-5]], f32), (128, 1))),
    }
    per_core = []
    ncores = x.shape[0] // ntok_core
    for c in range(ncores):
        sel = np.zeros((128, 8), f32)
        sel[:, c] = 1.0
        m = {"x": asc(x[c * ntok_core:(c + 1) * ntok_core]), "sel": sel}
        m.update(shared)
        per_core.append(m)
    return per_core



NTOK_CORE = 2048

def kernel(**inputs):
    from concourse.bass_utils import run_bass_kernel_spmd
    nc = build_kernel(NTOK_CORE, grp=512)
    per_core = host_prep(inputs, NTOK_CORE)
    res = run_bass_kernel_spmd(nc, per_core, core_ids=list(range(8)))
    out = np.concatenate([np.asarray(res.results[c]["out"]) for c in range(8)], axis=0)
    return out.reshape(1, 16384, 2048).astype(np.float32)
